# version 02
# baseline (speedup 1.0000x reference)
import os
import numpy as np
from contextlib import ExitStack

import concourse.bass as bass
import concourse.mybir as mybir
from concourse.bass_utils import run_bass_kernel_spmd

F32 = mybir.dt.float32
BF16 = mybir.dt.bfloat16
F16 = mybir.dt.float16
I8 = mybir.dt.int8
AF = mybir.ActivationFunctionType
ALU = mybir.AluOpType
AX = mybir.AxisListType

T = 2048
D = 2048
H = 16
HD = 128
NCH = 16
NBLK = 8
MAGIC = 12582912.0
EPS_LN = 1e-5
EPS_RMS = 1e-6
EPS_Q = 1e-7


class Sem:
    def __init__(self, ctx, nc, name):
        self.h = ctx.enter_context(nc.semaphore(name))
        self.n = 0

    def bump(self, k):
        self.n += k
        return self.n


class Em:

    ENGS = ("tensor", "vector", "scalar", "gpsimd", "sync")

    def __init__(self):
        self.q = {e: [] for e in self.ENGS}
        self.lastwait = {e: {} for e in self.ENGS}
        self.trace = {e: [] for e in self.ENGS}

    def wait(self, eng, sem, cnt=None):
        cnt = sem.n if cnt is None else cnt
        if cnt is None or cnt <= 0:
            return
        lw = self.lastwait[eng]
        if lw.get(id(sem), -1) >= cnt:
            return
        lw[id(sem)] = cnt
        self.trace[eng].append(("wait", sem.h.name, cnt))
        self.q[eng].append(lambda e, s=sem.h, c=cnt: e.wait_ge(s, c))

    def qop(self, eng, fn):
        self.q[eng].append(fn)

    def op(self, eng, sem, fn, k=1):
        v = sem.bump(k)
        import inspect
        try:
            srcl = inspect.getsource(fn).strip().split("\n")[0][:80]
        except Exception:
            srcl = "?"
        self.trace[eng].append(("op", sem.h.name, v, srcl))
        self.q[eng].append(lambda e, f=fn, s=sem.h, kk=k: f(e).then_inc(s, kk))
        return v

    def opw(self, eng, sem, fn, k=1):
        v = self.op(eng, sem, fn, k)
        self.wait(eng, sem, v)
        return v

    def raw(self, eng, fn):
        self.q[eng].append(fn)

    def barrier(self, sems):
        for eng in self.ENGS:
            for s in sems:
                self.wait(eng, s)


class C:
    pass


def build_nc(taps=(), stages=99):
    taps = set(taps)
    nc = bass.Bass("TRN2", target_bir_lowering=False)
    c = C()
    c.nc = nc
    c.taps = taps

    dx = nc.dram_tensor("x_own", [T, D], F32, kind="ExternalInput")
    dxp = nc.dram_tensor("x_pre", [T, D], F32, kind="ExternalInput")
    dwq = nc.dram_tensor("wqT", [D, D], F32, kind="ExternalInput")
    dwk = nc.dram_tensor("wkT", [D, D], F32, kind="ExternalInput")
    dwv = nc.dram_tensor("wvT", [D, D], F32, kind="ExternalInput")
    dwg = nc.dram_tensor("wgT", [D, D], F32, kind="ExternalInput")
    dwo = nc.dram_tensor("woT", [D, D], F32, kind="ExternalInput")
    dwf = nc.dram_tensor("wfT", [D, D], I8, kind="ExternalInput")
    dsl = nc.dram_tensor("slopes", [H], F32, kind="ExternalInput")
    dnw = nc.dram_tensor("normw", [D], F32, kind="ExternalInput")
    dws = nc.dram_tensor("wscale", [D], F32, kind="ExternalInput")
    dfb = nc.dram_tensor("fbias", [D], F32, kind="ExternalInput")
    dout = nc.dram_tensor("out", [T, D], F32, kind="ExternalOutput")
    c.dram = dict(x=dx, xp=dxp, wq=dwq, wk=dwk, wv=dwv, wg=dwg, wo=dwo,
                  wf=dwf, sl=dsl, nw=dnw, ws=dws, fb=dfb, out=dout)
    c.tap_dram = {}
    for name, shape, dt in [
        ("tap_xnfpre", [128, NCH * T], BF16), ("tap_xnf", [128, NCH * T], BF16),
        ("tap_kv", [128, H * HD], F32), ("tap_q0", [128, T], BF16),
        ("tap_k0", [128, T], BF16), ("tap_v0", [128, 4096], BF16),
        ("tap_outf", [128, NCH * T], BF16), ("tap_gout", [128, NCH * T], BF16),
        ("tap_x2", [128, NCH * T], BF16), ("tap_qxf", [128, NCH * 1024], BF16),
        ("tap_insc", [128, 16], F32),
        ("tap_cols", [128, 16 * 8], F32),
    ]:
        if name in taps:
            c.tap_dram[name] = nc.dram_tensor(name, shape, dt, kind="ExternalOutput")

    big = ExitStack()
    e = big.enter_context

    em = Em()
    c.em = em
    names = ["wl", "tp", "st", "tn", "vc", "sc", "gp",
             "x0", "x1", "t0", "t1", "wq", "wk", "wv", "wg0", "wg1",
             "wo", "wf", "cst", "xp4"]
    c.s = {nm: Sem(big, nc, "s_" + nm) for nm in names}
    c.sems = list(c.s.values())

    arena1 = e(nc.sbuf_tensor("arena1", [128, 32768], BF16))
    arena2 = e(nc.sbuf_tensor("arena2", [128, 32768], BF16))
    arena3 = e(nc.sbuf_tensor("arena3", [128, 32768], BF16))
    c.arena1, c.arena2, c.arena3 = arena1, arena2, arena3

    z = {}
    off = 0

    def zalloc(name, n):
        nonlocal off
        z[name] = (off, off + n)
        off += n

    zalloc("qdec", H * 256)
    zalloc("dt", H * 256)
    zalloc("qbuf", T)
    zalloc("kbuf", T)
    zalloc("ktbuf", T)
    zalloc("wq", T)
    zalloc("wk", T)
    zalloc("qkdt0", 256)
    zalloc("qkdt1", 256)
    zalloc("kv_bf", 128)
    zalloc("pad", 128)
    zalloc("shared", 10240)
    assert off <= 32768, off
    c.z = z

    def Z(name):
        lo, hi = z[name]
        return arena3[:, lo:hi]

    c.Z = Z
    sh = z["shared"][0]
    c.xn_tok = [arena3[:, sh:sh + 2048], arena3[:, sh + 2048:sh + 4096]]
    c.xchunk = [arena3[:, sh + 4096:sh + 6144], arena3[:, sh + 6144:sh + 8192]]
    c.sqz = arena3[:, sh + 8192:sh + 10240]
    c.vbuf = arena3[:, sh:sh + 4096]
    c.wv = arena3[:, sh + 4096:sh + 8192]

    kvst = e(nc.sbuf_tensor("kvst", [128, 2048], F32))
    c.kvst = kvst
    cols = e(nc.sbuf_tensor("cols", [128, 208], F32))
    c.cols = cols
    co = {}
    coff = 0

    def calloc(name, n):
        nonlocal coff
        co[name] = (coff, coff + n)
        coff += n

    for nm, n in [("slneg", 16), ("nw", 16), ("bd", 16), ("kdec", 32),
                  ("a2", 16), ("a2sq", 16), ("insc", 16), ("c256", 1),
                  ("s1", 1), ("s2", 2), ("mneg", 1), ("mm", 1), ("tv", 1),
                  ("rc", 1), ("acol", 1), ("bcol", 1), ("mx", 1), ("mn", 1),
                  ("t1", 1), ("t2", 1), ("scol", 1), ("invs", 1), ("c1", 1),
                  ("c2", 1), ("epsln", 1), ("var16", 16),
                  ("s1b", 1), ("s2b", 2), ("mnegb", 1), ("mmb", 1), ("tvb", 1),
                  ("rcb", 1), ("acolb", 1), ("bcolb", 1)]:
        calloc(nm, n)
    assert coff <= 208

    def CO(name):
        lo, hi = co[name]
        return cols[:, lo:lo + 1] if hi - lo == 2 else cols[:, lo:hi]

    c.CO = CO

    consts = e(nc.sbuf_tensor("consts", [128, 288], BF16))
    c.ones_col = consts[:, 0:1]
    c.onesq = consts[:, 32:160]
    c.ident = consts[:, 160:288]
    sigt = e(nc.sbuf_tensor("sigt", [128, 1024], BF16))
    c.sig = [sigt[:, 0:512], sigt[:, 512:1024]]
    biasb = e(nc.sbuf_tensor("biasb", [128, 2048], BF16))
    c.biasb = biasb

    a1f = arena1.bitcast(F32)
    c.iota_m1 = a1f[:, 0:256]
    c.iota_kd = a1f[:, 256:258]
    c.idxT0 = a1f[:, 512:768]
    c.slpos = a1f[:, 768:784]
    c.dt_scr = a1f[:, 1024:1280]

    kvb = kvst.bitcast(BF16)
    c.wg = [kvb[:, 0:2048], kvb[:, 2048:4096]]
    c.xchunk_p4 = kvb[:, 0:2048]
    c.sq_p4_bf = kvb[:, 2048:4096]
    c.outbuf = kvst[:, 0:2048]

    c.qxf = arena1[:, 0:16384]
    c.X32 = a1f[:, 8192:10240]
    c.Xbf = arena1[:, 16384:20480]
    c.qf = [arena1[:, 20480:22528], arena1[:, 22528:24576]]
    c.wsb = a1f[:, 12288:14336]
    c.ffn16 = arena1.bitcast(F16)[:, 28672:30720]
    c.tmp32 = a1f[:, 15360:15872]

    c.pP = [e(nc.psum_tensor("pP0", [128, 512], F32)),
            e(nc.psum_tensor("pP1", [128, 512], F32))]
    c.pBIG = e(nc.psum_tensor("pBIG", [128, 2048], F32))
    c.pS = [c.pBIG[:, 0:512], c.pBIG[:, 512:1024]]
    c.pO = [c.pBIG[:, 1024:1536], c.pBIG[:, 1536:2048]]
    c.pAUX = e(nc.psum_tensor("pAUX", [128, 512], F32))
    c.pAUX2 = e(nc.psum_tensor("pAUX2", [128, 512], F32))
    c.ps_kv = c.pAUX[:, 0:128]
    c.ps_kv2 = c.pAUX2[:, 0:128]
    c.ps_var = c.pAUX2[:, 256:272]

    def fmv(arena, fc):
        return arena[:, fc * T:(fc + 1) * T]

    c.fmv = fmv

    _skip = set(int(x) for x in os.environ.get("SKIP_STAGES", "").split(",") if x)

    def stage(i):
        return stages >= i and i not in _skip

    ph_consts(c)
    em.barrier(c.sems)
    if stage(1):
        ph_ln_transpose(c, dxp, arena1, "tap_xnfpre")
        em.barrier(c.sems)
    if stage(2):
        ph_prefix(c)
        em.barrier(c.sems)
        if "tap_kv" in taps:
            tap_store(c, c.tap_dram["tap_kv"], kvst[:, 0:H * HD])
            em.barrier(c.sems)
    if stage(3):
        ph_ln_transpose(c, dx, arena2, "tap_xnf")
        em.barrier(c.sems)
    if stage(4):
        ph_attn(c)
        em.barrier(c.sems)
        if "tap_outf" in taps:
            tap_store(c, c.tap_dram["tap_outf"], arena1[:, :])
            em.barrier(c.sems)
    if stage(5):
        ph_rms(c)
        em.barrier(c.sems)
    if stage(6):
        ph_gate(c)
        em.barrier(c.sems)
        if "tap_gout" in taps:
            tap_store(c, c.tap_dram["tap_gout"], arena1[:, :])
            em.barrier(c.sems)
    if stage(7):
        ph_outproj(c)
        em.barrier(c.sems)
        if "tap_x2" in taps:
            tap_store(c, c.tap_dram["tap_x2"], arena2[:, :])
            em.barrier(c.sems)
    if stage(8):
        ph_ffn(c)
    em.raw("sync", lambda e2, s=c.s["st"].h, v=c.s["st"].n: e2.wait_ge(s, v))

    with nc.Block() as block:
        @block.tensor
        def _(t):
            for f in em.q["tensor"]:
                f(t)

        @block.vector
        def _(v):
            for f in em.q["vector"]:
                f(v)

        @block.scalar
        def _(s2):
            for f in em.q["scalar"]:
                f(s2)

        @block.gpsimd
        def _(g):
            for f in em.q["gpsimd"]:
                f(g)

        @block.sync
        def _(sy):
            for f in em.q["sync"]:
                f(sy)

    big.close()
    return nc


def tap_store(c, dram, sbuf_ap):
    em, s = c.em, c.s
    n = sbuf_ap.shape[1]
    CH = 4096
    for i in range(0, n, CH):
        w = min(CH, n - i)
        em.op("sync", s["st"],
              lambda e, d=dram[:, i:i + w], src=sbuf_ap[:, i:i + w]: e.dma_start(d, src),
              16)
    em.wait("sync", s["st"])


def ph_consts(c):
    em, s = c.em, c.s
    _lite = int(os.environ.get("CONSTS_LITE", "0"))

    def g(fn):
        return em.op("gpsimd", s["gp"], fn)

    g(lambda e: e.memset(c.ones_col, 1.0))
    g(lambda e: e.memset(c.onesq, 1.0))
    em.wait("gpsimd", s["gp"])
    g(lambda e: e.affine_select(c.ident, c.onesq, [[-1, 128]], ALU.is_equal, 0.0,
                                base=0, channel_multiplier=1))
    g(lambda e: e.iota(c.iota_m1, [[1, 256]], base=1, channel_multiplier=0,
                       allow_small_or_imprecise_dtypes=True))
    g(lambda e: e.iota(c.iota_kd, [[-128, 2]], base=255, channel_multiplier=-1,
                       allow_small_or_imprecise_dtypes=True))
    g(lambda e: e.iota(c.idxT0, [[1, 256]], base=0, channel_multiplier=-1,
                       allow_small_or_imprecise_dtypes=True))
    g(lambda e: e.memset(c.CO("c256"), 256.0))
    g(lambda e: e.memset(c.CO("epsln"), EPS_LN))
    g(lambda e: e.memset(c.kvst[:, :], 0.0))
    g(lambda e: e.memset(c.Z("qkdt1"), 0.0))

    if _lite >= 3:
        return
    em.op("sync", s["st"],
          lambda e: e.dma_start(c.slpos, c.dram["sl"][None, :].broadcast_to((128, 16))),
          16)
    em.op("sync", s["st"],
          lambda e: e.dma_start(c.CO("nw"),
                                c.dram["nw"].rearrange("(cc p) -> p cc", p=128),
                                allow_slow_non_contiguous=True), 16)
    em.wait("vector", s["st"])
    em.op("vector", s["vc"],
          lambda e: e.tensor_scalar(c.CO("slneg"), c.slpos, -1.0, None, ALU.mult))

    em.wait("scalar", s["vc"])
    em.wait("scalar", s["gp"])
    em.wait("vector", s["gp"])
    if _lite >= 2:
        return
    for h in range(H):
        sl = c.CO("slneg")[:, h:h + 1]
        em.op("scalar", s["sc"],
              lambda e, h=h, sl=sl: e.activation(
                  c.Z("qdec")[:, h * 256:(h + 1) * 256], c.iota_m1, AF.Exp, scale=sl))
        em.op("scalar", s["sc"],
              lambda e, h=h, sl=sl: e.activation(
                  c.CO("kdec")[:, 2 * h:2 * h + 2], c.iota_kd, AF.Exp, scale=sl))
        em.op("scalar", s["sc"],
              lambda e, h=h, sl=sl: e.activation(
                  c.CO("bd")[:, h:h + 1], c.CO("c256"), AF.Exp, scale=sl))
        em.wait("vector", s["sc"])
        em.wait("vector", s["gp"])
        dm = em.op("vector", s["vc"],
                   lambda e, sl=sl: e.tensor_scalar(c.dt_scr, c.idxT0, sl, None, ALU.mult))
        em.wait("gpsimd", s["vc"], dm)
        af = em.op("gpsimd", s["gp"],
                   lambda e: e.affine_select(c.dt_scr, c.dt_scr, [[1, 256]], ALU.is_ge,
                                             -20000.0, base=0, channel_multiplier=-1))
        em.wait("scalar", s["gp"], af)
        em.op("scalar", s["sc"],
              lambda e, h=h: e.activation(
                  c.Z("dt")[:, h * 256:(h + 1) * 256], c.dt_scr, AF.Exp))


def ph_ln_transpose(c, src_dram, dst_arena, tapname):
    em, s = c.em, c.s
    import os
    _mode = os.environ.get("LN_MODE", "")
    _ser = os.environ.get("LN_SERIAL", "0") == "1"
    stats_done = [None, None]
    tp_done = [None, None]
    last_tv = [None, None]
    dst_v = dst_arena[:, :].rearrange("p (cc t) -> p cc t", cc=NCH)
    for tc in range(NCH):
        bi = tc % 2
        buf = c.xchunk[bi]
        if stats_done[bi] is not None:
            em.wait("gpsimd", s["vc"], stats_done[bi][0])
            em.wait("gpsimd", s["sc"], stats_done[bi][1])
        if _mode == "s3" and stats_done[1 - bi] is not None:
            em.wait("gpsimd", s["vc"], stats_done[1 - bi][0])
            em.wait("gpsimd", s["sc"], stats_done[1 - bi][1])
        sx = s["x0"] if bi == 0 else s["x1"]
        ld = em.op("gpsimd", sx,
                   lambda e, b=buf, d=src_dram, i=tc: e.dma_start(
                       b, d[i * 128:(i + 1) * 128, :]), 16)
        sfx = "b" if bi else ""
        S1, S2, MNEG, MM, TV, RC, ACOL, BCOL = (c.CO("s1" + sfx), c.CO("s2" + sfx),
                                                c.CO("mneg" + sfx), c.CO("mm" + sfx),
                                                c.CO("tv" + sfx), c.CO("rc" + sfx),
                                                c.CO("acol" + sfx), c.CO("bcol" + sfx))
        em.wait("scalar", sx, ld)
        if last_tv[bi] is not None:
            em.wait("scalar", s["vc"], last_tv[bi])
        sq = em.op("scalar", s["sc"],
                   lambda e, b=buf, S2=S2: e.activation(c.sqz, b, AF.Square,
                                                        accum_out=S2))
        em.wait("vector", sx, ld)
        em.opw("vector", s["vc"],
               lambda e, b=buf, S1=S1: e.reduce_sum(S1, b, axis=AX.X))
        em.wait("vector", s["sc"], sq)

        def vv(fn):
            return em.opw("vector", s["vc"], fn)

        vv(lambda e, S1=S1, MNEG=MNEG: e.tensor_scalar(MNEG, S1, -1.0 / D, None, ALU.mult))
        vv(lambda e, MNEG=MNEG, MM=MM: e.tensor_tensor(MM, MNEG, MNEG, ALU.mult))
        vv(lambda e, S2=S2, TV=TV: e.tensor_scalar(TV, S2, 1.0 / D, EPS_LN,
                                                   ALU.mult, ALU.add))
        last_tv[bi] = vv(lambda e, TV=TV, MM=MM: e.tensor_tensor(TV, TV, MM,
                                                                 ALU.subtract))
        rc = vv(lambda e, TV=TV, RC=RC: e.reciprocal(RC, TV))
        em.wait("scalar", s["vc"], rc)
        sq2 = em.op("scalar", s["sc"],
                    lambda e, RC=RC, ACOL=ACOL: e.activation(ACOL, RC, AF.Sqrt))
        em.wait("vector", s["sc"], sq2)
        bc = vv(lambda e, MNEG=MNEG, ACOL=ACOL, BCOL=BCOL: e.tensor_tensor(
            BCOL, MNEG, ACOL, ALU.mult))
        if _mode == "s1":
            em.barrier(c.sems)
        xb = c.xn_tok[bi]
        stp = s["t0"] if bi == 0 else s["t1"]
        if tp_done[bi] is not None:
            em.wait("vector", stp, tp_done[bi])
        ap = vv(lambda e, b=buf, o=xb, ACOL=ACOL, BCOL=BCOL: e.tensor_scalar(
            o, b, ACOL, BCOL, ALU.mult, ALU.add))
        stats_done[bi] = (ap, sq)
        if "tap_cols" in c.taps:
            em.wait("sync", s["vc"], ap)
            for ci, col in enumerate([S1, S2, MNEG, TV, RC, ACOL, BCOL]):
                em.op("sync", s["st"],
                      lambda e, tc=tc, ci=ci, col=col: e.dma_start(
                          c.tap_dram["tap_cols"][:, tc * 8 + ci:tc * 8 + ci + 1], col,
                          allow_slow_non_contiguous=True), 16)
        em.wait("sync", s["vc"], ap)
        tp_done[bi] = em.op("sync", stp,
                            lambda e, o=dst_v[:, :, tc * 128:(tc + 1) * 128], i=xb:
                            e.dma_start(o, i, transpose=True), 16)
    if tapname in c.taps and _mode != "notp":
        em.barrier(c.sems)
        tap_store(c, c.tap_dram[tapname], dst_arena[:, :])


def _load_w_slice(c, dram, col0, ncols, dst, sem, war_tn=None):
    em, s = c.em, c.s
    if war_tn is not None:
        em.wait("gpsimd", s["tn"], war_tn)
    src = dram[:, col0:col0 + ncols].rearrange("(cc p) j -> p cc j", p=128)
    dstv = dst.rearrange("p (cc j) -> p cc j", cc=NCH)
    return em.op("gpsimd", sem, lambda e: e.dma_start(dstv, src), 16)


def _vproj(c, xnF_arena, g, st):
    em, s = c.em, c.s
    wv_ld = _load_w_slice(c, c.dram["wv"], g * 256, 256, c.wv, s["wv"], st.get("wv_war"))
    wvv = c.wv.rearrange("p (cc j) -> p cc j", cc=NCH)
    em.wait("scalar", s["tn"])
    for tt in range(NCH):
        ps = c.pP[tt % 2]
        em.wait("tensor", s["wv"], wv_ld)
        pe = st["pP_evac"][tt % 2]
        if pe is not None:
            em.wait("tensor", s["sc"], pe)
        for fc in range(NCH):
            fn = (lambda e, ps=ps, fc=fc, tt=tt: e.matmul(
                ps[:, 0:256], c.fmv(xnF_arena, fc)[:, tt * 128:(tt + 1) * 128],
                wvv[:, fc, :], start=(fc == 0), stop=(fc == NCH - 1)))
            if fc < NCH - 1:
                em.qop("tensor", fn)
            else:
                em.op("tensor", s["tn"], fn)
        em.wait("scalar", s["tn"])
        st["pP_evac"][tt % 2] = em.op("scalar", s["sc"],
                                      lambda e, ps=ps, tt=tt: e.activation(
                                          c.vbuf[:, tt * 256:(tt + 1) * 256],
                                          ps[:, 0:256], AF.Silu))
    st["wv_war"] = c.s["tn"].n


def _fproj(c, xnF_arena, wdram, col0, dstbuf, wbuf, st, wkey, wsem):
    em, s = c.em, c.s
    wld = _load_w_slice(c, wdram, col0, 128, wbuf, wsem, st.get(wkey))
    wv = wbuf.rearrange("p (cc j) -> p cc j", cc=NCH)
    em.wait("scalar", s["tn"])
    em.wait("scalar", s["tp"])
    for nt in range(4):
        ps = c.pP[nt % 2]
        em.wait("tensor", wsem, wld)
        pe = st["pP_evac"][nt % 2]
        if pe is not None:
            em.wait("tensor", s["sc"], pe)
        for fc in range(NCH):
            fn = (lambda e, ps=ps, fc=fc, nt=nt, w=wv: e.matmul(
                ps[:, :], w[:, fc, :],
                c.fmv(xnF_arena, fc)[:, nt * 512:(nt + 1) * 512],
                start=(fc == 0), stop=(fc == NCH - 1)))
            if fc < NCH - 1:
                em.qop("tensor", fn)
            else:
                em.op("tensor", s["tn"], fn)
        em.wait("scalar", s["tn"])
        st["pP_evac"][nt % 2] = em.op("scalar", s["sc"],
                                      lambda e, ps=ps, nt=nt, d=dstbuf: e.activation(
                                          d[:, nt * 512:(nt + 1) * 512], ps[:, :],
                                          AF.Silu))
    st[wkey] = c.s["tn"].n


def _ktranspose(c, h):
    em, s = c.em, c.s
    ktv = c.Z("ktbuf").rearrange("p (cc t) -> p cc t", cc=NCH)
    em.wait("sync", s["sc"])
    em.wait("sync", s["tn"])
    em.wait("sync", s["vc"])
    tp = em.op("sync", s["tp"],
               lambda e: e.dma_start(ktv, c.Z("kbuf"), transpose=True), 16)
    em.wait("vector", s["tp"], tp)
    last = None
    for tile in range(NCH):
        u = tile % 2
        last = em.op("vector", s["vc"],
                     lambda e, t=tile, u=u, h=h: e.tensor_scalar(
                         c.Z("ktbuf")[:, t * 128:(t + 1) * 128],
                         c.Z("ktbuf")[:, t * 128:(t + 1) * 128],
                         c.CO("kdec")[:, 2 * h + u:2 * h + u + 1], None, ALU.mult))
    return last


def _kv_chain(c, h, hg, ktiles_ready):
    em, s = c.em, c.s
    kvs = c.kvst[:, h * 128:(h + 1) * 128]
    stt_done = None
    em.wait("tensor", s["sc"])
    for j in range(NBLK):
        em.wait("tensor", s["vc"], stt_done if stt_done is not None else ktiles_ready)
        for u in range(2):
            tile = 2 * j + u
            fn = (lambda e, t=tile, u=u, hg=hg: e.matmul(
                c.ps_kv, c.Z("ktbuf")[:, t * 128:(t + 1) * 128],
                c.vbuf[:, t * 256 + hg * 128:t * 256 + (hg + 1) * 128],
                start=(u == 0), stop=(u == 1)))
            if u == 0:
                em.qop("tensor", fn)
            else:
                em.op("tensor", s["tn"], fn)
        em.wait("vector", s["tn"])
        stt_done = em.opw("vector", s["vc"],
                          lambda e, kvs=kvs, h=h: e.scalar_tensor_tensor(
                              kvs, kvs, c.CO("bd")[:, h:h + 1], c.ps_kv,
                              ALU.mult, ALU.add))


def ph_prefix(c):
    em, s = c.em, c.s
    st = {"pP_evac": [None, None]}
    for g in range(H // 2):
        _vproj(c, c.arena1, g, st)
        for hg in range(2):
            h = 2 * g + hg
            _fproj(c, c.arena1, c.dram["wk"], h * 128, c.Z("kbuf"), c.Z("wk"), st, "wk_war", s["wk"])
            kr = _ktranspose(c, h)
            _kv_chain(c, h, hg, kr)


def ph_attn(c):
    em, s = c.em, c.s
    st = {"pP_evac": [None, None]}
    for g in range(int(os.environ.get("ATTN_GROUPS", str(H // 2)))):
        _vproj(c, c.arena2, g, st)
        for hg in range(2):
            h = 2 * g + hg
            _fproj(c, c.arena2, c.dram["wq"], h * 128, c.Z("qbuf"), c.Z("wq"), st, "wq_war", s["wq"])
            _fproj(c, c.arena2, c.dram["wk"], h * 128, c.Z("kbuf"), c.Z("wk"), st, "wk_war", s["wk"])
            kr = _ktranspose(c, h)
            if "tap_q0" in c.taps and h == 0:
                em.barrier(c.sems)
                tap_store(c, c.tap_dram["tap_q0"], c.Z("qbuf"))
                tap_store(c, c.tap_dram["tap_k0"], c.Z("kbuf"))
                tap_store(c, c.tap_dram["tap_v0"], c.vbuf)
            _scan(c, h, hg, kr)


def _scan(c, h, hg, ktiles_ready):
    em, s = c.em, c.s
    kvs = c.kvst[:, h * 128:(h + 1) * 128]
    qdec = c.Z("qdec")[:, h * 256:(h + 1) * 256]
    dt = c.Z("dt")[:, h * 256:(h + 1) * 256]
    q0, q1 = c.Z("qkdt0"), c.Z("qkdt1")
    stt_done = None
    evac_done = [None, None]
    out_mm_prev = None
    em.wait("tensor", s["sc"])
    em.wait("tensor", s["vc"])
    for j in range(NBLK):
        ping = j % 2
        pS = c.pS[ping]
        pO = c.pO[ping]
        if evac_done[ping] is not None:
            em.wait("tensor", s["vc"])
        for u in range(2):
            fn = (lambda e, j=j, u=u, pS=pS: e.matmul(
                pS[:, u * 256:(u + 1) * 256],
                c.Z("kbuf")[:, j * 256 + u * 128:j * 256 + (u + 1) * 128],
                c.Z("qbuf")[:, j * 256:(j + 1) * 256], start=True, stop=True))
            if u == 0:
                em.qop("tensor", fn)
            else:
                em.op("tensor", s["tn"], fn)
        sc_mm = c.s["tn"].n
        em.wait("vector", s["tn"], sc_mm)
        if out_mm_prev is not None:
            em.wait("vector", s["tn"], out_mm_prev)
        em.op("vector", s["vc"],
              lambda e, pS=pS: e.tensor_tensor(q0, pS[:, 0:256], dt, ALU.mult))
        em.op("vector", s["vc"],
              lambda e, pS=pS: e.tensor_tensor(
                  q1[:, 128:256], pS[:, 384:512], dt[:, 0:128], ALU.mult))
        qt = em.op("vector", s["vc"],
                   lambda e, j=j: e.tensor_tensor(
                       c.Z("qbuf")[:, j * 256:(j + 1) * 256],
                       c.Z("qbuf")[:, j * 256:(j + 1) * 256], qdec, ALU.mult))
        em.wait("scalar", s["vc"], stt_done if stt_done is not None else ktiles_ready)
        if out_mm_prev is not None:
            em.wait("scalar", s["tn"], out_mm_prev)
        kc = em.op("scalar", s["sc"], lambda e: e.copy(c.Z("kv_bf"), kvs))
        em.wait("tensor", s["sc"], kc)
        em.wait("tensor", s["vc"], qt)
        if evac_done[ping] is not None:
            em.wait("tensor", s["sc"], evac_done[ping])
        em.qop("tensor",
               lambda e, pO=pO, j=j: e.matmul(
                   pO[:, 0:256], c.Z("kv_bf"),
                   c.Z("qbuf")[:, j * 256:(j + 1) * 256], start=True, stop=False))
        for u in range(2):
            qk = q0 if u == 0 else q1
            fn = (lambda e, pO=pO, j=j, u=u, hg=hg, qk=qk: e.matmul(
                pO[:, 0:256],
                c.vbuf[:, (2 * j + u) * 256 + hg * 128:
                       (2 * j + u) * 256 + (hg + 1) * 128],
                qk, start=False, stop=(u == 1)))
            if u == 0:
                em.qop("tensor", fn)
            else:
                em.op("tensor", s["tn"], fn)
        out_mm = c.s["tn"].n
        em.wait("scalar", s["tn"], out_mm)
        evac_done[ping] = em.op("scalar", s["sc"],
                                lambda e, pO=pO, h=h, j=j: e.copy(
                                    c.arena1[:, h * 2048 + j * 256:
                                             h * 2048 + (j + 1) * 256], pO[:, 0:256]))
        em.wait("tensor", s["vc"], stt_done if stt_done is not None else ktiles_ready)
        for u in range(2):
            tile = 2 * j + u
            fn = (lambda e, t=tile, u=u, hg=hg: e.matmul(
                c.ps_kv, c.Z("ktbuf")[:, t * 128:(t + 1) * 128],
                c.vbuf[:, t * 256 + hg * 128:t * 256 + (hg + 1) * 128],
                start=(u == 0), stop=(u == 1)))
            if u == 0:
                em.qop("tensor", fn)
            else:
                em.op("tensor", s["tn"], fn)
        em.wait("vector", s["tn"])
        stt_done = em.opw("vector", s["vc"],
                          lambda e, kvs=kvs, h=h: e.scalar_tensor_tensor(
                              kvs, kvs, c.CO("bd")[:, h:h + 1], c.ps_kv,
                              ALU.mult, ALU.add))
        out_mm_prev = out_mm


def ph_rms(c):
    em, s = c.em, c.s
    acc_done = None
    for oc in range(NCH):
        em.wait("vector", s["tn"])
        sq = em.op("vector", s["vc"],
                   lambda e, oc=oc: e.tensor_tensor(
                       c.sqz, c.arena1[:, oc * T:(oc + 1) * T],
                       c.arena1[:, oc * T:(oc + 1) * T], ALU.mult))
        em.wait("tensor", s["vc"], sq)
        if acc_done is not None:
            em.wait("tensor", s["vc"], acc_done)
        for tc in range(NCH):
            fn = (lambda e, tc=tc: e.matmul(
                c.ps_var[:, tc:tc + 1], c.sqz[:, tc * 128:(tc + 1) * 128],
                c.ones_col, start=True, stop=True))
            if tc < NCH - 1:
                em.qop("tensor", fn)
            else:
                em.op("tensor", s["tn"], fn)
        em.wait("vector", s["tn"])
        if oc == 0:
            acc_done = em.opw("vector", s["vc"],
                              lambda e: e.tensor_copy(c.CO("var16"), c.ps_var))
        else:
            acc_done = em.opw("vector", s["vc"],
                              lambda e: e.tensor_tensor(c.CO("var16"), c.CO("var16"),
                                                        c.ps_var, ALU.add))

    def vv(fn):
        return em.opw("vector", s["vc"], fn)

    vv(lambda e: e.tensor_scalar(c.CO("var16"), c.CO("var16"), 1.0 / D, EPS_RMS,
                                 ALU.mult, ALU.add))
    rc = vv(lambda e: e.reciprocal(c.CO("var16"), c.CO("var16")))
    em.wait("scalar", s["vc"], rc)
    sq = em.op("scalar", s["sc"],
               lambda e: e.activation(c.CO("a2"), c.CO("var16"), AF.Sqrt))
    em.wait("vector", s["sc"], sq)
    vv(lambda e: e.tensor_tensor(c.CO("a2sq"), c.CO("a2"), c.CO("a2"), ALU.mult))


def ph_gate(c):
    em, s = c.em, c.s
    for oc in range(NCH):
        em.op("gpsimd", s["wo"],
              lambda e, oc=oc: e.dma_start(
                  c.arena3[:, oc * T:(oc + 1) * T],
                  c.dram["wo"][oc * 128:(oc + 1) * 128, :]), 16)
    em.wait("vector", s["wo"])
    for oc in range(NCH):
        em.op("vector", s["vc"],
              lambda e, oc=oc: e.tensor_scalar(
                  c.arena3[:, oc * T:(oc + 1) * T], c.arena3[:, oc * T:(oc + 1) * T],
                  c.CO("nw")[:, oc:oc + 1], None, ALU.mult))
    prev_mm = [None, None]
    for gc in range(NCH):
        wb = c.wg[gc % 2]
        swg = s["wg0"] if gc % 2 == 0 else s["wg1"]
        if prev_mm[gc % 2] is not None:
            em.wait("gpsimd", s["tn"], prev_mm[gc % 2])
        ld = em.op("gpsimd", swg,
                   lambda e, gc=gc, wb=wb: e.dma_start(
                       wb.rearrange("p (cc j) -> p cc j", cc=NCH),
                       c.dram["wg"][:, gc * 128:(gc + 1) * 128].rearrange(
                           "(cc p) j -> p cc j", p=128)), 16)
        wv = wb.rearrange("p (cc j) -> p cc j", cc=NCH)
        for nt in range(4):
            ps = c.pP[nt % 2]
            em.wait("tensor", swg, ld)
            em.wait("tensor", s["sc"])
            for fc in range(NCH):
                fn = (lambda e, ps=ps, fc=fc, nt=nt, w=wv: e.matmul(
                    ps[:, :], w[:, fc, :],
                    c.fmv(c.arena2, fc)[:, nt * 512:(nt + 1) * 512],
                    start=(fc == 0), stop=(fc == NCH - 1)))
                if fc < NCH - 1:
                    em.qop("tensor", fn)
                else:
                    em.op("tensor", s["tn"], fn)
            em.wait("scalar", s["tn"])
            em.wait("scalar", s["vc"])
            sg = em.op("scalar", s["sc"],
                       lambda e, ps=ps, nt=nt: e.activation(
                           c.sig[nt % 2], ps[:, :], AF.Sigmoid))
            em.wait("vector", s["sc"], sg)
            em.op("vector", s["vc"],
                  lambda e, gc=gc, nt=nt: e.tensor_tensor(
                      c.arena1[:, gc * T + nt * 512:gc * T + (nt + 1) * 512],
                      c.arena1[:, gc * T + nt * 512:gc * T + (nt + 1) * 512],
                      c.sig[nt % 2], ALU.mult))
        prev_mm[gc % 2] = c.s["tn"].n


def ph_outproj(c):
    import os
    _lvl = int(os.environ.get("P4_LEVEL", "3"))
    em, s = c.em, c.s
    evac_done = None
    _ntc = int(os.environ.get("P4_NTC", str(NCH)))
    for tc in range(_ntc):
        if evac_done is not None:
            em.wait("gpsimd", s["vc"], evac_done)
        if _lvl >= 1:
            ld = em.op("gpsimd", s["xp4"],
                       lambda e, tc=tc: e.dma_start(
                           c.xchunk_p4, c.dram["x"][tc * 128:(tc + 1) * 128, :]), 16)
        if evac_done is not None:
            em.wait("tensor", s["vc"], evac_done)
            em.wait("tensor", s["sc"])
        for ft in range(4):
            for kc in range(NCH):
                fn = (lambda e, tc=tc, ft=ft, kc=kc: e.matmul(
                    c.pBIG[:, ft * 512:(ft + 1) * 512],
                    c.arena1[:, kc * T + tc * 128:kc * T + (tc + 1) * 128],
                    c.arena3[:, kc * T + ft * 512:kc * T + (ft + 1) * 512],
                    start=(kc == 0), stop=(kc == NCH - 1)))
                if ft == 3 and kc == NCH - 1:
                    em.op("tensor", s["tn"], fn)
                else:
                    em.qop("tensor", fn)
        mm = c.s["tn"].n
        em.wait("scalar", s["tn"], mm)
        cp1 = em.op("scalar", s["sc"],
                    lambda e: e.activation(c.sq_p4_bf, c.pBIG[:, :], AF.Copy,
                                           accum_out=c.CO("s1")))
        em.wait("scalar", s["sc"], cp1)
        sq = em.op("scalar", s["sc"],
                   lambda e: e.activation(c.sq_p4_bf, c.pBIG[:, :], AF.Square,
                                          accum_out=c.CO("s2")))
        em.wait("vector", s["sc"], sq)

        def vv(fn):
            return em.opw("vector", s["vc"], fn)

        vv(lambda e: e.tensor_scalar(c.CO("mneg"), c.CO("s1"), -1.0 / D, None, ALU.mult))
        vv(lambda e: e.tensor_tensor(c.CO("mm"), c.CO("mneg"), c.CO("mneg"), ALU.mult))
        vv(lambda e: e.tensor_scalar(c.CO("tv"), c.CO("s2"), 1.0 / D, None, ALU.mult))
        vv(lambda e: e.tensor_tensor(c.CO("tv"), c.CO("tv"), c.CO("mm"), ALU.subtract))
        vv(lambda e, tc=tc: e.scalar_tensor_tensor(
            c.CO("tv"), c.CO("tv"), c.CO("a2sq")[:, tc:tc + 1], c.CO("epsln"),
            ALU.mult, ALU.add))
        rc = vv(lambda e: e.reciprocal(c.CO("rc"), c.CO("tv")))
        em.wait("scalar", s["vc"], rc)
        sq2 = em.op("scalar", s["sc"],
                    lambda e: e.activation(c.CO("acol"), c.CO("rc"), AF.Sqrt))
        em.wait("vector", s["sc"], sq2)
        vv(lambda e, tc=tc: e.tensor_tensor(
            c.CO("c1"), c.CO("a2")[:, tc:tc + 1], c.CO("acol"), ALU.mult))
        vv(lambda e: e.tensor_tensor(c.CO("bcol"), c.CO("mneg"), c.CO("c1"),
                                     ALU.mult))
        if _lvl >= 1:
            em.wait("vector", s["xp4"], ld)
        vv(lambda e: e.tensor_scalar(c.xchunk_p4, c.xchunk_p4, c.CO("bcol"), None,
                                     ALU.add))
        evac_done = vv(lambda e, tc=tc: e.scalar_tensor_tensor(
            c.arena2[:, tc * 2048:(tc + 1) * 2048], c.pBIG[:, :], c.CO("c1"),
            c.xchunk_p4, ALU.mult, ALU.add))


def ph_ffn(c):
    em, s = c.em, c.s
    for kc in range(NCH):
        em.op("gpsimd", s["wf"],
              lambda e, kc=kc: e.dma_start(
                  c.arena3[:, kc * T:(kc + 1) * T],
                  c.dram["wf"][kc * 128:(kc + 1) * 128, :]), 16)
    wf_ld = c.s["wf"].n
    em.op("gpsimd", s["cst"],
          lambda e: e.dma_start(c.wsb, c.dram["ws"][None, :].broadcast_to((128, D))), 16)
    em.op("gpsimd", s["cst"],
          lambda e: e.dma_start(c.biasb[:, :],
                                c.dram["fb"][None, :].broadcast_to((128, D))), 16)

    qxv = c.qxf.rearrange("p (cc t) -> p cc t", cc=NCH)
    tp_done = [None, None]
    store_done = None
    prev_ep = None
    for hf in range(2):
        if hf == 1:
            em.wait("sync", s["tn"])
        for ti in range(8):
            tc = hf * 8 + ti
            x2c = c.arena2[:, tc * 2048:(tc + 1) * 2048]
            em.wait("scalar", s["vc"])
            sq = em.op("scalar", s["sc"],
                       lambda e, x2c=x2c: e.activation(c.Xbf[:, 0:2048], x2c, AF.Square,
                                                       accum_out=c.CO("s2")))
            em.opw("vector", s["vc"],
                   lambda e, x2c=x2c: e.reduce_sum(c.CO("s1"), x2c, axis=AX.X))
            em.opw("vector", s["vc"],
                   lambda e, x2c=x2c: e.tensor_reduce(c.CO("mx"), x2c, AX.X, ALU.max))
            em.opw("vector", s["vc"],
                   lambda e, x2c=x2c: e.tensor_reduce(c.CO("mn"), x2c, AX.X, ALU.min))
            em.wait("vector", s["sc"], sq)

            def vv(fn):
                return em.opw("vector", s["vc"], fn)

            vv(lambda e: e.tensor_scalar(c.CO("mneg"), c.CO("s1"), -1.0 / D, None,
                                         ALU.mult))
            vv(lambda e: e.tensor_tensor(c.CO("mm"), c.CO("mneg"), c.CO("mneg"),
                                         ALU.mult))
            vv(lambda e: e.tensor_scalar(c.CO("tv"), c.CO("s2"), 1.0 / D, EPS_LN,
                                         ALU.mult, ALU.add))
            vv(lambda e: e.tensor_tensor(c.CO("tv"), c.CO("tv"), c.CO("mm"),
                                         ALU.subtract))
            rc = vv(lambda e: e.reciprocal(c.CO("rc"), c.CO("tv")))
            em.wait("scalar", s["vc"], rc)
            sq2 = em.op("scalar", s["sc"],
                        lambda e: e.activation(c.CO("acol"), c.CO("rc"), AF.Sqrt))
            em.wait("vector", s["sc"], sq2)
            vv(lambda e: e.tensor_scalar(c.CO("t1"), c.CO("mx"), c.CO("mneg"), None,
                                         ALU.add))
            vv(lambda e: e.tensor_scalar(c.CO("t2"), c.CO("mn"), c.CO("mneg"), -1.0,
                                         ALU.add, ALU.mult))
            vv(lambda e: e.tensor_tensor(c.CO("t1"), c.CO("t1"), c.CO("t2"), ALU.max))
            vv(lambda e: e.tensor_tensor(c.CO("t1"), c.CO("t1"), c.CO("acol"), ALU.mult))
            vv(lambda e: e.tensor_scalar(c.CO("scol"), c.CO("t1"), EPS_Q, 1.0 / 127.0,
                                         ALU.max, ALU.mult))
            vv(lambda e: e.reciprocal(c.CO("invs"), c.CO("scol")))
            vv(lambda e: e.tensor_tensor(c.CO("c1"), c.CO("acol"), c.CO("invs"),
                                         ALU.mult))
            vv(lambda e: e.tensor_tensor(c.CO("c2"), c.CO("mneg"), c.CO("c1"),
                                         ALU.mult))
            vv(lambda e, tc=tc: e.tensor_copy(c.CO("insc")[:, tc:tc + 1], c.CO("scol")))
            bi = ti % 2
            vv(lambda e, x2c=x2c: e.tensor_scalar(c.X32, x2c, c.CO("c1"), c.CO("c2"),
                                                  ALU.mult, ALU.add))
            vv(lambda e: e.tensor_scalar(c.X32, c.X32, MAGIC, MAGIC, ALU.add,
                                         ALU.subtract))
            stq = s["t0"] if bi == 0 else s["t1"]
            if tp_done[bi] is not None:
                em.wait("vector", stq, tp_done[bi])
            qd = vv(lambda e, bi=bi: e.tensor_scalar(c.qf[bi], c.X32, 127.0, -128.0,
                                                     ALU.min, ALU.max))
            em.wait("sync", s["vc"], qd)
            tp_done[bi] = em.op("sync", stq,
                                lambda e, bi=bi, ti=ti: e.dma_start(
                                    qxv[:, :, ti * 128:(ti + 1) * 128], c.qf[bi],
                                    transpose=True), 16)
        if "tap_qxf" in c.taps and hf == 0:
            em.barrier(c.sems)
            tap_store(c, c.tap_dram["tap_qxf"], c.qxf)
        if "tap_insc" in c.taps and hf == 1:
            em.barrier(c.sems)
            tap_store(c, c.tap_dram["tap_insc"], c.CO("insc"))
        em.wait("tensor", s["wf"], wf_ld)
        em.wait("tensor", s["t0"])
        em.wait("tensor", s["t1"])
        em.wait("vector", s["cst"])
        for ti in range(8):
            tc = hf * 8 + ti
            if prev_ep is not None:
                em.wait("tensor", s["vc"], prev_ep)
            for ft in range(4):
                for kc in range(NCH):
                    fn = (lambda e, ti=ti, ft=ft, kc=kc: e.matmul(
                        c.pBIG[:, ft * 512:(ft + 1) * 512],
                        c.qxf[:, kc * 1024 + ti * 128:kc * 1024 + (ti + 1) * 128],
                        c.arena3[:, kc * T + ft * 512:kc * T + (ft + 1) * 512],
                        start=(kc == 0), stop=(kc == NCH - 1)))
                    if ft == 3 and kc == NCH - 1:
                        em.op("tensor", s["tn"], fn)
                    else:
                        em.qop("tensor", fn)
            mm = c.s["tn"].n
            em.wait("vector", s["tn"], mm)
            for ft in range(4):
                em.opw("vector", s["vc"],
                       lambda e, ft=ft, tc=tc: e.scalar_tensor_tensor(
                           c.tmp32, c.pBIG[:, ft * 512:(ft + 1) * 512],
                           c.CO("insc")[:, tc:tc + 1],
                           c.wsb[:, ft * 512:(ft + 1) * 512], ALU.mult, ALU.mult))
                em.opw("vector", s["vc"],
                       lambda e, ft=ft: e.tensor_tensor(
                           c.ffn16[:, ft * 512:(ft + 1) * 512], c.tmp32,
                           c.biasb[:, ft * 512:(ft + 1) * 512], ALU.add))
            prev_ep = c.s["vc"].n
            em.wait("scalar", s["vc"], prev_ep)
            if store_done is not None:
                em.wait("scalar", s["st"], store_done)
                em.wait("vector", s["st"], store_done)
            sq = em.op("scalar", s["sc"],
                       lambda e: e.activation(c.outbuf, c.ffn16, AF.Square,
                                              accum_out=c.CO("s2")))
            em.opw("vector", s["vc"],
                   lambda e: e.reduce_sum(c.CO("s1"), c.ffn16, axis=AX.X))

            def vv(fn):
                return em.opw("vector", s["vc"], fn)

            vv(lambda e: e.tensor_scalar(c.CO("mneg"), c.CO("s1"), -1.0 / D, None,
                                         ALU.mult))
            vv(lambda e: e.tensor_tensor(c.CO("mm"), c.CO("mneg"), c.CO("mneg"),
                                         ALU.mult))
            em.wait("vector", s["sc"], sq)
            vv(lambda e: e.tensor_scalar(c.CO("tv"), c.CO("s2"), 1.0 / D, EPS_LN,
                                         ALU.mult, ALU.add))
            vv(lambda e: e.tensor_tensor(c.CO("tv"), c.CO("tv"), c.CO("mm"),
                                         ALU.subtract))
            rc = vv(lambda e: e.reciprocal(c.CO("rc"), c.CO("tv")))
            em.wait("scalar", s["vc"], rc)
            sq2 = em.op("scalar", s["sc"],
                        lambda e: e.activation(c.CO("acol"), c.CO("rc"), AF.Sqrt))
            em.wait("vector", s["sc"], sq2)
            vv(lambda e: e.tensor_tensor(c.CO("bcol"), c.CO("mneg"),
                                         c.CO("acol"), ALU.mult))
            vv(lambda e: e.tensor_scalar(c.outbuf, c.ffn16, c.CO("acol"), c.CO("bcol"),
                                         ALU.mult, ALU.add))
            od = vv(lambda e, tc=tc: e.tensor_tensor(
                c.outbuf, c.outbuf, c.arena2[:, tc * 2048:(tc + 1) * 2048], ALU.add))
            em.wait("sync", s["vc"], od)
            store_done = em.op("sync", s["st"],
                               lambda e, tc=tc: e.dma_start(
                                   c.dram["out"][tc * 128:(tc + 1) * 128, :],
                                   c.outbuf), 16)


def _host_prep(inputs):
    x = np.asarray(inputs["L0_attn_hidden_states"], dtype=np.float32)
    slopes = np.ascontiguousarray(
        np.asarray(inputs["L0_attn_slope_rate"], dtype=np.float32).reshape(H))
    wqkv = np.asarray(inputs["L0_attn_w_qkv"], dtype=np.float32)
    r = wqkv.reshape(H, 3, HD, D)
    wqT = np.ascontiguousarray(r[:, 0].transpose(2, 0, 1).reshape(D, D))
    wkT = np.ascontiguousarray(r[:, 1].transpose(2, 0, 1).reshape(D, D))
    wvT = np.ascontiguousarray(r[:, 2].transpose(2, 0, 1).reshape(D, D))
    wgT = np.ascontiguousarray(np.asarray(inputs["L0_attn_w_gate"], np.float32).T)
    woT = np.ascontiguousarray(np.asarray(inputs["L0_attn_w_out"], np.float32).T)
    wfT = np.ascontiguousarray(np.asarray(inputs["L0_ffn_weight"]).T)
    normw = np.ascontiguousarray(np.asarray(inputs["L0_attn_norm_w"], np.float32))
    wscale = np.ascontiguousarray(
        np.asarray(inputs["L0_ffn_w_scale"], np.float32).reshape(D))
    fbias = np.ascontiguousarray(np.asarray(inputs["L0_ffn_bias"], np.float32))
    zeros = np.zeros((T, D), np.float32)
    in_maps = []
    for core in range(8):
        b, hf = core // 2, core % 2
        x_own = np.ascontiguousarray(x[b, hf * T:(hf + 1) * T])
        x_pre = np.ascontiguousarray(x[b, 0:T]) if hf == 1 else zeros
        in_maps.append(dict(x_own=x_own, x_pre=x_pre, wqT=wqT, wkT=wkT, wvT=wvT,
                            wgT=wgT, woT=woT, wfT=wfT, slopes=slopes, normw=normw,
                            wscale=wscale, fbias=fbias))
    return x, in_maps


def kernel(**inputs):
    x, in_maps = _host_prep(inputs)
    nc = build_nc()
    res = run_bass_kernel_spmd(nc, in_maps, core_ids=list(range(8)))
    out = np.empty((4, 4096, D), np.float32)
    for core in range(8):
        b, hf = core // 2, core % 2
        out[b, hf * T:(hf + 1) * T, :] = res.results[core]["out"]
    return out


# version 03
# speedup vs baseline: 1.0694x; 1.0694x over previous
import os
import numpy as np
from contextlib import ExitStack

import concourse.bass as bass
import concourse.mybir as mybir
from concourse.bass_utils import run_bass_kernel_spmd

F32 = mybir.dt.float32
BF16 = mybir.dt.bfloat16
F16 = mybir.dt.float16
I8 = mybir.dt.int8
AF = mybir.ActivationFunctionType
ALU = mybir.AluOpType
AX = mybir.AxisListType

T = 2048
D = 2048
H = 16
HD = 128
NCH = 16
NBLK = 8
MAGIC = 12582912.0
EPS_LN = 1e-5
EPS_RMS = 1e-6
EPS_Q = 1e-7


class Sem:
    def __init__(self, ctx, nc, name):
        self.h = ctx.enter_context(nc.semaphore(name))
        self.n = 0

    def bump(self, k):
        self.n += k
        return self.n


class Em:

    ENGS = ("tensor", "vector", "scalar", "gpsimd", "sync")

    def __init__(self):
        self.q = {e: [] for e in self.ENGS}
        self.lastwait = {e: {} for e in self.ENGS}
        self.trace = {e: [] for e in self.ENGS}

    def wait(self, eng, sem, cnt=None):
        cnt = sem.n if cnt is None else cnt
        if cnt is None or cnt <= 0:
            return
        lw = self.lastwait[eng]
        if lw.get(id(sem), -1) >= cnt:
            return
        lw[id(sem)] = cnt
        self.trace[eng].append(("wait", sem.h.name, cnt))
        self.q[eng].append(lambda e, s=sem.h, c=cnt: e.wait_ge(s, c))

    def qop(self, eng, fn):
        self.q[eng].append(fn)

    def op(self, eng, sem, fn, k=1):
        v = sem.bump(k)
        import inspect
        try:
            srcl = inspect.getsource(fn).strip().split("\n")[0][:80]
        except Exception:
            srcl = "?"
        self.trace[eng].append(("op", sem.h.name, v, srcl))
        self.q[eng].append(lambda e, f=fn, s=sem.h, kk=k: f(e).then_inc(s, kk))
        return v

    def opw(self, eng, sem, fn, k=1):
        v = self.op(eng, sem, fn, k)
        self.wait(eng, sem, v)
        return v

    def raw(self, eng, fn):
        self.q[eng].append(fn)

    def barrier(self, sems):
        for eng in self.ENGS:
            for s in sems:
                self.wait(eng, s)


class C:
    pass


def build_nc(taps=(), stages=99):
    taps = set(taps)
    nc = bass.Bass("TRN2", target_bir_lowering=False)
    c = C()
    c.nc = nc
    c.taps = taps

    dx = nc.dram_tensor("x_own", [T, D], F32, kind="ExternalInput")
    dxp = nc.dram_tensor("x_pre", [T, D], F32, kind="ExternalInput")
    dwq = nc.dram_tensor("wqT", [D, D], F32, kind="ExternalInput")
    dwk = nc.dram_tensor("wkT", [D, D], F32, kind="ExternalInput")
    dwv = nc.dram_tensor("wvT", [D, D], F32, kind="ExternalInput")
    dwg = nc.dram_tensor("wgT", [D, D], F32, kind="ExternalInput")
    dwo = nc.dram_tensor("woT", [D, D], F32, kind="ExternalInput")
    dwf = nc.dram_tensor("wfT", [D, D], I8, kind="ExternalInput")
    dsl = nc.dram_tensor("slopes", [H], F32, kind="ExternalInput")
    dnw = nc.dram_tensor("normw", [D], F32, kind="ExternalInput")
    dws = nc.dram_tensor("wscale", [D], F32, kind="ExternalInput")
    dfb = nc.dram_tensor("fbias", [D], F32, kind="ExternalInput")
    dout = nc.dram_tensor("out", [T, D], F32, kind="ExternalOutput")
    c.dram = dict(x=dx, xp=dxp, wq=dwq, wk=dwk, wv=dwv, wg=dwg, wo=dwo,
                  wf=dwf, sl=dsl, nw=dnw, ws=dws, fb=dfb, out=dout)
    c.tap_dram = {}
    for name, shape, dt in [
        ("tap_xnfpre", [128, NCH * T], BF16), ("tap_xnf", [128, NCH * T], BF16),
        ("tap_kv", [128, H * HD], F32), ("tap_q0", [128, T], BF16),
        ("tap_k0", [128, T], BF16), ("tap_v0", [128, 4096], BF16),
        ("tap_outf", [128, NCH * T], BF16), ("tap_gout", [128, NCH * T], BF16),
        ("tap_x2", [128, NCH * T], BF16), ("tap_qxf", [128, NCH * 1024], BF16),
        ("tap_insc", [128, 16], F32),
        ("tap_cols", [128, 16 * 8], F32),
    ]:
        if name in taps:
            c.tap_dram[name] = nc.dram_tensor(name, shape, dt, kind="ExternalOutput")

    big = ExitStack()
    e = big.enter_context

    em = Em()
    c.em = em
    names = ["wl", "tp", "st", "tn", "vc", "sc", "gp",
             "x0", "x1", "t0", "t1", "wq", "wk", "wv", "wg0", "wg1",
             "wo", "wf", "cst", "xp4", "tp2"]
    c.s = {nm: Sem(big, nc, "s_" + nm) for nm in names}
    c.sems = list(c.s.values())

    arena1 = e(nc.sbuf_tensor("arena1", [128, 32768], BF16))
    arena2 = e(nc.sbuf_tensor("arena2", [128, 32768], BF16))
    arena3 = e(nc.sbuf_tensor("arena3", [128, 32768], BF16))
    c.arena1, c.arena2, c.arena3 = arena1, arena2, arena3

    z = {}
    off = 0

    def zalloc(name, n):
        nonlocal off
        z[name] = (off, off + n)
        off += n

    zalloc("qdec", H * 256)
    zalloc("dt", H * 256)
    zalloc("qbuf", T)
    zalloc("kbuf", T)
    zalloc("ktbuf", T)
    zalloc("wq", T)
    zalloc("wk", T)
    zalloc("qkdt0", 256)
    zalloc("qkdt1", 256)
    zalloc("kv_bf", 128)
    zalloc("kv_bf2", 128)
    zalloc("qkdt0b", 256)
    zalloc("qkdt1b", 256)
    zalloc("kbuf2", T)
    zalloc("shared", 10240)
    assert off <= 32768, off
    c.z = z

    def Z(name):
        lo, hi = z[name]
        return arena3[:, lo:hi]

    c.Z = Z
    sh = z["shared"][0]
    c.xn_tok = [arena3[:, sh:sh + 2048], arena3[:, sh + 2048:sh + 4096]]
    c.xchunk = [arena3[:, sh + 4096:sh + 6144], arena3[:, sh + 6144:sh + 8192]]
    c.sqz = arena3[:, sh + 8192:sh + 10240]
    c.vbuf = arena3[:, sh:sh + 4096]
    c.wv = arena3[:, sh + 4096:sh + 8192]

    kvst = e(nc.sbuf_tensor("kvst", [128, 2048], F32))
    c.kvst = kvst
    cols = e(nc.sbuf_tensor("cols", [128, 208], F32))
    c.cols = cols
    co = {}
    coff = 0

    def calloc(name, n):
        nonlocal coff
        co[name] = (coff, coff + n)
        coff += n

    for nm, n in [("slneg", 16), ("nw", 16), ("bd", 16), ("kdec", 32),
                  ("a2", 16), ("a2sq", 16), ("insc", 16), ("c256", 1),
                  ("s1", 1), ("s2", 2), ("mneg", 1), ("mm", 1), ("tv", 1),
                  ("rc", 1), ("acol", 1), ("bcol", 1), ("mx", 1), ("mn", 1),
                  ("t1", 1), ("t2", 1), ("scol", 1), ("invs", 1), ("c1", 1),
                  ("c2", 1), ("epsln", 1), ("var16", 16),
                  ("s1b", 1), ("s2b", 2), ("mnegb", 1), ("mmb", 1), ("tvb", 1),
                  ("rcb", 1), ("acolb", 1), ("bcolb", 1)]:
        calloc(nm, n)
    assert coff <= 208

    def CO(name):
        lo, hi = co[name]
        return cols[:, lo:lo + 1] if hi - lo == 2 else cols[:, lo:hi]

    c.CO = CO

    consts = e(nc.sbuf_tensor("consts", [128, 288], BF16))
    c.ones_col = consts[:, 0:1]
    c.onesq = consts[:, 32:160]
    c.ident = consts[:, 160:288]
    sigt = e(nc.sbuf_tensor("sigt", [128, 1024], BF16))
    c.sig = [sigt[:, 0:512], sigt[:, 512:1024]]
    biasb = e(nc.sbuf_tensor("biasb", [128, 2048], BF16))
    c.biasb = biasb
    c.qbuf2 = biasb[:, :]
    c.ktbuf2 = c.sqz

    a1f = arena1.bitcast(F32)
    c.iota_m1 = a1f[:, 0:256]
    c.iota_kd = a1f[:, 256:258]
    c.idxT0 = a1f[:, 512:768]
    c.slpos = a1f[:, 768:784]
    c.dt_scr = a1f[:, 1024:1280]

    kvb = kvst.bitcast(BF16)
    c.wg = [kvb[:, 0:2048], kvb[:, 2048:4096]]
    c.xchunk_p4 = kvb[:, 0:2048]
    c.sq_p4_bf = kvb[:, 2048:4096]
    c.outbuf = kvst[:, 0:2048]

    c.qxf = arena1[:, 0:16384]
    c.X32 = a1f[:, 8192:10240]
    c.Xbf = arena1[:, 16384:20480]
    c.qf = [arena1[:, 20480:22528], arena1[:, 22528:24576]]
    c.wsb = a1f[:, 12288:14336]
    c.ffn16 = arena1.bitcast(F16)[:, 28672:30720]
    c.tmp32 = a1f[:, 15360:15872]

    c.pP = [e(nc.psum_tensor("pP0", [128, 512], F32)),
            e(nc.psum_tensor("pP1", [128, 512], F32))]
    c.pBIG = e(nc.psum_tensor("pBIG", [128, 2048], F32))
    c.pS = [c.pBIG[:, 0:512], c.pBIG[:, 512:1024]]
    c.pO = [c.pBIG[:, 1024:1536], c.pBIG[:, 1536:2048]]
    c.pAUX = e(nc.psum_tensor("pAUX", [128, 512], F32))
    c.pAUX2 = e(nc.psum_tensor("pAUX2", [128, 512], F32))
    c.ps_kv = c.pAUX[:, 0:128]
    c.ps_kv2 = c.pAUX2[:, 0:128]
    c.ps_var = c.pAUX2[:, 256:272]

    def fmv(arena, fc):
        return arena[:, fc * T:(fc + 1) * T]

    c.fmv = fmv

    _skip = set(int(x) for x in os.environ.get("SKIP_STAGES", "").split(",") if x)

    def stage(i):
        return stages >= i and i not in _skip

    ph_consts(c)
    em.barrier(c.sems)
    if stage(1):
        ph_ln_transpose(c, dxp, arena1, "tap_xnfpre")
        em.barrier(c.sems)
    if stage(2):
        ph_prefix(c)
        em.barrier(c.sems)
        if "tap_kv" in taps:
            tap_store(c, c.tap_dram["tap_kv"], kvst[:, 0:H * HD])
            em.barrier(c.sems)
    if stage(3):
        ph_ln_transpose(c, dx, arena2, "tap_xnf")
        em.barrier(c.sems)
    if stage(4):
        ph_attn(c)
        em.barrier(c.sems)
        if "tap_outf" in taps:
            tap_store(c, c.tap_dram["tap_outf"], arena1[:, :])
            em.barrier(c.sems)
    if stage(5):
        ph_rms(c)
        em.barrier(c.sems)
    if stage(6):
        ph_gate(c)
        em.barrier(c.sems)
        if "tap_gout" in taps:
            tap_store(c, c.tap_dram["tap_gout"], arena1[:, :])
            em.barrier(c.sems)
    if stage(7):
        ph_outproj(c)
        em.barrier(c.sems)
        if "tap_x2" in taps:
            tap_store(c, c.tap_dram["tap_x2"], arena2[:, :])
            em.barrier(c.sems)
    if stage(8):
        ph_ffn(c)
    em.raw("sync", lambda e2, s=c.s["st"].h, v=c.s["st"].n: e2.wait_ge(s, v))

    with nc.Block() as block:
        @block.tensor
        def _(t):
            for f in em.q["tensor"]:
                f(t)

        @block.vector
        def _(v):
            for f in em.q["vector"]:
                f(v)

        @block.scalar
        def _(s2):
            for f in em.q["scalar"]:
                f(s2)

        @block.gpsimd
        def _(g):
            for f in em.q["gpsimd"]:
                f(g)

        @block.sync
        def _(sy):
            for f in em.q["sync"]:
                f(sy)

    big.close()
    return nc


def tap_store(c, dram, sbuf_ap):
    em, s = c.em, c.s
    n = sbuf_ap.shape[1]
    CH = 4096
    for i in range(0, n, CH):
        w = min(CH, n - i)
        em.op("sync", s["st"],
              lambda e, d=dram[:, i:i + w], src=sbuf_ap[:, i:i + w]: e.dma_start(d, src),
              16)
    em.wait("sync", s["st"])


def ph_consts(c):
    em, s = c.em, c.s
    _lite = int(os.environ.get("CONSTS_LITE", "0"))

    def g(fn):
        return em.op("gpsimd", s["gp"], fn)

    g(lambda e: e.memset(c.ones_col, 1.0))
    g(lambda e: e.memset(c.onesq, 1.0))
    em.wait("gpsimd", s["gp"])
    g(lambda e: e.affine_select(c.ident, c.onesq, [[-1, 128]], ALU.is_equal, 0.0,
                                base=0, channel_multiplier=1))
    g(lambda e: e.iota(c.iota_m1, [[1, 256]], base=1, channel_multiplier=0,
                       allow_small_or_imprecise_dtypes=True))
    g(lambda e: e.iota(c.iota_kd, [[-128, 2]], base=255, channel_multiplier=-1,
                       allow_small_or_imprecise_dtypes=True))
    g(lambda e: e.iota(c.idxT0, [[1, 256]], base=0, channel_multiplier=-1,
                       allow_small_or_imprecise_dtypes=True))
    g(lambda e: e.memset(c.CO("c256"), 256.0))
    g(lambda e: e.memset(c.CO("epsln"), EPS_LN))
    g(lambda e: e.memset(c.kvst[:, :], 0.0))
    g(lambda e: e.memset(c.Z("qkdt1"), 0.0))
    g(lambda e: e.memset(c.Z("qkdt1b"), 0.0))

    if _lite >= 3:
        return
    em.op("sync", s["st"],
          lambda e: e.dma_start(c.slpos, c.dram["sl"][None, :].broadcast_to((128, 16))),
          16)
    em.op("sync", s["st"],
          lambda e: e.dma_start(c.CO("nw"),
                                c.dram["nw"].rearrange("(cc p) -> p cc", p=128),
                                allow_slow_non_contiguous=True), 16)
    em.wait("vector", s["st"])
    em.op("vector", s["vc"],
          lambda e: e.tensor_scalar(c.CO("slneg"), c.slpos, -1.0, None, ALU.mult))

    em.wait("scalar", s["vc"])
    em.wait("scalar", s["gp"])
    em.wait("vector", s["gp"])
    if _lite >= 2:
        return
    for h in range(H):
        sl = c.CO("slneg")[:, h:h + 1]
        em.op("scalar", s["sc"],
              lambda e, h=h, sl=sl: e.activation(
                  c.Z("qdec")[:, h * 256:(h + 1) * 256], c.iota_m1, AF.Exp, scale=sl))
        em.op("scalar", s["sc"],
              lambda e, h=h, sl=sl: e.activation(
                  c.CO("kdec")[:, 2 * h:2 * h + 2], c.iota_kd, AF.Exp, scale=sl))
        em.op("scalar", s["sc"],
              lambda e, h=h, sl=sl: e.activation(
                  c.CO("bd")[:, h:h + 1], c.CO("c256"), AF.Exp, scale=sl))
        em.wait("vector", s["sc"])
        em.wait("vector", s["gp"])
        dm = em.op("vector", s["vc"],
                   lambda e, sl=sl: e.tensor_scalar(c.dt_scr, c.idxT0, sl, None, ALU.mult))
        em.wait("gpsimd", s["vc"], dm)
        af = em.op("gpsimd", s["gp"],
                   lambda e: e.affine_select(c.dt_scr, c.dt_scr, [[1, 256]], ALU.is_ge,
                                             -20000.0, base=0, channel_multiplier=-1))
        em.wait("scalar", s["gp"], af)
        em.op("scalar", s["sc"],
              lambda e, h=h: e.activation(
                  c.Z("dt")[:, h * 256:(h + 1) * 256], c.dt_scr, AF.Exp))


def ph_ln_transpose(c, src_dram, dst_arena, tapname):
    em, s = c.em, c.s
    import os
    _mode = os.environ.get("LN_MODE", "")
    _ser = os.environ.get("LN_SERIAL", "0") == "1"
    stats_done = [None, None]
    tp_done = [None, None]
    last_tv = [None, None]
    dst_v = dst_arena[:, :].rearrange("p (cc t) -> p cc t", cc=NCH)
    for tc in range(NCH):
        bi = tc % 2
        buf = c.xchunk[bi]
        if stats_done[bi] is not None:
            em.wait("gpsimd", s["vc"], stats_done[bi][0])
            em.wait("gpsimd", s["sc"], stats_done[bi][1])
        if _mode == "s3" and stats_done[1 - bi] is not None:
            em.wait("gpsimd", s["vc"], stats_done[1 - bi][0])
            em.wait("gpsimd", s["sc"], stats_done[1 - bi][1])
        sx = s["x0"] if bi == 0 else s["x1"]
        ld = em.op("gpsimd", sx,
                   lambda e, b=buf, d=src_dram, i=tc: e.dma_start(
                       b, d[i * 128:(i + 1) * 128, :]), 16)
        sfx = "b" if bi else ""
        S1, S2, MNEG, MM, TV, RC, ACOL, BCOL = (c.CO("s1" + sfx), c.CO("s2" + sfx),
                                                c.CO("mneg" + sfx), c.CO("mm" + sfx),
                                                c.CO("tv" + sfx), c.CO("rc" + sfx),
                                                c.CO("acol" + sfx), c.CO("bcol" + sfx))
        em.wait("scalar", sx, ld)
        if last_tv[bi] is not None:
            em.wait("scalar", s["vc"], last_tv[bi])
        sq = em.op("scalar", s["sc"],
                   lambda e, b=buf, S2=S2: e.activation(c.sqz, b, AF.Square,
                                                        accum_out=S2))
        em.wait("vector", sx, ld)
        em.opw("vector", s["vc"],
               lambda e, b=buf, S1=S1: e.reduce_sum(S1, b, axis=AX.X))
        em.wait("vector", s["sc"], sq)

        def vv(fn):
            return em.opw("vector", s["vc"], fn)

        vv(lambda e, S1=S1, MNEG=MNEG: e.tensor_scalar(MNEG, S1, -1.0 / D, None, ALU.mult))
        vv(lambda e, MNEG=MNEG, MM=MM: e.tensor_tensor(MM, MNEG, MNEG, ALU.mult))
        vv(lambda e, S2=S2, TV=TV: e.tensor_scalar(TV, S2, 1.0 / D, EPS_LN,
                                                   ALU.mult, ALU.add))
        last_tv[bi] = vv(lambda e, TV=TV, MM=MM: e.tensor_tensor(TV, TV, MM,
                                                                 ALU.subtract))
        rc = vv(lambda e, TV=TV, RC=RC: e.reciprocal(RC, TV))
        em.wait("scalar", s["vc"], rc)
        sq2 = em.op("scalar", s["sc"],
                    lambda e, RC=RC, ACOL=ACOL: e.activation(ACOL, RC, AF.Sqrt))
        em.wait("vector", s["sc"], sq2)
        bc = vv(lambda e, MNEG=MNEG, ACOL=ACOL, BCOL=BCOL: e.tensor_tensor(
            BCOL, MNEG, ACOL, ALU.mult))
        if _mode == "s1":
            em.barrier(c.sems)
        xb = c.xn_tok[bi]
        stp = s["t0"] if bi == 0 else s["t1"]
        if tp_done[bi] is not None:
            em.wait("vector", stp, tp_done[bi])
        ap = vv(lambda e, b=buf, o=xb, ACOL=ACOL, BCOL=BCOL: e.tensor_scalar(
            o, b, ACOL, BCOL, ALU.mult, ALU.add))
        stats_done[bi] = (ap, sq)
        if "tap_cols" in c.taps:
            em.wait("sync", s["vc"], ap)
            for ci, col in enumerate([S1, S2, MNEG, TV, RC, ACOL, BCOL]):
                em.op("sync", s["st"],
                      lambda e, tc=tc, ci=ci, col=col: e.dma_start(
                          c.tap_dram["tap_cols"][:, tc * 8 + ci:tc * 8 + ci + 1], col,
                          allow_slow_non_contiguous=True), 16)
        em.wait("sync", s["vc"], ap)
        tp_done[bi] = em.op("sync", stp,
                            lambda e, o=dst_v[:, :, tc * 128:(tc + 1) * 128], i=xb:
                            e.dma_start(o, i, transpose=True), 16)
    if tapname in c.taps and _mode != "notp":
        em.barrier(c.sems)
        tap_store(c, c.tap_dram[tapname], dst_arena[:, :])


def _load_w_slice(c, dram, col0, ncols, dst, sem, war_tn=None):
    em, s = c.em, c.s
    if war_tn is not None:
        em.wait("gpsimd", s["tn"], war_tn)
    src = dram[:, col0:col0 + ncols].rearrange("(cc p) j -> p cc j", p=128)
    dstv = dst.rearrange("p (cc j) -> p cc j", cc=NCH)
    return em.op("gpsimd", sem, lambda e: e.dma_start(dstv, src), 16)


def _vproj(c, xnF_arena, g, st):
    em, s = c.em, c.s
    wv_ld = _load_w_slice(c, c.dram["wv"], g * 256, 256, c.wv, s["wv"], st.get("wv_war"))
    wvv = c.wv.rearrange("p (cc j) -> p cc j", cc=NCH)
    em.wait("scalar", s["tn"])
    for tt in range(NCH):
        ps = c.pP[tt % 2]
        em.wait("tensor", s["wv"], wv_ld)
        pe = st["pP_evac"][tt % 2]
        if pe is not None:
            em.wait("tensor", s["sc"], pe)
        for fc in range(NCH):
            fn = (lambda e, ps=ps, fc=fc, tt=tt: e.matmul(
                ps[:, 0:256], c.fmv(xnF_arena, fc)[:, tt * 128:(tt + 1) * 128],
                wvv[:, fc, :], start=(fc == 0), stop=(fc == NCH - 1)))
            if fc < NCH - 1:
                em.qop("tensor", fn)
            else:
                em.op("tensor", s["tn"], fn)
        em.wait("scalar", s["tn"])
        st["pP_evac"][tt % 2] = em.op("scalar", s["sc"],
                                      lambda e, ps=ps, tt=tt: e.activation(
                                          c.vbuf[:, tt * 256:(tt + 1) * 256],
                                          ps[:, 0:256], AF.Silu))
    st["wv_war"] = c.s["tn"].n


def _fproj(c, xnF_arena, wdram, col0, dstbuf, wbuf, st, wkey, wsem):
    em, s = c.em, c.s
    wld = _load_w_slice(c, wdram, col0, 128, wbuf, wsem, st.get(wkey))
    wv = wbuf.rearrange("p (cc j) -> p cc j", cc=NCH)
    em.wait("scalar", s["tn"])
    em.wait("scalar", s["tp"])
    for nt in range(4):
        ps = c.pP[nt % 2]
        em.wait("tensor", wsem, wld)
        pe = st["pP_evac"][nt % 2]
        if pe is not None:
            em.wait("tensor", s["sc"], pe)
        for fc in range(NCH):
            fn = (lambda e, ps=ps, fc=fc, nt=nt, w=wv: e.matmul(
                ps[:, :], w[:, fc, :],
                c.fmv(xnF_arena, fc)[:, nt * 512:(nt + 1) * 512],
                start=(fc == 0), stop=(fc == NCH - 1)))
            if fc < NCH - 1:
                em.qop("tensor", fn)
            else:
                em.op("tensor", s["tn"], fn)
        em.wait("scalar", s["tn"])
        st["pP_evac"][nt % 2] = em.op("scalar", s["sc"],
                                      lambda e, ps=ps, nt=nt, d=dstbuf: e.activation(
                                          d[:, nt * 512:(nt + 1) * 512], ps[:, :],
                                          AF.Silu))
    st[wkey] = c.s["tn"].n


def _ktranspose(c, h, kb, ktb, tsem):
    em, s = c.em, c.s
    ktv = ktb.rearrange("p (cc t) -> p cc t", cc=NCH)
    em.wait("sync", s["sc"])
    em.wait("sync", s["tn"])
    em.wait("sync", s["vc"])
    tp = em.op("sync", tsem,
               lambda e, ktv=ktv, kb=kb: e.dma_start(ktv, kb, transpose=True), 16)
    em.wait("vector", tsem, tp)
    last = None
    for tile in range(NCH):
        u = tile % 2
        last = em.op("vector", s["vc"],
                     lambda e, t=tile, u=u, h=h, ktb=ktb: e.tensor_scalar(
                         ktb[:, t * 128:(t + 1) * 128],
                         ktb[:, t * 128:(t + 1) * 128],
                         c.CO("kdec")[:, 2 * h + u:2 * h + u + 1], None, ALU.mult))
    return last


def _kv_chain(c, h, hg, ktiles_ready):
    em, s = c.em, c.s
    kvs = c.kvst[:, h * 128:(h + 1) * 128]
    stt_done = None
    em.wait("tensor", s["sc"])
    for j in range(NBLK):
        em.wait("tensor", s["vc"], stt_done if stt_done is not None else ktiles_ready)
        for u in range(2):
            tile = 2 * j + u
            fn = (lambda e, t=tile, u=u, hg=hg: e.matmul(
                c.ps_kv, c.Z("ktbuf")[:, t * 128:(t + 1) * 128],
                c.vbuf[:, t * 256 + hg * 128:t * 256 + (hg + 1) * 128],
                start=(u == 0), stop=(u == 1)))
            if u == 0:
                em.qop("tensor", fn)
            else:
                em.op("tensor", s["tn"], fn)
        em.wait("vector", s["tn"])
        stt_done = em.opw("vector", s["vc"],
                          lambda e, kvs=kvs, h=h: e.scalar_tensor_tensor(
                              kvs, kvs, c.CO("bd")[:, h:h + 1], c.ps_kv,
                              ALU.mult, ALU.add))


def ph_prefix(c):
    em, s = c.em, c.s
    st = {"pP_evac": [None, None]}
    for g in range(H // 2):
        _vproj(c, c.arena1, g, st)
        for hg in range(2):
            h = 2 * g + hg
            _fproj(c, c.arena1, c.dram["wk"], h * 128, c.Z("kbuf"), c.Z("wk"), st, "wk_war", s["wk"])
            kr = _ktranspose(c, h, c.Z("kbuf"), c.Z("ktbuf"), s["tp"])
            _kv_chain(c, h, hg, kr)


def ph_attn(c):
    em, s = c.em, c.s
    st = {"pP_evac": [None, None]}
    for g in range(int(os.environ.get("ATTN_GROUPS", str(H // 2)))):
        h0, h1 = 2 * g, 2 * g + 1
        _vproj(c, c.arena2, g, st)
        _fproj(c, c.arena2, c.dram["wq"], h0 * 128, c.Z("qbuf"), c.Z("wq"), st,
               "wq_war", s["wq"])
        _fproj(c, c.arena2, c.dram["wk"], h0 * 128, c.Z("kbuf"), c.Z("wk"), st,
               "wk_war", s["wk"])
        kr0 = _ktranspose(c, h0, c.Z("kbuf"), c.Z("ktbuf"), s["tp"])
        _fproj(c, c.arena2, c.dram["wq"], h1 * 128, c.qbuf2, c.Z("wq"), st,
               "wq_war", s["wq"])
        _fproj(c, c.arena2, c.dram["wk"], h1 * 128, c.Z("kbuf2"), c.Z("wk"), st,
               "wk_war", s["wk"])
        kr1 = _ktranspose(c, h1, c.Z("kbuf2"), c.ktbuf2, s["tp2"])
        if "tap_q0" in c.taps and g == 0:
            em.barrier(c.sems)
            tap_store(c, c.tap_dram["tap_q0"], c.Z("qbuf"))
            tap_store(c, c.tap_dram["tap_k0"], c.Z("kbuf"))
            tap_store(c, c.tap_dram["tap_v0"], c.vbuf)
        _scan_pair(c, h0, h1, kr0, kr1)


def _scan_pair(c, h0, h1, kr0, kr1):
    em, s = c.em, c.s
    hh_state = []
    for hh, h in enumerate((h0, h1)):
        hh_state.append(dict(
            h=h,
            qb=c.Z("qbuf") if hh == 0 else c.qbuf2,
            kb=c.Z("kbuf") if hh == 0 else c.Z("kbuf2"),
            ktb=c.Z("ktbuf") if hh == 0 else c.ktbuf2,
            q0t=c.Z("qkdt0") if hh == 0 else c.Z("qkdt0b"),
            q1t=c.Z("qkdt1") if hh == 0 else c.Z("qkdt1b"),
            kvbf=c.Z("kv_bf") if hh == 0 else c.Z("kv_bf2"),
            kvs=c.kvst[:, h * 128:(h + 1) * 128],
            qdec=c.Z("qdec")[:, h * 256:(h + 1) * 256],
            dt=c.Z("dt")[:, h * 256:(h + 1) * 256],
            pS=c.pS[hh], pO=c.pO[hh],
            pKV=c.pAUX[:, 0:128] if hh == 0 else c.pAUX2[:, 0:128],
            stt=None, mask=None, evac=None, inter_mm=None, intra_mm=None,
            ktr=kr0 if hh == 0 else kr1,
        ))
    em.wait("tensor", s["sc"])
    em.wait("tensor", s["vc"])
    for j in range(NBLK):
        for hh in range(2):
            S = hh_state[hh]
            if S["mask"] is not None:
                em.wait("tensor", s["vc"], S["mask"])
            for u in range(2):
                fn = (lambda e, j=j, u=u, S=S: e.matmul(
                    S["pS"][:, u * 256:(u + 1) * 256],
                    S["kb"][:, j * 256 + u * 128:j * 256 + (u + 1) * 128],
                    S["qb"][:, j * 256:(j + 1) * 256], start=True, stop=True))
                if u == 0:
                    em.qop("tensor", fn)
                else:
                    em.op("tensor", s["tn"], fn)
            S["sc_mm"] = c.s["tn"].n
        for hh in range(2):
            S = hh_state[hh]
            em.wait("vector", s["tn"], S["sc_mm"])
            if S["intra_mm"] is not None:
                em.wait("vector", s["tn"], S["intra_mm"])
            em.op("vector", s["vc"],
                  lambda e, S=S: e.tensor_tensor(S["q0t"], S["pS"][:, 0:256],
                                                 S["dt"], ALU.mult))
            S["mask"] = em.op("vector", s["vc"],
                              lambda e, S=S: e.tensor_tensor(
                                  S["q1t"][:, 128:256], S["pS"][:, 384:512],
                                  S["dt"][:, 0:128], ALU.mult))
            S["qt"] = em.op("vector", s["vc"],
                            lambda e, j=j, S=S: e.tensor_tensor(
                                S["qb"][:, j * 256:(j + 1) * 256],
                                S["qb"][:, j * 256:(j + 1) * 256], S["qdec"],
                                ALU.mult))
            em.wait("scalar", s["vc"], S["stt"] if S["stt"] is not None else S["ktr"])
            if S["inter_mm"] is not None:
                em.wait("scalar", s["tn"], S["inter_mm"])
            S["kc"] = em.op("scalar", s["sc"],
                            lambda e, S=S: e.copy(S["kvbf"], S["kvs"]))
        for hh in range(2):
            S = hh_state[hh]
            em.wait("tensor", s["sc"], S["kc"])
            em.wait("tensor", s["vc"], S["qt"])
            if S["evac"] is not None:
                em.wait("tensor", s["sc"], S["evac"])
            em.qop("tensor",
                   lambda e, j=j, S=S: e.matmul(
                       S["pO"][:, 0:256], S["kvbf"],
                       S["qb"][:, j * 256:(j + 1) * 256], start=True, stop=False))
            S["inter_mm"] = None
            for u in range(2):
                qk = S["q0t"] if u == 0 else S["q1t"]
                fn = (lambda e, j=j, u=u, S=S, qk=qk: e.matmul(
                    S["pO"][:, 0:256],
                    c.vbuf[:, (2 * j + u) * 256 + (S["h"] % 2) * 128:
                           (2 * j + u) * 256 + (S["h"] % 2 + 1) * 128],
                    qk, start=False, stop=(u == 1)))
                if u == 0:
                    em.qop("tensor", fn)
                else:
                    em.op("tensor", s["tn"], fn)
            S["inter_mm"] = S["intra_mm"] = c.s["tn"].n
            em.wait("tensor", s["vc"], S["stt"] if S["stt"] is not None else S["ktr"])
            for u in range(2):
                tile = 2 * j + u
                fn = (lambda e, t=tile, u=u, S=S: e.matmul(
                    S["pKV"], S["ktb"][:, t * 128:(t + 1) * 128],
                    c.vbuf[:, t * 256 + (S["h"] % 2) * 128:
                           t * 256 + (S["h"] % 2 + 1) * 128],
                    start=(u == 0), stop=(u == 1)))
                if u == 0:
                    em.qop("tensor", fn)
                else:
                    em.op("tensor", s["tn"], fn)
            S["kv_mm"] = c.s["tn"].n
        for hh in range(2):
            S = hh_state[hh]
            em.wait("scalar", s["tn"], S["intra_mm"])
            S["evac"] = em.op("scalar", s["sc"],
                              lambda e, j=j, S=S: e.copy(
                                  c.arena1[:, S["h"] * 2048 + j * 256:
                                           S["h"] * 2048 + (j + 1) * 256],
                                  S["pO"][:, 0:256]))
            em.wait("vector", s["tn"], S["kv_mm"])
            S["stt"] = em.opw("vector", s["vc"],
                              lambda e, S=S: e.scalar_tensor_tensor(
                                  S["kvs"], S["kvs"], c.CO("bd")[:, S["h"]:S["h"] + 1],
                                  S["pKV"], ALU.mult, ALU.add))


def ph_rms(c):
    em, s = c.em, c.s
    acc_done = None
    for oc in range(NCH):
        em.wait("vector", s["tn"])
        sq = em.op("vector", s["vc"],
                   lambda e, oc=oc: e.tensor_tensor(
                       c.sqz, c.arena1[:, oc * T:(oc + 1) * T],
                       c.arena1[:, oc * T:(oc + 1) * T], ALU.mult))
        em.wait("tensor", s["vc"], sq)
        if acc_done is not None:
            em.wait("tensor", s["vc"], acc_done)
        for tc in range(NCH):
            fn = (lambda e, tc=tc: e.matmul(
                c.ps_var[:, tc:tc + 1], c.sqz[:, tc * 128:(tc + 1) * 128],
                c.ones_col, start=True, stop=True))
            if tc < NCH - 1:
                em.qop("tensor", fn)
            else:
                em.op("tensor", s["tn"], fn)
        em.wait("vector", s["tn"])
        if oc == 0:
            acc_done = em.opw("vector", s["vc"],
                              lambda e: e.tensor_copy(c.CO("var16"), c.ps_var))
        else:
            acc_done = em.opw("vector", s["vc"],
                              lambda e: e.tensor_tensor(c.CO("var16"), c.CO("var16"),
                                                        c.ps_var, ALU.add))

    def vv(fn):
        return em.opw("vector", s["vc"], fn)

    vv(lambda e: e.tensor_scalar(c.CO("var16"), c.CO("var16"), 1.0 / D, EPS_RMS,
                                 ALU.mult, ALU.add))
    rc = vv(lambda e: e.reciprocal(c.CO("var16"), c.CO("var16")))
    em.wait("scalar", s["vc"], rc)
    sq = em.op("scalar", s["sc"],
               lambda e: e.activation(c.CO("a2"), c.CO("var16"), AF.Sqrt))
    em.wait("vector", s["sc"], sq)
    vv(lambda e: e.tensor_tensor(c.CO("a2sq"), c.CO("a2"), c.CO("a2"), ALU.mult))


def ph_gate(c):
    em, s = c.em, c.s
    for oc in range(NCH):
        em.op("gpsimd", s["wo"],
              lambda e, oc=oc: e.dma_start(
                  c.arena3[:, oc * T:(oc + 1) * T],
                  c.dram["wo"][oc * 128:(oc + 1) * 128, :]), 16)
    em.wait("vector", s["wo"])
    for oc in range(NCH):
        em.op("vector", s["vc"],
              lambda e, oc=oc: e.tensor_scalar(
                  c.arena3[:, oc * T:(oc + 1) * T], c.arena3[:, oc * T:(oc + 1) * T],
                  c.CO("nw")[:, oc:oc + 1], None, ALU.mult))
    prev_mm = [None, None]
    for gc in range(NCH):
        wb = c.wg[gc % 2]
        swg = s["wg0"] if gc % 2 == 0 else s["wg1"]
        if prev_mm[gc % 2] is not None:
            em.wait("gpsimd", s["tn"], prev_mm[gc % 2])
        ld = em.op("gpsimd", swg,
                   lambda e, gc=gc, wb=wb: e.dma_start(
                       wb.rearrange("p (cc j) -> p cc j", cc=NCH),
                       c.dram["wg"][:, gc * 128:(gc + 1) * 128].rearrange(
                           "(cc p) j -> p cc j", p=128)), 16)
        wv = wb.rearrange("p (cc j) -> p cc j", cc=NCH)
        for nt in range(4):
            ps = c.pP[nt % 2]
            em.wait("tensor", swg, ld)
            em.wait("tensor", s["sc"])
            for fc in range(NCH):
                fn = (lambda e, ps=ps, fc=fc, nt=nt, w=wv: e.matmul(
                    ps[:, :], w[:, fc, :],
                    c.fmv(c.arena2, fc)[:, nt * 512:(nt + 1) * 512],
                    start=(fc == 0), stop=(fc == NCH - 1)))
                if fc < NCH - 1:
                    em.qop("tensor", fn)
                else:
                    em.op("tensor", s["tn"], fn)
            em.wait("scalar", s["tn"])
            em.wait("scalar", s["vc"])
            sg = em.op("scalar", s["sc"],
                       lambda e, ps=ps, nt=nt: e.activation(
                           c.sig[nt % 2], ps[:, :], AF.Sigmoid))
            em.wait("vector", s["sc"], sg)
            em.op("vector", s["vc"],
                  lambda e, gc=gc, nt=nt: e.tensor_tensor(
                      c.arena1[:, gc * T + nt * 512:gc * T + (nt + 1) * 512],
                      c.arena1[:, gc * T + nt * 512:gc * T + (nt + 1) * 512],
                      c.sig[nt % 2], ALU.mult))
        prev_mm[gc % 2] = c.s["tn"].n


def ph_outproj(c):
    import os
    _lvl = int(os.environ.get("P4_LEVEL", "3"))
    em, s = c.em, c.s
    evac_done = None
    _ntc = int(os.environ.get("P4_NTC", str(NCH)))
    for tc in range(_ntc):
        if evac_done is not None:
            em.wait("gpsimd", s["vc"], evac_done)
        if _lvl >= 1:
            ld = em.op("gpsimd", s["xp4"],
                       lambda e, tc=tc: e.dma_start(
                           c.xchunk_p4, c.dram["x"][tc * 128:(tc + 1) * 128, :]), 16)
        if evac_done is not None:
            em.wait("tensor", s["vc"], evac_done)
            em.wait("tensor", s["sc"])
        for ft in range(4):
            for kc in range(NCH):
                fn = (lambda e, tc=tc, ft=ft, kc=kc: e.matmul(
                    c.pBIG[:, ft * 512:(ft + 1) * 512],
                    c.arena1[:, kc * T + tc * 128:kc * T + (tc + 1) * 128],
                    c.arena3[:, kc * T + ft * 512:kc * T + (ft + 1) * 512],
                    start=(kc == 0), stop=(kc == NCH - 1)))
                if ft == 3 and kc == NCH - 1:
                    em.op("tensor", s["tn"], fn)
                else:
                    em.qop("tensor", fn)
        mm = c.s["tn"].n
        em.wait("scalar", s["tn"], mm)
        cp1 = em.op("scalar", s["sc"],
                    lambda e: e.activation(c.sq_p4_bf, c.pBIG[:, :], AF.Copy,
                                           accum_out=c.CO("s1")))
        em.wait("scalar", s["sc"], cp1)
        sq = em.op("scalar", s["sc"],
                   lambda e: e.activation(c.sq_p4_bf, c.pBIG[:, :], AF.Square,
                                          accum_out=c.CO("s2")))
        em.wait("vector", s["sc"], sq)

        def vv(fn):
            return em.opw("vector", s["vc"], fn)

        vv(lambda e: e.tensor_scalar(c.CO("mneg"), c.CO("s1"), -1.0 / D, None, ALU.mult))
        vv(lambda e: e.tensor_tensor(c.CO("mm"), c.CO("mneg"), c.CO("mneg"), ALU.mult))
        vv(lambda e: e.tensor_scalar(c.CO("tv"), c.CO("s2"), 1.0 / D, None, ALU.mult))
        vv(lambda e: e.tensor_tensor(c.CO("tv"), c.CO("tv"), c.CO("mm"), ALU.subtract))
        vv(lambda e, tc=tc: e.scalar_tensor_tensor(
            c.CO("tv"), c.CO("tv"), c.CO("a2sq")[:, tc:tc + 1], c.CO("epsln"),
            ALU.mult, ALU.add))
        rc = vv(lambda e: e.reciprocal(c.CO("rc"), c.CO("tv")))
        em.wait("scalar", s["vc"], rc)
        sq2 = em.op("scalar", s["sc"],
                    lambda e: e.activation(c.CO("acol"), c.CO("rc"), AF.Sqrt))
        em.wait("vector", s["sc"], sq2)
        vv(lambda e, tc=tc: e.tensor_tensor(
            c.CO("c1"), c.CO("a2")[:, tc:tc + 1], c.CO("acol"), ALU.mult))
        vv(lambda e: e.tensor_tensor(c.CO("bcol"), c.CO("mneg"), c.CO("c1"),
                                     ALU.mult))
        if _lvl >= 1:
            em.wait("vector", s["xp4"], ld)
        vv(lambda e: e.tensor_scalar(c.xchunk_p4, c.xchunk_p4, c.CO("bcol"), None,
                                     ALU.add))
        evac_done = vv(lambda e, tc=tc: e.scalar_tensor_tensor(
            c.arena2[:, tc * 2048:(tc + 1) * 2048], c.pBIG[:, :], c.CO("c1"),
            c.xchunk_p4, ALU.mult, ALU.add))


def ph_ffn(c):
    em, s = c.em, c.s
    for kc in range(NCH):
        em.op("gpsimd", s["wf"],
              lambda e, kc=kc: e.dma_start(
                  c.arena3[:, kc * T:(kc + 1) * T],
                  c.dram["wf"][kc * 128:(kc + 1) * 128, :]), 16)
    wf_ld = c.s["wf"].n
    em.op("gpsimd", s["cst"],
          lambda e: e.dma_start(c.wsb, c.dram["ws"][None, :].broadcast_to((128, D))), 16)
    em.op("gpsimd", s["cst"],
          lambda e: e.dma_start(c.biasb[:, :],
                                c.dram["fb"][None, :].broadcast_to((128, D))), 16)

    qxv = c.qxf.rearrange("p (cc t) -> p cc t", cc=NCH)
    tp_done = [None, None]
    store_done = None
    prev_ep = None
    for hf in range(2):
        if hf == 1:
            em.wait("sync", s["tn"])
        for ti in range(8):
            tc = hf * 8 + ti
            x2c = c.arena2[:, tc * 2048:(tc + 1) * 2048]
            em.wait("scalar", s["vc"])
            sq = em.op("scalar", s["sc"],
                       lambda e, x2c=x2c: e.activation(c.Xbf[:, 0:2048], x2c, AF.Square,
                                                       accum_out=c.CO("s2")))
            em.opw("vector", s["vc"],
                   lambda e, x2c=x2c: e.reduce_sum(c.CO("s1"), x2c, axis=AX.X))
            em.opw("vector", s["vc"],
                   lambda e, x2c=x2c: e.tensor_reduce(c.CO("mx"), x2c, AX.X, ALU.max))
            em.opw("vector", s["vc"],
                   lambda e, x2c=x2c: e.tensor_reduce(c.CO("mn"), x2c, AX.X, ALU.min))
            em.wait("vector", s["sc"], sq)

            def vv(fn):
                return em.opw("vector", s["vc"], fn)

            vv(lambda e: e.tensor_scalar(c.CO("mneg"), c.CO("s1"), -1.0 / D, None,
                                         ALU.mult))
            vv(lambda e: e.tensor_tensor(c.CO("mm"), c.CO("mneg"), c.CO("mneg"),
                                         ALU.mult))
            vv(lambda e: e.tensor_scalar(c.CO("tv"), c.CO("s2"), 1.0 / D, EPS_LN,
                                         ALU.mult, ALU.add))
            vv(lambda e: e.tensor_tensor(c.CO("tv"), c.CO("tv"), c.CO("mm"),
                                         ALU.subtract))
            rc = vv(lambda e: e.reciprocal(c.CO("rc"), c.CO("tv")))
            em.wait("scalar", s["vc"], rc)
            sq2 = em.op("scalar", s["sc"],
                        lambda e: e.activation(c.CO("acol"), c.CO("rc"), AF.Sqrt))
            em.wait("vector", s["sc"], sq2)
            vv(lambda e: e.tensor_scalar(c.CO("t1"), c.CO("mx"), c.CO("mneg"), None,
                                         ALU.add))
            vv(lambda e: e.tensor_scalar(c.CO("t2"), c.CO("mn"), c.CO("mneg"), -1.0,
                                         ALU.add, ALU.mult))
            vv(lambda e: e.tensor_tensor(c.CO("t1"), c.CO("t1"), c.CO("t2"), ALU.max))
            vv(lambda e: e.tensor_tensor(c.CO("t1"), c.CO("t1"), c.CO("acol"), ALU.mult))
            vv(lambda e: e.tensor_scalar(c.CO("scol"), c.CO("t1"), EPS_Q, 1.0 / 127.0,
                                         ALU.max, ALU.mult))
            vv(lambda e: e.reciprocal(c.CO("invs"), c.CO("scol")))
            vv(lambda e: e.tensor_tensor(c.CO("c1"), c.CO("acol"), c.CO("invs"),
                                         ALU.mult))
            vv(lambda e: e.tensor_tensor(c.CO("c2"), c.CO("mneg"), c.CO("c1"),
                                         ALU.mult))
            vv(lambda e, tc=tc: e.tensor_copy(c.CO("insc")[:, tc:tc + 1], c.CO("scol")))
            bi = ti % 2
            vv(lambda e, x2c=x2c: e.tensor_scalar(c.X32, x2c, c.CO("c1"), c.CO("c2"),
                                                  ALU.mult, ALU.add))
            vv(lambda e: e.tensor_scalar(c.X32, c.X32, MAGIC, MAGIC, ALU.add,
                                         ALU.subtract))
            stq = s["t0"] if bi == 0 else s["t1"]
            if tp_done[bi] is not None:
                em.wait("vector", stq, tp_done[bi])
            qd = vv(lambda e, bi=bi: e.tensor_scalar(c.qf[bi], c.X32, 127.0, -128.0,
                                                     ALU.min, ALU.max))
            em.wait("sync", s["vc"], qd)
            tp_done[bi] = em.op("sync", stq,
                                lambda e, bi=bi, ti=ti: e.dma_start(
                                    qxv[:, :, ti * 128:(ti + 1) * 128], c.qf[bi],
                                    transpose=True), 16)
        if "tap_qxf" in c.taps and hf == 0:
            em.barrier(c.sems)
            tap_store(c, c.tap_dram["tap_qxf"], c.qxf)
        if "tap_insc" in c.taps and hf == 1:
            em.barrier(c.sems)
            tap_store(c, c.tap_dram["tap_insc"], c.CO("insc"))
        em.wait("tensor", s["wf"], wf_ld)
        em.wait("tensor", s["t0"])
        em.wait("tensor", s["t1"])
        em.wait("vector", s["cst"])
        for ti in range(8):
            tc = hf * 8 + ti
            if prev_ep is not None:
                em.wait("tensor", s["vc"], prev_ep)
            for ft in range(4):
                for kc in range(NCH):
                    fn = (lambda e, ti=ti, ft=ft, kc=kc: e.matmul(
                        c.pBIG[:, ft * 512:(ft + 1) * 512],
                        c.qxf[:, kc * 1024 + ti * 128:kc * 1024 + (ti + 1) * 128],
                        c.arena3[:, kc * T + ft * 512:kc * T + (ft + 1) * 512],
                        start=(kc == 0), stop=(kc == NCH - 1)))
                    if ft == 3 and kc == NCH - 1:
                        em.op("tensor", s["tn"], fn)
                    else:
                        em.qop("tensor", fn)
            mm = c.s["tn"].n
            em.wait("vector", s["tn"], mm)
            for ft in range(4):
                em.opw("vector", s["vc"],
                       lambda e, ft=ft, tc=tc: e.scalar_tensor_tensor(
                           c.tmp32, c.pBIG[:, ft * 512:(ft + 1) * 512],
                           c.CO("insc")[:, tc:tc + 1],
                           c.wsb[:, ft * 512:(ft + 1) * 512], ALU.mult, ALU.mult))
                em.opw("vector", s["vc"],
                       lambda e, ft=ft: e.tensor_tensor(
                           c.ffn16[:, ft * 512:(ft + 1) * 512], c.tmp32,
                           c.biasb[:, ft * 512:(ft + 1) * 512], ALU.add))
            prev_ep = c.s["vc"].n
            em.wait("scalar", s["vc"], prev_ep)
            if store_done is not None:
                em.wait("scalar", s["st"], store_done)
                em.wait("vector", s["st"], store_done)
            sq = em.op("scalar", s["sc"],
                       lambda e: e.activation(c.outbuf, c.ffn16, AF.Square,
                                              accum_out=c.CO("s2")))
            em.opw("vector", s["vc"],
                   lambda e: e.reduce_sum(c.CO("s1"), c.ffn16, axis=AX.X))

            def vv(fn):
                return em.opw("vector", s["vc"], fn)

            vv(lambda e: e.tensor_scalar(c.CO("mneg"), c.CO("s1"), -1.0 / D, None,
                                         ALU.mult))
            vv(lambda e: e.tensor_tensor(c.CO("mm"), c.CO("mneg"), c.CO("mneg"),
                                         ALU.mult))
            em.wait("vector", s["sc"], sq)
            vv(lambda e: e.tensor_scalar(c.CO("tv"), c.CO("s2"), 1.0 / D, EPS_LN,
                                         ALU.mult, ALU.add))
            vv(lambda e: e.tensor_tensor(c.CO("tv"), c.CO("tv"), c.CO("mm"),
                                         ALU.subtract))
            rc = vv(lambda e: e.reciprocal(c.CO("rc"), c.CO("tv")))
            em.wait("scalar", s["vc"], rc)
            sq2 = em.op("scalar", s["sc"],
                        lambda e: e.activation(c.CO("acol"), c.CO("rc"), AF.Sqrt))
            em.wait("vector", s["sc"], sq2)
            vv(lambda e: e.tensor_tensor(c.CO("bcol"), c.CO("mneg"),
                                         c.CO("acol"), ALU.mult))
            vv(lambda e: e.tensor_scalar(c.outbuf, c.ffn16, c.CO("acol"), c.CO("bcol"),
                                         ALU.mult, ALU.add))
            od = vv(lambda e, tc=tc: e.tensor_tensor(
                c.outbuf, c.outbuf, c.arena2[:, tc * 2048:(tc + 1) * 2048], ALU.add))
            em.wait("sync", s["vc"], od)
            store_done = em.op("sync", s["st"],
                               lambda e, tc=tc: e.dma_start(
                                   c.dram["out"][tc * 128:(tc + 1) * 128, :],
                                   c.outbuf), 16)


def _host_prep(inputs):
    x = np.asarray(inputs["L0_attn_hidden_states"], dtype=np.float32)
    slopes = np.ascontiguousarray(
        np.asarray(inputs["L0_attn_slope_rate"], dtype=np.float32).reshape(H))
    wqkv = np.asarray(inputs["L0_attn_w_qkv"], dtype=np.float32)
    r = wqkv.reshape(H, 3, HD, D)
    wqT = np.ascontiguousarray(r[:, 0].transpose(2, 0, 1).reshape(D, D))
    wkT = np.ascontiguousarray(r[:, 1].transpose(2, 0, 1).reshape(D, D))
    wvT = np.ascontiguousarray(r[:, 2].transpose(2, 0, 1).reshape(D, D))
    wgT = np.ascontiguousarray(np.asarray(inputs["L0_attn_w_gate"], np.float32).T)
    woT = np.ascontiguousarray(np.asarray(inputs["L0_attn_w_out"], np.float32).T)
    wfT = np.ascontiguousarray(np.asarray(inputs["L0_ffn_weight"]).T)
    normw = np.ascontiguousarray(np.asarray(inputs["L0_attn_norm_w"], np.float32))
    wscale = np.ascontiguousarray(
        np.asarray(inputs["L0_ffn_w_scale"], np.float32).reshape(D))
    fbias = np.ascontiguousarray(np.asarray(inputs["L0_ffn_bias"], np.float32))
    zeros = np.zeros((T, D), np.float32)
    in_maps = []
    for core in range(8):
        b, hf = core // 2, core % 2
        x_own = np.ascontiguousarray(x[b, hf * T:(hf + 1) * T])
        x_pre = np.ascontiguousarray(x[b, 0:T]) if hf == 1 else zeros
        in_maps.append(dict(x_own=x_own, x_pre=x_pre, wqT=wqT, wkT=wkT, wvT=wvT,
                            wgT=wgT, woT=woT, wfT=wfT, slopes=slopes, normw=normw,
                            wscale=wscale, fbias=fbias))
    return x, in_maps


def kernel(**inputs):
    x, in_maps = _host_prep(inputs)
    nc = build_nc()
    res = run_bass_kernel_spmd(nc, in_maps, core_ids=list(range(8)))
    out = np.empty((4, 4096, D), np.float32)
    for core in range(8):
        b, hf = core // 2, core % 2
        out[b, hf * T:(hf + 1) * T, :] = res.results[core]["out"]
    return out


# version 04
# speedup vs baseline: 1.0746x; 1.0049x over previous
import os
import numpy as np
from contextlib import ExitStack

import concourse.bass as bass
import concourse.mybir as mybir
from concourse.bass_utils import run_bass_kernel_spmd

F32 = mybir.dt.float32
BF16 = mybir.dt.bfloat16
F16 = mybir.dt.float16
I8 = mybir.dt.int8
AF = mybir.ActivationFunctionType
ALU = mybir.AluOpType
AX = mybir.AxisListType

T = 2048
D = 2048
H = 16
HD = 128
NCH = 16
NBLK = 8
MAGIC = 12582912.0
EPS_LN = 1e-5
EPS_RMS = 1e-6
EPS_Q = 1e-7


class Sem:
    def __init__(self, ctx, nc, name):
        self.h = ctx.enter_context(nc.semaphore(name))
        self.n = 0

    def bump(self, k):
        self.n += k
        return self.n


class Em:

    ENGS = ("tensor", "vector", "scalar", "gpsimd", "sync")

    def __init__(self):
        self.q = {e: [] for e in self.ENGS}
        self.lastwait = {e: {} for e in self.ENGS}
        self.trace = {e: [] for e in self.ENGS}

    def wait(self, eng, sem, cnt=None):
        cnt = sem.n if cnt is None else cnt
        if cnt is None or cnt <= 0:
            return
        lw = self.lastwait[eng]
        if lw.get(id(sem), -1) >= cnt:
            return
        lw[id(sem)] = cnt
        self.trace[eng].append(("wait", sem.h.name, cnt))
        self.q[eng].append(lambda e, s=sem.h, c=cnt: e.wait_ge(s, c))

    def qop(self, eng, fn):
        self.q[eng].append(fn)

    def op(self, eng, sem, fn, k=1):
        v = sem.bump(k)
        import inspect
        try:
            srcl = inspect.getsource(fn).strip().split("\n")[0][:80]
        except Exception:
            srcl = "?"
        self.trace[eng].append(("op", sem.h.name, v, srcl))
        self.q[eng].append(lambda e, f=fn, s=sem.h, kk=k: f(e).then_inc(s, kk))
        return v

    def opw(self, eng, sem, fn, k=1):
        v = self.op(eng, sem, fn, k)
        self.wait(eng, sem, v)
        return v

    def raw(self, eng, fn):
        self.q[eng].append(fn)

    def barrier(self, sems):
        for eng in self.ENGS:
            for s in sems:
                self.wait(eng, s)


class C:
    pass


def build_nc(taps=(), stages=99):
    taps = set(taps)
    nc = bass.Bass("TRN2", target_bir_lowering=False)
    c = C()
    c.nc = nc
    c.taps = taps

    dx = nc.dram_tensor("x_own", [T, D], F32, kind="ExternalInput")
    dxp = nc.dram_tensor("x_pre", [T, D], F32, kind="ExternalInput")
    dwq = nc.dram_tensor("wqT", [D, D], F32, kind="ExternalInput")
    dwk = nc.dram_tensor("wkT", [D, D], F32, kind="ExternalInput")
    dwv = nc.dram_tensor("wvT", [D, D], F32, kind="ExternalInput")
    dwg = nc.dram_tensor("wgT", [D, D], F32, kind="ExternalInput")
    dwo = nc.dram_tensor("woT", [D, D], F32, kind="ExternalInput")
    dwf = nc.dram_tensor("wfT", [D, D], I8, kind="ExternalInput")
    dsl = nc.dram_tensor("slopes", [H], F32, kind="ExternalInput")
    dnw = nc.dram_tensor("normw", [D], F32, kind="ExternalInput")
    dws = nc.dram_tensor("wscale", [D], F32, kind="ExternalInput")
    dfb = nc.dram_tensor("fbias", [D], F32, kind="ExternalInput")
    dout = nc.dram_tensor("out", [T, D], F32, kind="ExternalOutput")
    c.dram = dict(x=dx, xp=dxp, wq=dwq, wk=dwk, wv=dwv, wg=dwg, wo=dwo,
                  wf=dwf, sl=dsl, nw=dnw, ws=dws, fb=dfb, out=dout)
    c.tap_dram = {}
    for name, shape, dt in [
        ("tap_xnfpre", [128, NCH * T], BF16), ("tap_xnf", [128, NCH * T], BF16),
        ("tap_kv", [128, H * HD], F32), ("tap_q0", [128, T], BF16),
        ("tap_k0", [128, T], BF16), ("tap_v0", [128, 4096], BF16),
        ("tap_outf", [128, NCH * T], BF16), ("tap_gout", [128, NCH * T], BF16),
        ("tap_x2", [128, NCH * T], BF16), ("tap_qxf", [128, NCH * 1024], BF16),
        ("tap_insc", [128, 16], F32),
        ("tap_cols", [128, 16 * 8], F32),
    ]:
        if name in taps:
            c.tap_dram[name] = nc.dram_tensor(name, shape, dt, kind="ExternalOutput")

    big = ExitStack()
    e = big.enter_context

    em = Em()
    c.em = em
    names = ["wl", "tp", "st", "tn", "vc", "sc", "gp",
             "x0", "x1", "t0", "t1", "wq", "wk", "wv", "wg0", "wg1",
             "wo", "wf", "cst", "xp4", "tp2"]
    c.s = {nm: Sem(big, nc, "s_" + nm) for nm in names}
    c.sems = list(c.s.values())

    arena1 = e(nc.sbuf_tensor("arena1", [128, 32768], BF16))
    arena2 = e(nc.sbuf_tensor("arena2", [128, 32768], BF16))
    arena3 = e(nc.sbuf_tensor("arena3", [128, 32768], BF16))
    c.arena1, c.arena2, c.arena3 = arena1, arena2, arena3

    z = {}
    off = 0

    def zalloc(name, n):
        nonlocal off
        z[name] = (off, off + n)
        off += n

    zalloc("qdec", H * 256)
    zalloc("dt", H * 256)
    zalloc("qbuf", T)
    zalloc("kbuf", T)
    zalloc("ktbuf", T)
    zalloc("wq", T)
    zalloc("wk", T)
    zalloc("qkdt0", 256)
    zalloc("qkdt1", 256)
    zalloc("kv_bf", 128)
    zalloc("kv_bf2", 128)
    zalloc("qkdt0b", 256)
    zalloc("qkdt1b", 256)
    zalloc("kbuf2", T)
    zalloc("shared", 10240)
    assert off <= 32768, off
    c.z = z

    def Z(name):
        lo, hi = z[name]
        return arena3[:, lo:hi]

    c.Z = Z
    sh = z["shared"][0]
    c.xn_tok = [arena3[:, sh:sh + 2048], arena3[:, sh + 2048:sh + 4096]]
    c.xchunk = [arena3[:, sh + 4096:sh + 6144], arena3[:, sh + 6144:sh + 8192]]
    c.sqz = arena3[:, sh + 8192:sh + 10240]
    c.vbuf = arena3[:, sh:sh + 4096]
    c.wv = arena3[:, sh + 4096:sh + 8192]

    kvst = e(nc.sbuf_tensor("kvst", [128, 2048], F32))
    c.kvst = kvst
    cols = e(nc.sbuf_tensor("cols", [128, 208], F32))
    c.cols = cols
    co = {}
    coff = 0

    def calloc(name, n):
        nonlocal coff
        co[name] = (coff, coff + n)
        coff += n

    for nm, n in [("slneg", 16), ("nw", 16), ("bd", 16), ("kdec", 32),
                  ("a2", 16), ("a2sq", 16), ("insc", 16), ("c256", 1),
                  ("s1", 1), ("s2", 2), ("mneg", 1), ("mm", 1), ("tv", 1),
                  ("rc", 1), ("acol", 1), ("bcol", 1), ("mx", 1), ("mn", 1),
                  ("t1", 1), ("t2", 1), ("scol", 1), ("invs", 1), ("c1", 1),
                  ("c2", 1), ("epsln", 1), ("var16", 16),
                  ("s1b", 1), ("s2b", 2), ("mnegb", 1), ("mmb", 1), ("tvb", 1),
                  ("rcb", 1), ("acolb", 1), ("bcolb", 1)]:
        calloc(nm, n)
    assert coff <= 208

    def CO(name):
        lo, hi = co[name]
        return cols[:, lo:lo + 1] if hi - lo == 2 else cols[:, lo:hi]

    c.CO = CO

    consts = e(nc.sbuf_tensor("consts", [128, 288], BF16))
    c.ones_col = consts[:, 0:1]
    c.onesq = consts[:, 32:160]
    c.ident = consts[:, 160:288]
    sigt = e(nc.sbuf_tensor("sigt", [128, 1024], BF16))
    c.sig = [sigt[:, 0:512], sigt[:, 512:1024]]
    biasb = e(nc.sbuf_tensor("biasb", [128, 2048], BF16))
    c.biasb = biasb
    c.qbuf2 = biasb[:, :]
    c.ktbuf2 = c.sqz

    a1f = arena1.bitcast(F32)
    c.iota_m1 = a1f[:, 0:256]
    c.iota_kd = a1f[:, 256:258]
    c.idxT0 = a1f[:, 512:768]
    c.slpos = a1f[:, 768:784]
    c.dt_scr = a1f[:, 1024:1280]

    kvb = kvst.bitcast(BF16)
    c.wg = [kvb[:, 0:2048], kvb[:, 2048:4096]]
    c.xchunk_p4 = kvb[:, 0:2048]
    c.sq_p4_bf = kvb[:, 2048:4096]
    c.outbuf = kvst[:, 0:2048]

    c.qxf = arena1[:, 0:16384]
    c.X32 = a1f[:, 8192:10240]
    c.Xbf = arena1[:, 16384:20480]
    c.qf = [arena1[:, 20480:22528], arena1[:, 22528:24576]]
    c.wsb = a1f[:, 12288:14336]
    c.ffn16 = arena1.bitcast(F16)[:, 28672:30720]
    c.tmp32 = a1f[:, 15360:15872]

    c.pP = [e(nc.psum_tensor("pP0", [128, 512], F32)),
            e(nc.psum_tensor("pP1", [128, 512], F32))]
    c.pBIG = e(nc.psum_tensor("pBIG", [128, 2048], F32))
    c.pS = [c.pBIG[:, 0:512], c.pBIG[:, 512:1024]]
    c.pO = [c.pBIG[:, 1024:1536], c.pBIG[:, 1536:2048]]
    c.pAUX = e(nc.psum_tensor("pAUX", [128, 512], F32))
    c.pAUX2 = e(nc.psum_tensor("pAUX2", [128, 512], F32))
    c.ps_kv = c.pAUX[:, 0:128]
    c.ps_kv2 = c.pAUX2[:, 0:128]
    c.ps_var = c.pAUX2[:, 256:272]

    def fmv(arena, fc):
        return arena[:, fc * T:(fc + 1) * T]

    c.fmv = fmv

    _skip = set(int(x) for x in os.environ.get("SKIP_STAGES", "").split(",") if x)

    def stage(i):
        return stages >= i and i not in _skip

    ph_consts(c)
    em.barrier(c.sems)
    if stage(1):
        ph_ln_transpose(c, dxp, arena1, "tap_xnfpre")
        em.barrier(c.sems)
    if stage(2):
        ph_prefix(c)
        em.barrier(c.sems)
        if "tap_kv" in taps:
            tap_store(c, c.tap_dram["tap_kv"], kvst[:, 0:H * HD])
            em.barrier(c.sems)
    if stage(3):
        ph_ln_transpose(c, dx, arena2, "tap_xnf")
        em.barrier(c.sems)
    if stage(4):
        ph_attn(c)
        em.barrier(c.sems)
        if "tap_outf" in taps:
            tap_store(c, c.tap_dram["tap_outf"], arena1[:, :])
            em.barrier(c.sems)
    if stage(5):
        ph_rms(c)
        em.barrier(c.sems)
    if stage(6):
        ph_gate(c)
        em.barrier(c.sems)
        if "tap_gout" in taps:
            tap_store(c, c.tap_dram["tap_gout"], arena1[:, :])
            em.barrier(c.sems)
    if stage(7):
        ph_outproj(c)
        em.barrier(c.sems)
        if "tap_x2" in taps:
            tap_store(c, c.tap_dram["tap_x2"], arena2[:, :])
            em.barrier(c.sems)
    if stage(8):
        ph_ffn(c)
    em.raw("sync", lambda e2, s=c.s["st"].h, v=c.s["st"].n: e2.wait_ge(s, v))

    with nc.Block() as block:
        @block.tensor
        def _(t):
            for f in em.q["tensor"]:
                f(t)

        @block.vector
        def _(v):
            for f in em.q["vector"]:
                f(v)

        @block.scalar
        def _(s2):
            for f in em.q["scalar"]:
                f(s2)

        @block.gpsimd
        def _(g):
            for f in em.q["gpsimd"]:
                f(g)

        @block.sync
        def _(sy):
            for f in em.q["sync"]:
                f(sy)

    big.close()
    return nc


def tap_store(c, dram, sbuf_ap):
    em, s = c.em, c.s
    n = sbuf_ap.shape[1]
    CH = 4096
    for i in range(0, n, CH):
        w = min(CH, n - i)
        em.op("sync", s["st"],
              lambda e, d=dram[:, i:i + w], src=sbuf_ap[:, i:i + w]: e.dma_start(d, src),
              16)
    em.wait("sync", s["st"])


def ph_consts(c):
    em, s = c.em, c.s
    _lite = int(os.environ.get("CONSTS_LITE", "0"))

    def g(fn):
        return em.op("gpsimd", s["gp"], fn)

    g(lambda e: e.memset(c.ones_col, 1.0))
    g(lambda e: e.memset(c.onesq, 1.0))
    em.wait("gpsimd", s["gp"])
    g(lambda e: e.affine_select(c.ident, c.onesq, [[-1, 128]], ALU.is_equal, 0.0,
                                base=0, channel_multiplier=1))
    g(lambda e: e.iota(c.iota_m1, [[1, 256]], base=1, channel_multiplier=0,
                       allow_small_or_imprecise_dtypes=True))
    g(lambda e: e.iota(c.iota_kd, [[-128, 2]], base=255, channel_multiplier=-1,
                       allow_small_or_imprecise_dtypes=True))
    g(lambda e: e.iota(c.idxT0, [[1, 256]], base=0, channel_multiplier=-1,
                       allow_small_or_imprecise_dtypes=True))
    g(lambda e: e.memset(c.CO("c256"), 256.0))
    g(lambda e: e.memset(c.CO("epsln"), EPS_LN))
    g(lambda e: e.memset(c.kvst[:, :], 0.0))
    g(lambda e: e.memset(c.Z("qkdt1"), 0.0))
    g(lambda e: e.memset(c.Z("qkdt1b"), 0.0))

    if _lite >= 3:
        return
    em.op("sync", s["st"],
          lambda e: e.dma_start(c.slpos, c.dram["sl"][None, :].broadcast_to((128, 16))),
          16)
    em.op("sync", s["st"],
          lambda e: e.dma_start(c.CO("nw"),
                                c.dram["nw"].rearrange("(cc p) -> p cc", p=128),
                                allow_slow_non_contiguous=True), 16)
    em.wait("vector", s["st"])
    em.op("vector", s["vc"],
          lambda e: e.tensor_scalar(c.CO("slneg"), c.slpos, -1.0, None, ALU.mult))

    em.wait("scalar", s["vc"])
    em.wait("scalar", s["gp"])
    em.wait("vector", s["gp"])
    if _lite >= 2:
        return
    for h in range(H):
        sl = c.CO("slneg")[:, h:h + 1]
        em.op("scalar", s["sc"],
              lambda e, h=h, sl=sl: e.activation(
                  c.Z("qdec")[:, h * 256:(h + 1) * 256], c.iota_m1, AF.Exp, scale=sl))
        em.op("scalar", s["sc"],
              lambda e, h=h, sl=sl: e.activation(
                  c.CO("kdec")[:, 2 * h:2 * h + 2], c.iota_kd, AF.Exp, scale=sl))
        em.op("scalar", s["sc"],
              lambda e, h=h, sl=sl: e.activation(
                  c.CO("bd")[:, h:h + 1], c.CO("c256"), AF.Exp, scale=sl))
        em.wait("vector", s["sc"])
        em.wait("vector", s["gp"])
        dm = em.op("vector", s["vc"],
                   lambda e, sl=sl: e.tensor_scalar(c.dt_scr, c.idxT0, sl, None, ALU.mult))
        em.wait("gpsimd", s["vc"], dm)
        af = em.op("gpsimd", s["gp"],
                   lambda e: e.affine_select(c.dt_scr, c.dt_scr, [[1, 256]], ALU.is_ge,
                                             -20000.0, base=0, channel_multiplier=-1))
        em.wait("scalar", s["gp"], af)
        em.op("scalar", s["sc"],
              lambda e, h=h: e.activation(
                  c.Z("dt")[:, h * 256:(h + 1) * 256], c.dt_scr, AF.Exp))


def ph_ln_transpose(c, src_dram, dst_arena, tapname):
    em, s = c.em, c.s
    import os
    _mode = os.environ.get("LN_MODE", "")
    _ser = os.environ.get("LN_SERIAL", "0") == "1"
    stats_done = [None, None]
    tp_done = [None, None]
    last_tv = [None, None]
    dst_v = dst_arena[:, :].rearrange("p (cc t) -> p cc t", cc=NCH)
    for tc in range(NCH):
        bi = tc % 2
        buf = c.xchunk[bi]
        if stats_done[bi] is not None:
            em.wait("gpsimd", s["vc"], stats_done[bi][0])
            em.wait("gpsimd", s["sc"], stats_done[bi][1])
        if _mode == "s3" and stats_done[1 - bi] is not None:
            em.wait("gpsimd", s["vc"], stats_done[1 - bi][0])
            em.wait("gpsimd", s["sc"], stats_done[1 - bi][1])
        sx = s["x0"] if bi == 0 else s["x1"]
        ld = em.op("gpsimd", sx,
                   lambda e, b=buf, d=src_dram, i=tc: e.dma_start(
                       b, d[i * 128:(i + 1) * 128, :]), 16)
        sfx = "b" if bi else ""
        S1, S2, MNEG, MM, TV, RC, ACOL, BCOL = (c.CO("s1" + sfx), c.CO("s2" + sfx),
                                                c.CO("mneg" + sfx), c.CO("mm" + sfx),
                                                c.CO("tv" + sfx), c.CO("rc" + sfx),
                                                c.CO("acol" + sfx), c.CO("bcol" + sfx))
        em.wait("scalar", sx, ld)
        if last_tv[bi] is not None:
            em.wait("scalar", s["vc"], last_tv[bi])
        sq = em.op("scalar", s["sc"],
                   lambda e, b=buf, S2=S2: e.activation(c.sqz, b, AF.Square,
                                                        accum_out=S2))
        em.wait("vector", sx, ld)
        em.opw("vector", s["vc"],
               lambda e, b=buf, S1=S1: e.reduce_sum(S1, b, axis=AX.X))
        em.wait("vector", s["sc"], sq)

        def vv(fn):
            return em.opw("vector", s["vc"], fn)

        vv(lambda e, S1=S1, MNEG=MNEG: e.tensor_scalar(MNEG, S1, -1.0 / D, None, ALU.mult))
        vv(lambda e, MNEG=MNEG, MM=MM: e.tensor_tensor(MM, MNEG, MNEG, ALU.mult))
        vv(lambda e, S2=S2, TV=TV: e.tensor_scalar(TV, S2, 1.0 / D, EPS_LN,
                                                   ALU.mult, ALU.add))
        last_tv[bi] = vv(lambda e, TV=TV, MM=MM: e.tensor_tensor(TV, TV, MM,
                                                                 ALU.subtract))
        rc = vv(lambda e, TV=TV, RC=RC: e.reciprocal(RC, TV))
        em.wait("scalar", s["vc"], rc)
        sq2 = em.op("scalar", s["sc"],
                    lambda e, RC=RC, ACOL=ACOL: e.activation(ACOL, RC, AF.Sqrt))
        em.wait("vector", s["sc"], sq2)
        bc = vv(lambda e, MNEG=MNEG, ACOL=ACOL, BCOL=BCOL: e.tensor_tensor(
            BCOL, MNEG, ACOL, ALU.mult))
        if _mode == "s1":
            em.barrier(c.sems)
        xb = c.xn_tok[bi]
        stp = s["t0"] if bi == 0 else s["t1"]
        if tp_done[bi] is not None:
            em.wait("vector", stp, tp_done[bi])
        ap = vv(lambda e, b=buf, o=xb, ACOL=ACOL, BCOL=BCOL: e.tensor_scalar(
            o, b, ACOL, BCOL, ALU.mult, ALU.add))
        stats_done[bi] = (ap, sq)
        if "tap_cols" in c.taps:
            em.wait("sync", s["vc"], ap)
            for ci, col in enumerate([S1, S2, MNEG, TV, RC, ACOL, BCOL]):
                em.op("sync", s["st"],
                      lambda e, tc=tc, ci=ci, col=col: e.dma_start(
                          c.tap_dram["tap_cols"][:, tc * 8 + ci:tc * 8 + ci + 1], col,
                          allow_slow_non_contiguous=True), 16)
        em.wait("sync", s["vc"], ap)
        tp_done[bi] = em.op("sync", stp,
                            lambda e, o=dst_v[:, :, tc * 128:(tc + 1) * 128], i=xb:
                            e.dma_start(o, i, transpose=True), 16)
    if tapname in c.taps and _mode != "notp":
        em.barrier(c.sems)
        tap_store(c, c.tap_dram[tapname], dst_arena[:, :])


def _load_w_slice(c, dram, col0, ncols, dst, sem, war_tn=None):
    em, s = c.em, c.s
    if war_tn is not None:
        em.wait("gpsimd", s["tn"], war_tn)
    src = dram[:, col0:col0 + ncols].rearrange("(cc p) j -> p cc j", p=128)
    dstv = dst.rearrange("p (cc j) -> p cc j", cc=NCH)
    return em.op("gpsimd", sem, lambda e: e.dma_start(dstv, src), 16)


def _vproj(c, xnF_arena, g, st):
    em, s = c.em, c.s
    wv_ld = _load_w_slice(c, c.dram["wv"], g * 256, 256, c.wv, s["wv"], st.get("wv_war"))
    wvv = c.wv.rearrange("p (cc j) -> p cc j", cc=NCH)
    em.wait("scalar", s["tn"])
    for tt in range(NCH):
        ps = c.pP[tt % 2]
        em.wait("tensor", s["wv"], wv_ld)
        pe = st["pP_evac"][tt % 2]
        if pe is not None:
            em.wait("tensor", s["sc"], pe)
        for fc in range(NCH):
            fn = (lambda e, ps=ps, fc=fc, tt=tt: e.matmul(
                ps[:, 0:256], c.fmv(xnF_arena, fc)[:, tt * 128:(tt + 1) * 128],
                wvv[:, fc, :], start=(fc == 0), stop=(fc == NCH - 1)))
            if fc < NCH - 1:
                em.qop("tensor", fn)
            else:
                em.op("tensor", s["tn"], fn)
        em.wait("scalar", s["tn"])
        st["pP_evac"][tt % 2] = em.op("scalar", s["sc"],
                                      lambda e, ps=ps, tt=tt: e.activation(
                                          c.vbuf[:, tt * 256:(tt + 1) * 256],
                                          ps[:, 0:256], AF.Silu))
    st["wv_war"] = c.s["tn"].n


def _fproj(c, xnF_arena, wdram, col0, dstbuf, wbuf, st, wkey, wsem):
    em, s = c.em, c.s
    wld = _load_w_slice(c, wdram, col0, 128, wbuf, wsem, st.get(wkey))
    wv = wbuf.rearrange("p (cc j) -> p cc j", cc=NCH)
    em.wait("scalar", s["tn"])
    em.wait("scalar", s["tp"])
    for nt in range(4):
        ps = c.pP[nt % 2]
        em.wait("tensor", wsem, wld)
        pe = st["pP_evac"][nt % 2]
        if pe is not None:
            em.wait("tensor", s["sc"], pe)
        for fc in range(NCH):
            fn = (lambda e, ps=ps, fc=fc, nt=nt, w=wv: e.matmul(
                ps[:, :], w[:, fc, :],
                c.fmv(xnF_arena, fc)[:, nt * 512:(nt + 1) * 512],
                start=(fc == 0), stop=(fc == NCH - 1)))
            if fc < NCH - 1:
                em.qop("tensor", fn)
            else:
                em.op("tensor", s["tn"], fn)
        em.wait("scalar", s["tn"])
        st["pP_evac"][nt % 2] = em.op("scalar", s["sc"],
                                      lambda e, ps=ps, nt=nt, d=dstbuf: e.activation(
                                          d[:, nt * 512:(nt + 1) * 512], ps[:, :],
                                          AF.Silu))
    st[wkey] = c.s["tn"].n


def _ktranspose(c, h, kb, ktb, tsem):
    em, s = c.em, c.s
    ktv = ktb.rearrange("p (cc t) -> p cc t", cc=NCH)
    em.wait("sync", s["sc"])
    em.wait("sync", s["tn"])
    em.wait("sync", s["vc"])
    tp = em.op("sync", tsem,
               lambda e, ktv=ktv, kb=kb: e.dma_start(ktv, kb, transpose=True), 16)
    em.wait("vector", tsem, tp)
    last = None
    for tile in range(NCH):
        u = tile % 2
        last = em.op("vector", s["vc"],
                     lambda e, t=tile, u=u, h=h, ktb=ktb: e.tensor_scalar(
                         ktb[:, t * 128:(t + 1) * 128],
                         ktb[:, t * 128:(t + 1) * 128],
                         c.CO("kdec")[:, 2 * h + u:2 * h + u + 1], None, ALU.mult))
    return last


def _kv_chain(c, h, hg, ktiles_ready):
    em, s = c.em, c.s
    kvs = c.kvst[:, h * 128:(h + 1) * 128]
    stt_done = None
    em.wait("tensor", s["sc"])
    for j in range(NBLK):
        em.wait("tensor", s["vc"], stt_done if stt_done is not None else ktiles_ready)
        for u in range(2):
            tile = 2 * j + u
            fn = (lambda e, t=tile, u=u, hg=hg: e.matmul(
                c.ps_kv, c.Z("ktbuf")[:, t * 128:(t + 1) * 128],
                c.vbuf[:, t * 256 + hg * 128:t * 256 + (hg + 1) * 128],
                start=(u == 0), stop=(u == 1)))
            if u == 0:
                em.qop("tensor", fn)
            else:
                em.op("tensor", s["tn"], fn)
        em.wait("vector", s["tn"])
        stt_done = em.opw("vector", s["vc"],
                          lambda e, kvs=kvs, h=h: e.scalar_tensor_tensor(
                              kvs, kvs, c.CO("bd")[:, h:h + 1], c.ps_kv,
                              ALU.mult, ALU.add))


def _kv_chain2(c, h, hg, ktiles_ready):
    em, s = c.em, c.s
    kvs = c.kvst[:, h * 128:(h + 1) * 128]
    stt_done = None
    em.wait("tensor", s["sc"])
    for j in range(NBLK):
        em.wait("tensor", s["vc"], stt_done if stt_done is not None else ktiles_ready)
        for u in range(2):
            tile = 2 * j + u
            fn = (lambda e, t=tile, u=u, hg=hg: e.matmul(
                c.ps_kv2, c.ktbuf2[:, t * 128:(t + 1) * 128],
                c.vbuf[:, t * 256 + hg * 128:t * 256 + (hg + 1) * 128],
                start=(u == 0), stop=(u == 1)))
            if u == 0:
                em.qop("tensor", fn)
            else:
                em.op("tensor", s["tn"], fn)
        em.wait("vector", s["tn"])
        stt_done = em.opw("vector", s["vc"],
                          lambda e, kvs=kvs, h=h: e.scalar_tensor_tensor(
                              kvs, kvs, c.CO("bd")[:, h:h + 1], c.ps_kv2,
                              ALU.mult, ALU.add))


def ph_prefix(c):
    em, s = c.em, c.s
    st = {"pP_evac": [None, None]}
    for g in range(H // 2):
        h0, h1 = 2 * g, 2 * g + 1
        _vproj(c, c.arena1, g, st)
        _fproj(c, c.arena1, c.dram["wk"], h0 * 128, c.Z("kbuf"), c.Z("wk"), st,
               "wk_war", s["wk"])
        _fproj(c, c.arena1, c.dram["wk"], h1 * 128, c.Z("kbuf2"), c.Z("wk"), st,
               "wk_war", s["wk"])
        kr0 = _ktranspose(c, h0, c.Z("kbuf"), c.Z("ktbuf"), s["tp"])
        kr1 = _ktranspose(c, h1, c.Z("kbuf2"), c.ktbuf2, s["tp2"])
        _kv_chain(c, h0, 0, kr0)
        _kv_chain2(c, h1, 1, kr1)


def ph_attn(c):
    em, s = c.em, c.s
    st = {"pP_evac": [None, None]}
    for g in range(int(os.environ.get("ATTN_GROUPS", str(H // 2)))):
        h0, h1 = 2 * g, 2 * g + 1
        _vproj(c, c.arena2, g, st)
        _fproj(c, c.arena2, c.dram["wk"], h0 * 128, c.Z("kbuf"), c.Z("wk"), st,
               "wk_war", s["wk"])
        _fproj(c, c.arena2, c.dram["wk"], h1 * 128, c.Z("kbuf2"), c.Z("wk"), st,
               "wk_war", s["wk"])
        kr0 = _ktranspose(c, h0, c.Z("kbuf"), c.Z("ktbuf"), s["tp"])
        kr1 = _ktranspose(c, h1, c.Z("kbuf2"), c.ktbuf2, s["tp2"])
        _fproj(c, c.arena2, c.dram["wq"], h0 * 128, c.Z("qbuf"), c.Z("wq"), st,
               "wq_war", s["wq"])
        _fproj(c, c.arena2, c.dram["wq"], h1 * 128, c.qbuf2, c.Z("wq"), st,
               "wq_war", s["wq"])
        if "tap_q0" in c.taps and g == 0:
            em.barrier(c.sems)
            tap_store(c, c.tap_dram["tap_q0"], c.Z("qbuf"))
            tap_store(c, c.tap_dram["tap_k0"], c.Z("kbuf"))
            tap_store(c, c.tap_dram["tap_v0"], c.vbuf)
        _scan_pair(c, h0, h1, kr0, kr1)


def _scan_pair(c, h0, h1, kr0, kr1):
    em, s = c.em, c.s
    hh_state = []
    for hh, h in enumerate((h0, h1)):
        hh_state.append(dict(
            h=h,
            qb=c.Z("qbuf") if hh == 0 else c.qbuf2,
            kb=c.Z("kbuf") if hh == 0 else c.Z("kbuf2"),
            ktb=c.Z("ktbuf") if hh == 0 else c.ktbuf2,
            q0t=c.Z("qkdt0") if hh == 0 else c.Z("qkdt0b"),
            q1t=c.Z("qkdt1") if hh == 0 else c.Z("qkdt1b"),
            kvbf=c.Z("kv_bf") if hh == 0 else c.Z("kv_bf2"),
            kvs=c.kvst[:, h * 128:(h + 1) * 128],
            qdec=c.Z("qdec")[:, h * 256:(h + 1) * 256],
            dt=c.Z("dt")[:, h * 256:(h + 1) * 256],
            pS=c.pS[hh], pO=c.pO[hh],
            pKV=c.pAUX[:, 0:128] if hh == 0 else c.pAUX2[:, 0:128],
            stt=None, mask=None, evac=None, inter_mm=None, intra_mm=None,
            ktr=kr0 if hh == 0 else kr1,
        ))
    em.wait("tensor", s["sc"])
    em.wait("tensor", s["vc"])
    for j in range(NBLK):
        for hh in range(2):
            S = hh_state[hh]
            if S["mask"] is not None:
                em.wait("tensor", s["vc"], S["mask"])
            for u in range(2):
                fn = (lambda e, j=j, u=u, S=S: e.matmul(
                    S["pS"][:, u * 256:(u + 1) * 256],
                    S["kb"][:, j * 256 + u * 128:j * 256 + (u + 1) * 128],
                    S["qb"][:, j * 256:(j + 1) * 256], start=True, stop=True))
                if u == 0:
                    em.qop("tensor", fn)
                else:
                    em.op("tensor", s["tn"], fn)
            S["sc_mm"] = c.s["tn"].n
        for hh in range(2):
            S = hh_state[hh]
            em.wait("vector", s["tn"], S["sc_mm"])
            if S["intra_mm"] is not None:
                em.wait("vector", s["tn"], S["intra_mm"])
            em.op("vector", s["vc"],
                  lambda e, S=S: e.tensor_tensor(S["q0t"], S["pS"][:, 0:256],
                                                 S["dt"], ALU.mult))
            S["mask"] = em.op("vector", s["vc"],
                              lambda e, S=S: e.tensor_tensor(
                                  S["q1t"][:, 128:256], S["pS"][:, 384:512],
                                  S["dt"][:, 0:128], ALU.mult))
            S["qt"] = em.op("vector", s["vc"],
                            lambda e, j=j, S=S: e.tensor_tensor(
                                S["qb"][:, j * 256:(j + 1) * 256],
                                S["qb"][:, j * 256:(j + 1) * 256], S["qdec"],
                                ALU.mult))
            em.wait("scalar", s["vc"], S["stt"] if S["stt"] is not None else S["ktr"])
            if S["inter_mm"] is not None:
                em.wait("scalar", s["tn"], S["inter_mm"])
            S["kc"] = em.op("scalar", s["sc"],
                            lambda e, S=S: e.copy(S["kvbf"], S["kvs"]))
        for hh in range(2):
            S = hh_state[hh]
            em.wait("tensor", s["sc"], S["kc"])
            em.wait("tensor", s["vc"], S["qt"])
            if S["evac"] is not None:
                em.wait("tensor", s["sc"], S["evac"])
            em.qop("tensor",
                   lambda e, j=j, S=S: e.matmul(
                       S["pO"][:, 0:256], S["kvbf"],
                       S["qb"][:, j * 256:(j + 1) * 256], start=True, stop=False))
            S["inter_mm"] = None
            for u in range(2):
                qk = S["q0t"] if u == 0 else S["q1t"]
                fn = (lambda e, j=j, u=u, S=S, qk=qk: e.matmul(
                    S["pO"][:, 0:256],
                    c.vbuf[:, (2 * j + u) * 256 + (S["h"] % 2) * 128:
                           (2 * j + u) * 256 + (S["h"] % 2 + 1) * 128],
                    qk, start=False, stop=(u == 1)))
                if u == 0:
                    em.qop("tensor", fn)
                else:
                    em.op("tensor", s["tn"], fn)
            S["inter_mm"] = S["intra_mm"] = c.s["tn"].n
            em.wait("tensor", s["vc"], S["stt"] if S["stt"] is not None else S["ktr"])
            for u in range(2):
                tile = 2 * j + u
                fn = (lambda e, t=tile, u=u, S=S: e.matmul(
                    S["pKV"], S["ktb"][:, t * 128:(t + 1) * 128],
                    c.vbuf[:, t * 256 + (S["h"] % 2) * 128:
                           t * 256 + (S["h"] % 2 + 1) * 128],
                    start=(u == 0), stop=(u == 1)))
                if u == 0:
                    em.qop("tensor", fn)
                else:
                    em.op("tensor", s["tn"], fn)
            S["kv_mm"] = c.s["tn"].n
        for hh in range(2):
            S = hh_state[hh]
            em.wait("scalar", s["tn"], S["intra_mm"])
            S["evac"] = em.op("scalar", s["sc"],
                              lambda e, j=j, S=S: e.copy(
                                  c.arena1[:, S["h"] * 2048 + j * 256:
                                           S["h"] * 2048 + (j + 1) * 256],
                                  S["pO"][:, 0:256]))
            em.wait("vector", s["tn"], S["kv_mm"])
            S["stt"] = em.opw("vector", s["vc"],
                              lambda e, S=S: e.scalar_tensor_tensor(
                                  S["kvs"], S["kvs"], c.CO("bd")[:, S["h"]:S["h"] + 1],
                                  S["pKV"], ALU.mult, ALU.add))


def ph_rms(c):
    em, s = c.em, c.s
    acc_done = None
    for oc in range(NCH):
        em.wait("vector", s["tn"])
        sq = em.op("vector", s["vc"],
                   lambda e, oc=oc: e.tensor_tensor(
                       c.sqz, c.arena1[:, oc * T:(oc + 1) * T],
                       c.arena1[:, oc * T:(oc + 1) * T], ALU.mult))
        em.wait("tensor", s["vc"], sq)
        if acc_done is not None:
            em.wait("tensor", s["vc"], acc_done)
        for tc in range(NCH):
            fn = (lambda e, tc=tc: e.matmul(
                c.ps_var[:, tc:tc + 1], c.sqz[:, tc * 128:(tc + 1) * 128],
                c.ones_col, start=True, stop=True))
            if tc < NCH - 1:
                em.qop("tensor", fn)
            else:
                em.op("tensor", s["tn"], fn)
        em.wait("vector", s["tn"])
        if oc == 0:
            acc_done = em.opw("vector", s["vc"],
                              lambda e: e.tensor_copy(c.CO("var16"), c.ps_var))
        else:
            acc_done = em.opw("vector", s["vc"],
                              lambda e: e.tensor_tensor(c.CO("var16"), c.CO("var16"),
                                                        c.ps_var, ALU.add))

    def vv(fn):
        return em.opw("vector", s["vc"], fn)

    vv(lambda e: e.tensor_scalar(c.CO("var16"), c.CO("var16"), 1.0 / D, EPS_RMS,
                                 ALU.mult, ALU.add))
    rc = vv(lambda e: e.reciprocal(c.CO("var16"), c.CO("var16")))
    em.wait("scalar", s["vc"], rc)
    sq = em.op("scalar", s["sc"],
               lambda e: e.activation(c.CO("a2"), c.CO("var16"), AF.Sqrt))
    em.wait("vector", s["sc"], sq)
    vv(lambda e: e.tensor_tensor(c.CO("a2sq"), c.CO("a2"), c.CO("a2"), ALU.mult))


def ph_gate(c):
    em, s = c.em, c.s
    for oc in range(NCH):
        em.op("gpsimd", s["wo"],
              lambda e, oc=oc: e.dma_start(
                  c.arena3[:, oc * T:(oc + 1) * T],
                  c.dram["wo"][oc * 128:(oc + 1) * 128, :]), 16)
    em.wait("vector", s["wo"])
    for oc in range(NCH):
        em.op("vector", s["vc"],
              lambda e, oc=oc: e.tensor_scalar(
                  c.arena3[:, oc * T:(oc + 1) * T], c.arena3[:, oc * T:(oc + 1) * T],
                  c.CO("nw")[:, oc:oc + 1], None, ALU.mult))
    prev_mm = [None, None]
    for gc in range(NCH):
        wb = c.wg[gc % 2]
        swg = s["wg0"] if gc % 2 == 0 else s["wg1"]
        if prev_mm[gc % 2] is not None:
            em.wait("gpsimd", s["tn"], prev_mm[gc % 2])
        ld = em.op("gpsimd", swg,
                   lambda e, gc=gc, wb=wb: e.dma_start(
                       wb.rearrange("p (cc j) -> p cc j", cc=NCH),
                       c.dram["wg"][:, gc * 128:(gc + 1) * 128].rearrange(
                           "(cc p) j -> p cc j", p=128)), 16)
        wv = wb.rearrange("p (cc j) -> p cc j", cc=NCH)
        for nt in range(4):
            ps = c.pP[nt % 2]
            em.wait("tensor", swg, ld)
            em.wait("tensor", s["sc"])
            for fc in range(NCH):
                fn = (lambda e, ps=ps, fc=fc, nt=nt, w=wv: e.matmul(
                    ps[:, :], w[:, fc, :],
                    c.fmv(c.arena2, fc)[:, nt * 512:(nt + 1) * 512],
                    start=(fc == 0), stop=(fc == NCH - 1)))
                if fc < NCH - 1:
                    em.qop("tensor", fn)
                else:
                    em.op("tensor", s["tn"], fn)
            em.wait("scalar", s["tn"])
            em.wait("scalar", s["vc"])
            sg = em.op("scalar", s["sc"],
                       lambda e, ps=ps, nt=nt: e.activation(
                           c.sig[nt % 2], ps[:, :], AF.Sigmoid))
            em.wait("vector", s["sc"], sg)
            em.op("vector", s["vc"],
                  lambda e, gc=gc, nt=nt: e.tensor_tensor(
                      c.arena1[:, gc * T + nt * 512:gc * T + (nt + 1) * 512],
                      c.arena1[:, gc * T + nt * 512:gc * T + (nt + 1) * 512],
                      c.sig[nt % 2], ALU.mult))
        prev_mm[gc % 2] = c.s["tn"].n


def ph_outproj(c):
    import os
    _lvl = int(os.environ.get("P4_LEVEL", "3"))
    em, s = c.em, c.s
    evac_done = None
    _ntc = int(os.environ.get("P4_NTC", str(NCH)))
    for tc in range(_ntc):
        if evac_done is not None:
            em.wait("gpsimd", s["vc"], evac_done)
        if _lvl >= 1:
            ld = em.op("gpsimd", s["xp4"],
                       lambda e, tc=tc: e.dma_start(
                           c.xchunk_p4, c.dram["x"][tc * 128:(tc + 1) * 128, :]), 16)
        if evac_done is not None:
            em.wait("tensor", s["vc"], evac_done)
            em.wait("tensor", s["sc"])
        for ft in range(4):
            for kc in range(NCH):
                fn = (lambda e, tc=tc, ft=ft, kc=kc: e.matmul(
                    c.pBIG[:, ft * 512:(ft + 1) * 512],
                    c.arena1[:, kc * T + tc * 128:kc * T + (tc + 1) * 128],
                    c.arena3[:, kc * T + ft * 512:kc * T + (ft + 1) * 512],
                    start=(kc == 0), stop=(kc == NCH - 1)))
                if ft == 3 and kc == NCH - 1:
                    em.op("tensor", s["tn"], fn)
                else:
                    em.qop("tensor", fn)
        mm = c.s["tn"].n
        em.wait("scalar", s["tn"], mm)
        cp1 = em.op("scalar", s["sc"],
                    lambda e: e.activation(c.sq_p4_bf, c.pBIG[:, :], AF.Copy,
                                           accum_out=c.CO("s1")))
        em.wait("scalar", s["sc"], cp1)
        sq = em.op("scalar", s["sc"],
                   lambda e: e.activation(c.sq_p4_bf, c.pBIG[:, :], AF.Square,
                                          accum_out=c.CO("s2")))
        em.wait("vector", s["sc"], sq)

        def vv(fn):
            return em.opw("vector", s["vc"], fn)

        vv(lambda e: e.tensor_scalar(c.CO("mneg"), c.CO("s1"), -1.0 / D, None, ALU.mult))
        vv(lambda e: e.tensor_tensor(c.CO("mm"), c.CO("mneg"), c.CO("mneg"), ALU.mult))
        vv(lambda e: e.tensor_scalar(c.CO("tv"), c.CO("s2"), 1.0 / D, None, ALU.mult))
        vv(lambda e: e.tensor_tensor(c.CO("tv"), c.CO("tv"), c.CO("mm"), ALU.subtract))
        vv(lambda e, tc=tc: e.scalar_tensor_tensor(
            c.CO("tv"), c.CO("tv"), c.CO("a2sq")[:, tc:tc + 1], c.CO("epsln"),
            ALU.mult, ALU.add))
        rc = vv(lambda e: e.reciprocal(c.CO("rc"), c.CO("tv")))
        em.wait("scalar", s["vc"], rc)
        sq2 = em.op("scalar", s["sc"],
                    lambda e: e.activation(c.CO("acol"), c.CO("rc"), AF.Sqrt))
        em.wait("vector", s["sc"], sq2)
        vv(lambda e, tc=tc: e.tensor_tensor(
            c.CO("c1"), c.CO("a2")[:, tc:tc + 1], c.CO("acol"), ALU.mult))
        vv(lambda e: e.tensor_tensor(c.CO("bcol"), c.CO("mneg"), c.CO("c1"),
                                     ALU.mult))
        if _lvl >= 1:
            em.wait("vector", s["xp4"], ld)
        vv(lambda e: e.tensor_scalar(c.xchunk_p4, c.xchunk_p4, c.CO("bcol"), None,
                                     ALU.add))
        evac_done = vv(lambda e, tc=tc: e.scalar_tensor_tensor(
            c.arena2[:, tc * 2048:(tc + 1) * 2048], c.pBIG[:, :], c.CO("c1"),
            c.xchunk_p4, ALU.mult, ALU.add))


def ph_ffn(c):
    em, s = c.em, c.s
    for kc in range(NCH):
        em.op("gpsimd", s["wf"],
              lambda e, kc=kc: e.dma_start(
                  c.arena3[:, kc * T:(kc + 1) * T],
                  c.dram["wf"][kc * 128:(kc + 1) * 128, :]), 16)
    wf_ld = c.s["wf"].n
    em.op("gpsimd", s["cst"],
          lambda e: e.dma_start(c.wsb, c.dram["ws"][None, :].broadcast_to((128, D))), 16)
    em.op("gpsimd", s["cst"],
          lambda e: e.dma_start(c.biasb[:, :],
                                c.dram["fb"][None, :].broadcast_to((128, D))), 16)

    qxv = c.qxf.rearrange("p (cc t) -> p cc t", cc=NCH)
    tp_done = [None, None]
    store_done = None
    prev_ep = None
    for hf in range(2):
        if hf == 1:
            em.wait("sync", s["tn"])
        for ti in range(8):
            tc = hf * 8 + ti
            x2c = c.arena2[:, tc * 2048:(tc + 1) * 2048]
            em.wait("scalar", s["vc"])
            sq = em.op("scalar", s["sc"],
                       lambda e, x2c=x2c: e.activation(c.Xbf[:, 0:2048], x2c, AF.Square,
                                                       accum_out=c.CO("s2")))
            em.opw("vector", s["vc"],
                   lambda e, x2c=x2c: e.reduce_sum(c.CO("s1"), x2c, axis=AX.X))
            em.opw("vector", s["vc"],
                   lambda e, x2c=x2c: e.tensor_reduce(c.CO("mx"), x2c, AX.X, ALU.max))
            em.opw("vector", s["vc"],
                   lambda e, x2c=x2c: e.tensor_reduce(c.CO("mn"), x2c, AX.X, ALU.min))
            em.wait("vector", s["sc"], sq)

            def vv(fn):
                return em.opw("vector", s["vc"], fn)

            vv(lambda e: e.tensor_scalar(c.CO("mneg"), c.CO("s1"), -1.0 / D, None,
                                         ALU.mult))
            vv(lambda e: e.tensor_tensor(c.CO("mm"), c.CO("mneg"), c.CO("mneg"),
                                         ALU.mult))
            vv(lambda e: e.tensor_scalar(c.CO("tv"), c.CO("s2"), 1.0 / D, EPS_LN,
                                         ALU.mult, ALU.add))
            vv(lambda e: e.tensor_tensor(c.CO("tv"), c.CO("tv"), c.CO("mm"),
                                         ALU.subtract))
            rc = vv(lambda e: e.reciprocal(c.CO("rc"), c.CO("tv")))
            em.wait("scalar", s["vc"], rc)
            sq2 = em.op("scalar", s["sc"],
                        lambda e: e.activation(c.CO("acol"), c.CO("rc"), AF.Sqrt))
            em.wait("vector", s["sc"], sq2)
            vv(lambda e: e.tensor_scalar(c.CO("t1"), c.CO("mx"), c.CO("mneg"), None,
                                         ALU.add))
            vv(lambda e: e.tensor_scalar(c.CO("t2"), c.CO("mn"), c.CO("mneg"), -1.0,
                                         ALU.add, ALU.mult))
            vv(lambda e: e.tensor_tensor(c.CO("t1"), c.CO("t1"), c.CO("t2"), ALU.max))
            vv(lambda e: e.tensor_tensor(c.CO("t1"), c.CO("t1"), c.CO("acol"), ALU.mult))
            vv(lambda e: e.tensor_scalar(c.CO("scol"), c.CO("t1"), EPS_Q, 1.0 / 127.0,
                                         ALU.max, ALU.mult))
            vv(lambda e: e.reciprocal(c.CO("invs"), c.CO("scol")))
            vv(lambda e: e.tensor_tensor(c.CO("c1"), c.CO("acol"), c.CO("invs"),
                                         ALU.mult))
            vv(lambda e: e.tensor_tensor(c.CO("c2"), c.CO("mneg"), c.CO("c1"),
                                         ALU.mult))
            vv(lambda e, tc=tc: e.tensor_copy(c.CO("insc")[:, tc:tc + 1], c.CO("scol")))
            bi = ti % 2
            vv(lambda e, x2c=x2c: e.tensor_scalar(c.X32, x2c, c.CO("c1"), c.CO("c2"),
                                                  ALU.mult, ALU.add))
            vv(lambda e: e.tensor_scalar(c.X32, c.X32, MAGIC, MAGIC, ALU.add,
                                         ALU.subtract))
            stq = s["t0"] if bi == 0 else s["t1"]
            if tp_done[bi] is not None:
                em.wait("vector", stq, tp_done[bi])
            qd = vv(lambda e, bi=bi: e.tensor_scalar(c.qf[bi], c.X32, 127.0, -128.0,
                                                     ALU.min, ALU.max))
            em.wait("sync", s["vc"], qd)
            tp_done[bi] = em.op("sync", stq,
                                lambda e, bi=bi, ti=ti: e.dma_start(
                                    qxv[:, :, ti * 128:(ti + 1) * 128], c.qf[bi],
                                    transpose=True), 16)
        if "tap_qxf" in c.taps and hf == 0:
            em.barrier(c.sems)
            tap_store(c, c.tap_dram["tap_qxf"], c.qxf)
        if "tap_insc" in c.taps and hf == 1:
            em.barrier(c.sems)
            tap_store(c, c.tap_dram["tap_insc"], c.CO("insc"))
        em.wait("tensor", s["wf"], wf_ld)
        em.wait("tensor", s["t0"])
        em.wait("tensor", s["t1"])
        em.wait("vector", s["cst"])
        for ti in range(8):
            tc = hf * 8 + ti
            if prev_ep is not None:
                em.wait("tensor", s["vc"], prev_ep)
            for ft in range(4):
                for kc in range(NCH):
                    fn = (lambda e, ti=ti, ft=ft, kc=kc: e.matmul(
                        c.pBIG[:, ft * 512:(ft + 1) * 512],
                        c.qxf[:, kc * 1024 + ti * 128:kc * 1024 + (ti + 1) * 128],
                        c.arena3[:, kc * T + ft * 512:kc * T + (ft + 1) * 512],
                        start=(kc == 0), stop=(kc == NCH - 1)))
                    if ft == 3 and kc == NCH - 1:
                        em.op("tensor", s["tn"], fn)
                    else:
                        em.qop("tensor", fn)
            mm = c.s["tn"].n
            em.wait("vector", s["tn"], mm)
            for ft in range(4):
                em.opw("vector", s["vc"],
                       lambda e, ft=ft, tc=tc: e.scalar_tensor_tensor(
                           c.tmp32, c.pBIG[:, ft * 512:(ft + 1) * 512],
                           c.CO("insc")[:, tc:tc + 1],
                           c.wsb[:, ft * 512:(ft + 1) * 512], ALU.mult, ALU.mult))
                em.opw("vector", s["vc"],
                       lambda e, ft=ft: e.tensor_tensor(
                           c.ffn16[:, ft * 512:(ft + 1) * 512], c.tmp32,
                           c.biasb[:, ft * 512:(ft + 1) * 512], ALU.add))
            prev_ep = c.s["vc"].n
            em.wait("scalar", s["vc"], prev_ep)
            if store_done is not None:
                em.wait("scalar", s["st"], store_done)
                em.wait("vector", s["st"], store_done)
            sq = em.op("scalar", s["sc"],
                       lambda e: e.activation(c.outbuf, c.ffn16, AF.Square,
                                              accum_out=c.CO("s2")))
            em.opw("vector", s["vc"],
                   lambda e: e.reduce_sum(c.CO("s1"), c.ffn16, axis=AX.X))

            def vv(fn):
                return em.opw("vector", s["vc"], fn)

            vv(lambda e: e.tensor_scalar(c.CO("mneg"), c.CO("s1"), -1.0 / D, None,
                                         ALU.mult))
            vv(lambda e: e.tensor_tensor(c.CO("mm"), c.CO("mneg"), c.CO("mneg"),
                                         ALU.mult))
            em.wait("vector", s["sc"], sq)
            vv(lambda e: e.tensor_scalar(c.CO("tv"), c.CO("s2"), 1.0 / D, EPS_LN,
                                         ALU.mult, ALU.add))
            vv(lambda e: e.tensor_tensor(c.CO("tv"), c.CO("tv"), c.CO("mm"),
                                         ALU.subtract))
            rc = vv(lambda e: e.reciprocal(c.CO("rc"), c.CO("tv")))
            em.wait("scalar", s["vc"], rc)
            sq2 = em.op("scalar", s["sc"],
                        lambda e: e.activation(c.CO("acol"), c.CO("rc"), AF.Sqrt))
            em.wait("vector", s["sc"], sq2)
            vv(lambda e: e.tensor_tensor(c.CO("bcol"), c.CO("mneg"),
                                         c.CO("acol"), ALU.mult))
            vv(lambda e: e.tensor_scalar(c.outbuf, c.ffn16, c.CO("acol"), c.CO("bcol"),
                                         ALU.mult, ALU.add))
            od = vv(lambda e, tc=tc: e.tensor_tensor(
                c.outbuf, c.outbuf, c.arena2[:, tc * 2048:(tc + 1) * 2048], ALU.add))
            em.wait("sync", s["vc"], od)
            store_done = em.op("sync", s["st"],
                               lambda e, tc=tc: e.dma_start(
                                   c.dram["out"][tc * 128:(tc + 1) * 128, :],
                                   c.outbuf), 16)


def _host_prep(inputs):
    x = np.asarray(inputs["L0_attn_hidden_states"], dtype=np.float32)
    slopes = np.ascontiguousarray(
        np.asarray(inputs["L0_attn_slope_rate"], dtype=np.float32).reshape(H))
    wqkv = np.asarray(inputs["L0_attn_w_qkv"], dtype=np.float32)
    r = wqkv.reshape(H, 3, HD, D)
    wqT = np.ascontiguousarray(r[:, 0].transpose(2, 0, 1).reshape(D, D))
    wkT = np.ascontiguousarray(r[:, 1].transpose(2, 0, 1).reshape(D, D))
    wvT = np.ascontiguousarray(r[:, 2].transpose(2, 0, 1).reshape(D, D))
    wgT = np.ascontiguousarray(np.asarray(inputs["L0_attn_w_gate"], np.float32).T)
    woT = np.ascontiguousarray(np.asarray(inputs["L0_attn_w_out"], np.float32).T)
    wfT = np.ascontiguousarray(np.asarray(inputs["L0_ffn_weight"]).T)
    normw = np.ascontiguousarray(np.asarray(inputs["L0_attn_norm_w"], np.float32))
    wscale = np.ascontiguousarray(
        np.asarray(inputs["L0_ffn_w_scale"], np.float32).reshape(D))
    fbias = np.ascontiguousarray(np.asarray(inputs["L0_ffn_bias"], np.float32))
    zeros = np.zeros((T, D), np.float32)
    in_maps = []
    for core in range(8):
        b, hf = core // 2, core % 2
        x_own = np.ascontiguousarray(x[b, hf * T:(hf + 1) * T])
        x_pre = np.ascontiguousarray(x[b, 0:T]) if hf == 1 else zeros
        in_maps.append(dict(x_own=x_own, x_pre=x_pre, wqT=wqT, wkT=wkT, wvT=wvT,
                            wgT=wgT, woT=woT, wfT=wfT, slopes=slopes, normw=normw,
                            wscale=wscale, fbias=fbias))
    return x, in_maps


def kernel(**inputs):
    x, in_maps = _host_prep(inputs)
    nc = build_nc()
    res = run_bass_kernel_spmd(nc, in_maps, core_ids=list(range(8)))
    out = np.empty((4, 4096, D), np.float32)
    for core in range(8):
        b, hf = core // 2, core % 2
        out[b, hf * T:(hf + 1) * T, :] = res.results[core]["out"]
    return out


# version 05
# speedup vs baseline: 1.0864x; 1.0109x over previous
import os
import numpy as np
from contextlib import ExitStack

import concourse.bass as bass
import concourse.mybir as mybir
from concourse.bass_utils import run_bass_kernel_spmd

F32 = mybir.dt.float32
BF16 = mybir.dt.bfloat16
F16 = mybir.dt.float16
I8 = mybir.dt.int8
AF = mybir.ActivationFunctionType
ALU = mybir.AluOpType
AX = mybir.AxisListType

T = 2048
D = 2048
H = 16
HD = 128
NCH = 16
NBLK = 8
MAGIC = 12582912.0
EPS_LN = 1e-5
EPS_RMS = 1e-6
EPS_Q = 1e-7


class Sem:
    def __init__(self, ctx, nc, name):
        self.h = ctx.enter_context(nc.semaphore(name))
        self.n = 0

    def bump(self, k):
        self.n += k
        return self.n


class Em:

    ENGS = ("tensor", "vector", "scalar", "gpsimd", "sync")

    def __init__(self):
        self.q = {e: [] for e in self.ENGS}
        self.lastwait = {e: {} for e in self.ENGS}
        self.trace = {e: [] for e in self.ENGS}

    def wait(self, eng, sem, cnt=None):
        cnt = sem.n if cnt is None else cnt
        if cnt is None or cnt <= 0:
            return
        lw = self.lastwait[eng]
        if lw.get(id(sem), -1) >= cnt:
            return
        lw[id(sem)] = cnt
        self.trace[eng].append(("wait", sem.h.name, cnt))
        self.q[eng].append(lambda e, s=sem.h, c=cnt: e.wait_ge(s, c))

    def qop(self, eng, fn):
        self.q[eng].append(fn)

    def op(self, eng, sem, fn, k=1):
        v = sem.bump(k)
        import inspect
        try:
            srcl = inspect.getsource(fn).strip().split("\n")[0][:80]
        except Exception:
            srcl = "?"
        self.trace[eng].append(("op", sem.h.name, v, srcl))
        self.q[eng].append(lambda e, f=fn, s=sem.h, kk=k: f(e).then_inc(s, kk))
        return v

    def opw(self, eng, sem, fn, k=1):
        v = self.op(eng, sem, fn, k)
        self.wait(eng, sem, v)
        return v

    def raw(self, eng, fn):
        self.q[eng].append(fn)

    def barrier(self, sems):
        for eng in self.ENGS:
            for s in sems:
                self.wait(eng, s)


class C:
    pass


def build_nc(taps=(), stages=99):
    taps = set(taps)
    nc = bass.Bass("TRN2", target_bir_lowering=False)
    c = C()
    c.nc = nc
    c.taps = taps

    dx = nc.dram_tensor("x_own", [T, D], F32, kind="ExternalInput")
    dxp = nc.dram_tensor("x_pre", [T, D], F32, kind="ExternalInput")
    dwq = nc.dram_tensor("wqT", [D, D], F32, kind="ExternalInput")
    dwk = nc.dram_tensor("wkT", [D, D], F32, kind="ExternalInput")
    dwv = nc.dram_tensor("wvT", [D, D], F32, kind="ExternalInput")
    dwg = nc.dram_tensor("wgT", [D, D], F32, kind="ExternalInput")
    dwo = nc.dram_tensor("woT", [D, D], F32, kind="ExternalInput")
    dwf = nc.dram_tensor("wfT", [D, D], I8, kind="ExternalInput")
    dsl = nc.dram_tensor("slopes", [H], F32, kind="ExternalInput")
    dnw = nc.dram_tensor("normw", [D], F32, kind="ExternalInput")
    dws = nc.dram_tensor("wscale", [D], F32, kind="ExternalInput")
    dfb = nc.dram_tensor("fbias", [D], F32, kind="ExternalInput")
    dout = nc.dram_tensor("out", [T, D], F32, kind="ExternalOutput")
    c.dram = dict(x=dx, xp=dxp, wq=dwq, wk=dwk, wv=dwv, wg=dwg, wo=dwo,
                  wf=dwf, sl=dsl, nw=dnw, ws=dws, fb=dfb, out=dout)
    c.tap_dram = {}
    for name, shape, dt in [
        ("tap_xnfpre", [128, NCH * T], BF16), ("tap_xnf", [128, NCH * T], BF16),
        ("tap_kv", [128, H * HD], F32), ("tap_q0", [128, T], BF16),
        ("tap_k0", [128, T], BF16), ("tap_v0", [128, 4096], BF16),
        ("tap_outf", [128, NCH * T], BF16), ("tap_gout", [128, NCH * T], BF16),
        ("tap_x2", [128, NCH * T], BF16), ("tap_qxf", [128, NCH * 1024], BF16),
        ("tap_insc", [128, 16], F32),
        ("tap_cols", [128, 16 * 8], F32),
    ]:
        if name in taps:
            c.tap_dram[name] = nc.dram_tensor(name, shape, dt, kind="ExternalOutput")

    big = ExitStack()
    e = big.enter_context

    em = Em()
    c.em = em
    names = ["wl", "tp", "st", "tn", "vc", "sc", "gp",
             "x0", "x1", "t0", "t1", "wq", "wk", "wv", "wg0", "wg1",
             "wo", "wf", "cst", "xp4", "tp2"]
    c.s = {nm: Sem(big, nc, "s_" + nm) for nm in names}
    c.sems = list(c.s.values())

    arena1 = e(nc.sbuf_tensor("arena1", [128, 32768], BF16))
    arena2 = e(nc.sbuf_tensor("arena2", [128, 32768], BF16))
    arena3 = e(nc.sbuf_tensor("arena3", [128, 32768], BF16))
    c.arena1, c.arena2, c.arena3 = arena1, arena2, arena3

    z = {}
    off = 0

    def zalloc(name, n):
        nonlocal off
        z[name] = (off, off + n)
        off += n

    zalloc("qdec", H * 256)
    zalloc("dt", H * 256)
    zalloc("qbuf", T)
    zalloc("kbuf", T)
    zalloc("ktbuf", T)
    zalloc("wq", T)
    zalloc("wk", T)
    zalloc("qkdt0", 256)
    zalloc("qkdt1", 256)
    zalloc("kv_bf", 128)
    zalloc("kv_bf2", 128)
    zalloc("qkdt0b", 256)
    zalloc("qkdt1b", 256)
    zalloc("kbuf2", T)
    zalloc("shared", 10240)
    assert off <= 32768, off
    c.z = z

    def Z(name):
        lo, hi = z[name]
        return arena3[:, lo:hi]

    c.Z = Z
    sh = z["shared"][0]
    c.xn_tok = [arena3[:, sh:sh + 2048], arena3[:, sh + 2048:sh + 4096]]
    c.xchunk = [arena3[:, sh + 4096:sh + 6144], arena3[:, sh + 6144:sh + 8192]]
    c.sqz = arena3[:, sh + 8192:sh + 10240]
    c.vbuf = arena3[:, sh:sh + 4096]
    c.wv = arena3[:, sh + 4096:sh + 8192]

    kvst = e(nc.sbuf_tensor("kvst", [128, 2048], F32))
    c.kvst = kvst
    cols = e(nc.sbuf_tensor("cols", [128, 208], F32))
    c.cols = cols
    co = {}
    coff = 0

    def calloc(name, n):
        nonlocal coff
        co[name] = (coff, coff + n)
        coff += n

    for nm, n in [("slneg", 16), ("nw", 16), ("bd", 16), ("kdec", 32),
                  ("a2", 16), ("a2sq", 16), ("insc", 16), ("c256", 1),
                  ("s1", 1), ("s2", 2), ("mneg", 1), ("mm", 1), ("tv", 1),
                  ("rc", 1), ("acol", 1), ("bcol", 1), ("mx", 1), ("mn", 1),
                  ("t1", 1), ("t2", 1), ("scol", 1), ("invs", 1), ("c1", 1),
                  ("c2", 1), ("epsln", 1), ("var16", 16),
                  ("s1b", 1), ("s2b", 2), ("mnegb", 1), ("mmb", 1), ("tvb", 1),
                  ("rcb", 1), ("acolb", 1), ("bcolb", 1)]:
        calloc(nm, n)
    assert coff <= 208

    def CO(name):
        lo, hi = co[name]
        return cols[:, lo:lo + 1] if hi - lo == 2 else cols[:, lo:hi]

    c.CO = CO

    consts = e(nc.sbuf_tensor("consts", [128, 288], BF16))
    c.ones_col = consts[:, 0:1]
    c.onesq = consts[:, 32:160]
    c.ident = consts[:, 160:288]
    sigt = e(nc.sbuf_tensor("sigt", [128, 1024], BF16))
    c.sig = [sigt[:, 0:512], sigt[:, 512:1024]]
    biasb = e(nc.sbuf_tensor("biasb", [128, 2048], BF16))
    c.biasb = biasb
    c.qbuf2 = biasb[:, :]
    c.ktbuf2 = c.sqz

    a1f = arena1.bitcast(F32)
    c.iota_m1 = a1f[:, 0:256]
    c.iota_kd = a1f[:, 256:258]
    c.idxT0 = a1f[:, 512:768]
    c.slpos = a1f[:, 768:784]
    c.dt_scr = a1f[:, 1024:1280]

    kvb = kvst.bitcast(BF16)
    c.wg = [kvb[:, 0:2048], kvb[:, 2048:4096]]
    c.xchunk_p4 = kvb[:, 0:2048]
    c.sq_p4_bf = kvb[:, 2048:4096]
    c.outbuf = kvst[:, 0:2048]

    c.qxf_q = [arena1[:, 0:8192], arena1[:, 8192:16384]]
    c.X32 = a1f[:, 8192:10240]
    c.Xbf = arena1[:, 16384:20480]
    c.qf = [arena1[:, 20480:22528], arena1[:, 22528:24576]]
    c.wsb = a1f[:, 12288:14336]
    c.ffn16 = arena1.bitcast(F16)[:, 28672:30720]
    c.tmp32 = a1f[:, 15360:15872]

    c.pP = [e(nc.psum_tensor("pP0", [128, 512], F32)),
            e(nc.psum_tensor("pP1", [128, 512], F32))]
    c.pBIG = e(nc.psum_tensor("pBIG", [128, 2048], F32))
    c.pS = [c.pBIG[:, 0:512], c.pBIG[:, 512:1024]]
    c.pO = [c.pBIG[:, 1024:1536], c.pBIG[:, 1536:2048]]
    c.pAUX = e(nc.psum_tensor("pAUX", [128, 512], F32))
    c.pAUX2 = e(nc.psum_tensor("pAUX2", [128, 512], F32))
    c.ps_kv = c.pAUX[:, 0:128]
    c.ps_kv2 = c.pAUX2[:, 0:128]
    c.ps_var = c.pAUX2[:, 256:272]

    def fmv(arena, fc):
        return arena[:, fc * T:(fc + 1) * T]

    c.fmv = fmv

    _skip = set(int(x) for x in os.environ.get("SKIP_STAGES", "").split(",") if x)

    def stage(i):
        return stages >= i and i not in _skip

    ph_consts(c)
    em.barrier(c.sems)
    if stage(1):
        ph_ln_transpose(c, dxp, arena1, "tap_xnfpre")
        em.barrier(c.sems)
    if stage(2):
        ph_prefix(c)
        em.barrier(c.sems)
        if "tap_kv" in taps:
            tap_store(c, c.tap_dram["tap_kv"], kvst[:, 0:H * HD])
            em.barrier(c.sems)
    if stage(3):
        ph_ln_transpose(c, dx, arena2, "tap_xnf")
        em.barrier(c.sems)
    if stage(4):
        ph_attn(c)
        em.barrier(c.sems)
        if "tap_outf" in taps:
            tap_store(c, c.tap_dram["tap_outf"], arena1[:, :])
            em.barrier(c.sems)
    if stage(5):
        ph_rms(c)
        em.barrier(c.sems)
    if stage(6):
        ph_gate(c)
        em.barrier(c.sems)
        if "tap_gout" in taps:
            tap_store(c, c.tap_dram["tap_gout"], arena1[:, :])
            em.barrier(c.sems)
    if stage(7):
        ph_outproj(c)
        em.barrier(c.sems)
        if "tap_x2" in taps:
            tap_store(c, c.tap_dram["tap_x2"], arena2[:, :])
            em.barrier(c.sems)
    if stage(8):
        ph_ffn(c)
    em.raw("sync", lambda e2, s=c.s["st"].h, v=c.s["st"].n: e2.wait_ge(s, v))

    with nc.Block() as block:
        @block.tensor
        def _(t):
            for f in em.q["tensor"]:
                f(t)

        @block.vector
        def _(v):
            for f in em.q["vector"]:
                f(v)

        @block.scalar
        def _(s2):
            for f in em.q["scalar"]:
                f(s2)

        @block.gpsimd
        def _(g):
            for f in em.q["gpsimd"]:
                f(g)

        @block.sync
        def _(sy):
            for f in em.q["sync"]:
                f(sy)

    big.close()
    return nc


def tap_store(c, dram, sbuf_ap):
    em, s = c.em, c.s
    n = sbuf_ap.shape[1]
    CH = 4096
    for i in range(0, n, CH):
        w = min(CH, n - i)
        em.op("sync", s["st"],
              lambda e, d=dram[:, i:i + w], src=sbuf_ap[:, i:i + w]: e.dma_start(d, src),
              16)
    em.wait("sync", s["st"])


def ph_consts(c):
    em, s = c.em, c.s
    _lite = int(os.environ.get("CONSTS_LITE", "0"))

    def g(fn):
        return em.op("gpsimd", s["gp"], fn)

    g(lambda e: e.memset(c.ones_col, 1.0))
    g(lambda e: e.memset(c.onesq, 1.0))
    em.wait("gpsimd", s["gp"])
    g(lambda e: e.affine_select(c.ident, c.onesq, [[-1, 128]], ALU.is_equal, 0.0,
                                base=0, channel_multiplier=1))
    g(lambda e: e.iota(c.iota_m1, [[1, 256]], base=1, channel_multiplier=0,
                       allow_small_or_imprecise_dtypes=True))
    g(lambda e: e.iota(c.iota_kd, [[-128, 2]], base=255, channel_multiplier=-1,
                       allow_small_or_imprecise_dtypes=True))
    g(lambda e: e.iota(c.idxT0, [[1, 256]], base=0, channel_multiplier=-1,
                       allow_small_or_imprecise_dtypes=True))
    g(lambda e: e.memset(c.CO("c256"), 256.0))
    g(lambda e: e.memset(c.CO("epsln"), EPS_LN))
    g(lambda e: e.memset(c.kvst[:, :], 0.0))
    g(lambda e: e.memset(c.Z("qkdt1"), 0.0))
    g(lambda e: e.memset(c.Z("qkdt1b"), 0.0))

    if _lite >= 3:
        return
    em.op("sync", s["st"],
          lambda e: e.dma_start(c.slpos, c.dram["sl"][None, :].broadcast_to((128, 16))),
          16)
    em.op("sync", s["st"],
          lambda e: e.dma_start(c.CO("nw"),
                                c.dram["nw"].rearrange("(cc p) -> p cc", p=128),
                                allow_slow_non_contiguous=True), 16)
    em.wait("vector", s["st"])
    em.op("vector", s["vc"],
          lambda e: e.tensor_scalar(c.CO("slneg"), c.slpos, -1.0, None, ALU.mult))

    em.wait("scalar", s["vc"])
    em.wait("scalar", s["gp"])
    em.wait("vector", s["gp"])
    if _lite >= 2:
        return
    for h in range(H):
        sl = c.CO("slneg")[:, h:h + 1]
        em.op("scalar", s["sc"],
              lambda e, h=h, sl=sl: e.activation(
                  c.Z("qdec")[:, h * 256:(h + 1) * 256], c.iota_m1, AF.Exp, scale=sl))
        em.op("scalar", s["sc"],
              lambda e, h=h, sl=sl: e.activation(
                  c.CO("kdec")[:, 2 * h:2 * h + 2], c.iota_kd, AF.Exp, scale=sl))
        em.op("scalar", s["sc"],
              lambda e, h=h, sl=sl: e.activation(
                  c.CO("bd")[:, h:h + 1], c.CO("c256"), AF.Exp, scale=sl))
        em.wait("vector", s["sc"])
        em.wait("vector", s["gp"])
        dm = em.op("vector", s["vc"],
                   lambda e, sl=sl: e.tensor_scalar(c.dt_scr, c.idxT0, sl, None, ALU.mult))
        em.wait("gpsimd", s["vc"], dm)
        af = em.op("gpsimd", s["gp"],
                   lambda e: e.affine_select(c.dt_scr, c.dt_scr, [[1, 256]], ALU.is_ge,
                                             -20000.0, base=0, channel_multiplier=-1))
        em.wait("scalar", s["gp"], af)
        em.op("scalar", s["sc"],
              lambda e, h=h: e.activation(
                  c.Z("dt")[:, h * 256:(h + 1) * 256], c.dt_scr, AF.Exp))


def ph_ln_transpose(c, src_dram, dst_arena, tapname):
    em, s = c.em, c.s
    import os
    _mode = os.environ.get("LN_MODE", "")
    _ser = os.environ.get("LN_SERIAL", "0") == "1"
    stats_done = [None, None]
    tp_done = [None, None]
    last_tv = [None, None]
    dst_v = dst_arena[:, :].rearrange("p (cc t) -> p cc t", cc=NCH)
    for tc in range(NCH):
        bi = tc % 2
        buf = c.xchunk[bi]
        if stats_done[bi] is not None:
            em.wait("gpsimd", s["vc"], stats_done[bi][0])
            em.wait("gpsimd", s["sc"], stats_done[bi][1])
        if _mode == "s3" and stats_done[1 - bi] is not None:
            em.wait("gpsimd", s["vc"], stats_done[1 - bi][0])
            em.wait("gpsimd", s["sc"], stats_done[1 - bi][1])
        sx = s["x0"] if bi == 0 else s["x1"]
        ld = em.op("gpsimd", sx,
                   lambda e, b=buf, d=src_dram, i=tc: e.dma_start(
                       b, d[i * 128:(i + 1) * 128, :]), 16)
        sfx = "b" if bi else ""
        S1, S2, MNEG, MM, TV, RC, ACOL, BCOL = (c.CO("s1" + sfx), c.CO("s2" + sfx),
                                                c.CO("mneg" + sfx), c.CO("mm" + sfx),
                                                c.CO("tv" + sfx), c.CO("rc" + sfx),
                                                c.CO("acol" + sfx), c.CO("bcol" + sfx))
        em.wait("scalar", sx, ld)
        if last_tv[bi] is not None:
            em.wait("scalar", s["vc"], last_tv[bi])
        sq = em.op("scalar", s["sc"],
                   lambda e, b=buf, S2=S2: e.activation(c.sqz, b, AF.Square,
                                                        accum_out=S2))
        em.wait("vector", sx, ld)
        em.opw("vector", s["vc"],
               lambda e, b=buf, S1=S1: e.reduce_sum(S1, b, axis=AX.X))
        em.wait("vector", s["sc"], sq)

        def vv(fn):
            return em.opw("vector", s["vc"], fn)

        vv(lambda e, S1=S1, MNEG=MNEG: e.tensor_scalar(MNEG, S1, -1.0 / D, None, ALU.mult))
        vv(lambda e, MNEG=MNEG, MM=MM: e.tensor_tensor(MM, MNEG, MNEG, ALU.mult))
        vv(lambda e, S2=S2, TV=TV: e.tensor_scalar(TV, S2, 1.0 / D, EPS_LN,
                                                   ALU.mult, ALU.add))
        last_tv[bi] = vv(lambda e, TV=TV, MM=MM: e.tensor_tensor(TV, TV, MM,
                                                                 ALU.subtract))
        rc = vv(lambda e, TV=TV, RC=RC: e.reciprocal(RC, TV))
        em.wait("scalar", s["vc"], rc)
        sq2 = em.op("scalar", s["sc"],
                    lambda e, RC=RC, ACOL=ACOL: e.activation(ACOL, RC, AF.Sqrt))
        em.wait("vector", s["sc"], sq2)
        bc = vv(lambda e, MNEG=MNEG, ACOL=ACOL, BCOL=BCOL: e.tensor_tensor(
            BCOL, MNEG, ACOL, ALU.mult))
        if _mode == "s1":
            em.barrier(c.sems)
        xb = c.xn_tok[bi]
        stp = s["t0"] if bi == 0 else s["t1"]
        if tp_done[bi] is not None:
            em.wait("vector", stp, tp_done[bi])
        ap = vv(lambda e, b=buf, o=xb, ACOL=ACOL, BCOL=BCOL: e.tensor_scalar(
            o, b, ACOL, BCOL, ALU.mult, ALU.add))
        stats_done[bi] = (ap, sq)
        if "tap_cols" in c.taps:
            em.wait("sync", s["vc"], ap)
            for ci, col in enumerate([S1, S2, MNEG, TV, RC, ACOL, BCOL]):
                em.op("sync", s["st"],
                      lambda e, tc=tc, ci=ci, col=col: e.dma_start(
                          c.tap_dram["tap_cols"][:, tc * 8 + ci:tc * 8 + ci + 1], col,
                          allow_slow_non_contiguous=True), 16)
        em.wait("sync", s["vc"], ap)
        tp_done[bi] = em.op("sync", stp,
                            lambda e, o=dst_v[:, :, tc * 128:(tc + 1) * 128], i=xb:
                            e.dma_start(o, i, transpose=True), 16)
    if tapname in c.taps and _mode != "notp":
        em.barrier(c.sems)
        tap_store(c, c.tap_dram[tapname], dst_arena[:, :])


def _load_w_slice(c, dram, col0, ncols, dst, sem, war_tn=None):
    em, s = c.em, c.s
    if war_tn is not None:
        em.wait("gpsimd", s["tn"], war_tn)
    src = dram[:, col0:col0 + ncols].rearrange("(cc p) j -> p cc j", p=128)
    dstv = dst.rearrange("p (cc j) -> p cc j", cc=NCH)
    return em.op("gpsimd", sem, lambda e: e.dma_start(dstv, src), 16)


def _vproj(c, xnF_arena, g, st):
    em, s = c.em, c.s
    wv_ld = _load_w_slice(c, c.dram["wv"], g * 256, 256, c.wv, s["wv"], st.get("wv_war"))
    wvv = c.wv.rearrange("p (cc j) -> p cc j", cc=NCH)
    em.wait("scalar", s["tn"])
    for tt in range(NCH):
        ps = c.pP[tt % 2]
        em.wait("tensor", s["wv"], wv_ld)
        pe = st["pP_evac"][tt % 2]
        if pe is not None:
            em.wait("tensor", s["sc"], pe)
        for fc in range(NCH):
            fn = (lambda e, ps=ps, fc=fc, tt=tt: e.matmul(
                ps[:, 0:256], c.fmv(xnF_arena, fc)[:, tt * 128:(tt + 1) * 128],
                wvv[:, fc, :], start=(fc == 0), stop=(fc == NCH - 1)))
            if fc < NCH - 1:
                em.qop("tensor", fn)
            else:
                em.op("tensor", s["tn"], fn)
        em.wait("scalar", s["tn"])
        st["pP_evac"][tt % 2] = em.op("scalar", s["sc"],
                                      lambda e, ps=ps, tt=tt: e.activation(
                                          c.vbuf[:, tt * 256:(tt + 1) * 256],
                                          ps[:, 0:256], AF.Silu))
    st["wv_war"] = c.s["tn"].n


def _fproj(c, xnF_arena, wdram, col0, dstbuf, wbuf, st, wkey, wsem):
    em, s = c.em, c.s
    wld = _load_w_slice(c, wdram, col0, 128, wbuf, wsem, st.get(wkey))
    wv = wbuf.rearrange("p (cc j) -> p cc j", cc=NCH)
    em.wait("scalar", s["tn"])
    em.wait("scalar", s["tp"])
    for nt in range(4):
        ps = c.pP[nt % 2]
        em.wait("tensor", wsem, wld)
        pe = st["pP_evac"][nt % 2]
        if pe is not None:
            em.wait("tensor", s["sc"], pe)
        for fc in range(NCH):
            fn = (lambda e, ps=ps, fc=fc, nt=nt, w=wv: e.matmul(
                ps[:, :], w[:, fc, :],
                c.fmv(xnF_arena, fc)[:, nt * 512:(nt + 1) * 512],
                start=(fc == 0), stop=(fc == NCH - 1)))
            if fc < NCH - 1:
                em.qop("tensor", fn)
            else:
                em.op("tensor", s["tn"], fn)
        em.wait("scalar", s["tn"])
        st["pP_evac"][nt % 2] = em.op("scalar", s["sc"],
                                      lambda e, ps=ps, nt=nt, d=dstbuf: e.activation(
                                          d[:, nt * 512:(nt + 1) * 512], ps[:, :],
                                          AF.Silu))
    st[wkey] = c.s["tn"].n


def _ktranspose(c, h, kb, ktb, tsem):
    em, s = c.em, c.s
    ktv = ktb.rearrange("p (cc t) -> p cc t", cc=NCH)
    em.wait("sync", s["sc"])
    em.wait("sync", s["tn"])
    em.wait("sync", s["vc"])
    tp = em.op("sync", tsem,
               lambda e, ktv=ktv, kb=kb: e.dma_start(ktv, kb, transpose=True), 16)
    em.wait("vector", tsem, tp)
    last = None
    for tile in range(NCH):
        u = tile % 2
        last = em.op("vector", s["vc"],
                     lambda e, t=tile, u=u, h=h, ktb=ktb: e.tensor_scalar(
                         ktb[:, t * 128:(t + 1) * 128],
                         ktb[:, t * 128:(t + 1) * 128],
                         c.CO("kdec")[:, 2 * h + u:2 * h + u + 1], None, ALU.mult))
    return last


def _kv_chain(c, h, hg, ktiles_ready):
    em, s = c.em, c.s
    kvs = c.kvst[:, h * 128:(h + 1) * 128]
    stt_done = None
    em.wait("tensor", s["sc"])
    for j in range(NBLK):
        em.wait("tensor", s["vc"], stt_done if stt_done is not None else ktiles_ready)
        for u in range(2):
            tile = 2 * j + u
            fn = (lambda e, t=tile, u=u, hg=hg: e.matmul(
                c.ps_kv, c.Z("ktbuf")[:, t * 128:(t + 1) * 128],
                c.vbuf[:, t * 256 + hg * 128:t * 256 + (hg + 1) * 128],
                start=(u == 0), stop=(u == 1)))
            if u == 0:
                em.qop("tensor", fn)
            else:
                em.op("tensor", s["tn"], fn)
        em.wait("vector", s["tn"])
        stt_done = em.opw("vector", s["vc"],
                          lambda e, kvs=kvs, h=h: e.scalar_tensor_tensor(
                              kvs, kvs, c.CO("bd")[:, h:h + 1], c.ps_kv,
                              ALU.mult, ALU.add))


def _kv_chain2(c, h, hg, ktiles_ready):
    em, s = c.em, c.s
    kvs = c.kvst[:, h * 128:(h + 1) * 128]
    stt_done = None
    em.wait("tensor", s["sc"])
    for j in range(NBLK):
        em.wait("tensor", s["vc"], stt_done if stt_done is not None else ktiles_ready)
        for u in range(2):
            tile = 2 * j + u
            fn = (lambda e, t=tile, u=u, hg=hg: e.matmul(
                c.ps_kv2, c.ktbuf2[:, t * 128:(t + 1) * 128],
                c.vbuf[:, t * 256 + hg * 128:t * 256 + (hg + 1) * 128],
                start=(u == 0), stop=(u == 1)))
            if u == 0:
                em.qop("tensor", fn)
            else:
                em.op("tensor", s["tn"], fn)
        em.wait("vector", s["tn"])
        stt_done = em.opw("vector", s["vc"],
                          lambda e, kvs=kvs, h=h: e.scalar_tensor_tensor(
                              kvs, kvs, c.CO("bd")[:, h:h + 1], c.ps_kv2,
                              ALU.mult, ALU.add))


def ph_prefix(c):
    em, s = c.em, c.s
    st = {"pP_evac": [None, None]}
    for g in range(H // 2):
        h0, h1 = 2 * g, 2 * g + 1
        _vproj(c, c.arena1, g, st)
        _fproj(c, c.arena1, c.dram["wk"], h0 * 128, c.Z("kbuf"), c.Z("wk"), st,
               "wk_war", s["wk"])
        _fproj(c, c.arena1, c.dram["wk"], h1 * 128, c.Z("kbuf2"), c.Z("wk"), st,
               "wk_war", s["wk"])
        kr0 = _ktranspose(c, h0, c.Z("kbuf"), c.Z("ktbuf"), s["tp"])
        kr1 = _ktranspose(c, h1, c.Z("kbuf2"), c.ktbuf2, s["tp2"])
        _kv_chain(c, h0, 0, kr0)
        _kv_chain2(c, h1, 1, kr1)


def ph_attn(c):
    em, s = c.em, c.s
    st = {"pP_evac": [None, None]}
    for g in range(int(os.environ.get("ATTN_GROUPS", str(H // 2)))):
        h0, h1 = 2 * g, 2 * g + 1
        _vproj(c, c.arena2, g, st)
        _fproj(c, c.arena2, c.dram["wk"], h0 * 128, c.Z("kbuf"), c.Z("wk"), st,
               "wk_war", s["wk"])
        _fproj(c, c.arena2, c.dram["wk"], h1 * 128, c.Z("kbuf2"), c.Z("wk"), st,
               "wk_war", s["wk"])
        kr0 = _ktranspose(c, h0, c.Z("kbuf"), c.Z("ktbuf"), s["tp"])
        kr1 = _ktranspose(c, h1, c.Z("kbuf2"), c.ktbuf2, s["tp2"])
        _fproj(c, c.arena2, c.dram["wq"], h0 * 128, c.Z("qbuf"), c.Z("wq"), st,
               "wq_war", s["wq"])
        _fproj(c, c.arena2, c.dram["wq"], h1 * 128, c.qbuf2, c.Z("wq"), st,
               "wq_war", s["wq"])
        if "tap_q0" in c.taps and g == 0:
            em.barrier(c.sems)
            tap_store(c, c.tap_dram["tap_q0"], c.Z("qbuf"))
            tap_store(c, c.tap_dram["tap_k0"], c.Z("kbuf"))
            tap_store(c, c.tap_dram["tap_v0"], c.vbuf)
        _scan_pair(c, h0, h1, kr0, kr1)


def _scan_pair(c, h0, h1, kr0, kr1):
    em, s = c.em, c.s
    hh_state = []
    for hh, h in enumerate((h0, h1)):
        hh_state.append(dict(
            h=h,
            qb=c.Z("qbuf") if hh == 0 else c.qbuf2,
            kb=c.Z("kbuf") if hh == 0 else c.Z("kbuf2"),
            ktb=c.Z("ktbuf") if hh == 0 else c.ktbuf2,
            q0t=c.Z("qkdt0") if hh == 0 else c.Z("qkdt0b"),
            q1t=c.Z("qkdt1") if hh == 0 else c.Z("qkdt1b"),
            kvbf=c.Z("kv_bf") if hh == 0 else c.Z("kv_bf2"),
            kvs=c.kvst[:, h * 128:(h + 1) * 128],
            qdec=c.Z("qdec")[:, h * 256:(h + 1) * 256],
            dt=c.Z("dt")[:, h * 256:(h + 1) * 256],
            pS=c.pS[hh], pO=c.pO[hh],
            pKV=c.pAUX[:, 0:128] if hh == 0 else c.pAUX2[:, 0:128],
            stt=None, mask=None, evac=None, inter_mm=None, intra_mm=None,
            ktr=kr0 if hh == 0 else kr1,
        ))
    em.wait("tensor", s["sc"])
    em.wait("tensor", s["vc"])
    for j in range(NBLK):
        for hh in range(2):
            S = hh_state[hh]
            if S["mask"] is not None:
                em.wait("tensor", s["vc"], S["mask"])
            for u in range(2):
                fn = (lambda e, j=j, u=u, S=S: e.matmul(
                    S["pS"][:, u * 256:(u + 1) * 256],
                    S["kb"][:, j * 256 + u * 128:j * 256 + (u + 1) * 128],
                    S["qb"][:, j * 256:(j + 1) * 256], start=True, stop=True))
                if u == 0:
                    em.qop("tensor", fn)
                else:
                    em.op("tensor", s["tn"], fn)
            S["sc_mm"] = c.s["tn"].n
        for hh in range(2):
            S = hh_state[hh]
            em.wait("vector", s["tn"], S["sc_mm"])
            if S["intra_mm"] is not None:
                em.wait("vector", s["tn"], S["intra_mm"])
            em.op("vector", s["vc"],
                  lambda e, S=S: e.tensor_tensor(S["q0t"], S["pS"][:, 0:256],
                                                 S["dt"], ALU.mult))
            S["mask"] = em.op("vector", s["vc"],
                              lambda e, S=S: e.tensor_tensor(
                                  S["q1t"][:, 128:256], S["pS"][:, 384:512],
                                  S["dt"][:, 0:128], ALU.mult))
            S["qt"] = em.op("vector", s["vc"],
                            lambda e, j=j, S=S: e.tensor_tensor(
                                S["qb"][:, j * 256:(j + 1) * 256],
                                S["qb"][:, j * 256:(j + 1) * 256], S["qdec"],
                                ALU.mult))
            em.wait("scalar", s["vc"], S["stt"] if S["stt"] is not None else S["ktr"])
            if S["inter_mm"] is not None:
                em.wait("scalar", s["tn"], S["inter_mm"])
            S["kc"] = em.op("scalar", s["sc"],
                            lambda e, S=S: e.copy(S["kvbf"], S["kvs"]))
        for hh in range(2):
            S = hh_state[hh]
            em.wait("tensor", s["sc"], S["kc"])
            em.wait("tensor", s["vc"], S["qt"])
            if S["evac"] is not None:
                em.wait("tensor", s["sc"], S["evac"])
            em.qop("tensor",
                   lambda e, j=j, S=S: e.matmul(
                       S["pO"][:, 0:256], S["kvbf"],
                       S["qb"][:, j * 256:(j + 1) * 256], start=True, stop=False))
            S["inter_mm"] = None
            for u in range(2):
                qk = S["q0t"] if u == 0 else S["q1t"]
                fn = (lambda e, j=j, u=u, S=S, qk=qk: e.matmul(
                    S["pO"][:, 0:256],
                    c.vbuf[:, (2 * j + u) * 256 + (S["h"] % 2) * 128:
                           (2 * j + u) * 256 + (S["h"] % 2 + 1) * 128],
                    qk, start=False, stop=(u == 1)))
                if u == 0:
                    em.qop("tensor", fn)
                else:
                    em.op("tensor", s["tn"], fn)
            S["inter_mm"] = S["intra_mm"] = c.s["tn"].n
            em.wait("tensor", s["vc"], S["stt"] if S["stt"] is not None else S["ktr"])
            for u in range(2):
                tile = 2 * j + u
                fn = (lambda e, t=tile, u=u, S=S: e.matmul(
                    S["pKV"], S["ktb"][:, t * 128:(t + 1) * 128],
                    c.vbuf[:, t * 256 + (S["h"] % 2) * 128:
                           t * 256 + (S["h"] % 2 + 1) * 128],
                    start=(u == 0), stop=(u == 1)))
                if u == 0:
                    em.qop("tensor", fn)
                else:
                    em.op("tensor", s["tn"], fn)
            S["kv_mm"] = c.s["tn"].n
        for hh in range(2):
            S = hh_state[hh]
            em.wait("scalar", s["tn"], S["intra_mm"])
            S["evac"] = em.op("scalar", s["sc"],
                              lambda e, j=j, S=S: e.copy(
                                  c.arena1[:, S["h"] * 2048 + j * 256:
                                           S["h"] * 2048 + (j + 1) * 256],
                                  S["pO"][:, 0:256]))
            em.wait("vector", s["tn"], S["kv_mm"])
            S["stt"] = em.opw("vector", s["vc"],
                              lambda e, S=S: e.scalar_tensor_tensor(
                                  S["kvs"], S["kvs"], c.CO("bd")[:, S["h"]:S["h"] + 1],
                                  S["pKV"], ALU.mult, ALU.add))


def ph_rms(c):
    em, s = c.em, c.s
    acc_done = None
    for oc in range(NCH):
        em.wait("vector", s["tn"])
        sq = em.op("vector", s["vc"],
                   lambda e, oc=oc: e.tensor_tensor(
                       c.sqz, c.arena1[:, oc * T:(oc + 1) * T],
                       c.arena1[:, oc * T:(oc + 1) * T], ALU.mult))
        em.wait("tensor", s["vc"], sq)
        if acc_done is not None:
            em.wait("tensor", s["vc"], acc_done)
        for tc in range(NCH):
            fn = (lambda e, tc=tc: e.matmul(
                c.ps_var[:, tc:tc + 1], c.sqz[:, tc * 128:(tc + 1) * 128],
                c.ones_col, start=True, stop=True))
            if tc < NCH - 1:
                em.qop("tensor", fn)
            else:
                em.op("tensor", s["tn"], fn)
        em.wait("vector", s["tn"])
        if oc == 0:
            acc_done = em.opw("vector", s["vc"],
                              lambda e: e.tensor_copy(c.CO("var16"), c.ps_var))
        else:
            acc_done = em.opw("vector", s["vc"],
                              lambda e: e.tensor_tensor(c.CO("var16"), c.CO("var16"),
                                                        c.ps_var, ALU.add))

    def vv(fn):
        return em.opw("vector", s["vc"], fn)

    vv(lambda e: e.tensor_scalar(c.CO("var16"), c.CO("var16"), 1.0 / D, EPS_RMS,
                                 ALU.mult, ALU.add))
    rc = vv(lambda e: e.reciprocal(c.CO("var16"), c.CO("var16")))
    em.wait("scalar", s["vc"], rc)
    sq = em.op("scalar", s["sc"],
               lambda e: e.activation(c.CO("a2"), c.CO("var16"), AF.Sqrt))
    em.wait("vector", s["sc"], sq)
    vv(lambda e: e.tensor_tensor(c.CO("a2sq"), c.CO("a2"), c.CO("a2"), ALU.mult))


def ph_gate(c):
    em, s = c.em, c.s
    for oc in range(NCH):
        em.op("gpsimd", s["wo"],
              lambda e, oc=oc: e.dma_start(
                  c.arena3[:, oc * T:(oc + 1) * T],
                  c.dram["wo"][oc * 128:(oc + 1) * 128, :]), 16)
    em.wait("vector", s["wo"])
    for oc in range(NCH):
        em.op("vector", s["vc"],
              lambda e, oc=oc: e.tensor_scalar(
                  c.arena3[:, oc * T:(oc + 1) * T], c.arena3[:, oc * T:(oc + 1) * T],
                  c.CO("nw")[:, oc:oc + 1], None, ALU.mult))
    prev_mm = [None, None]
    for gc in range(NCH):
        wb = c.wg[gc % 2]
        swg = s["wg0"] if gc % 2 == 0 else s["wg1"]
        if prev_mm[gc % 2] is not None:
            em.wait("gpsimd", s["tn"], prev_mm[gc % 2])
        ld = em.op("gpsimd", swg,
                   lambda e, gc=gc, wb=wb: e.dma_start(
                       wb.rearrange("p (cc j) -> p cc j", cc=NCH),
                       c.dram["wg"][:, gc * 128:(gc + 1) * 128].rearrange(
                           "(cc p) j -> p cc j", p=128)), 16)
        wv = wb.rearrange("p (cc j) -> p cc j", cc=NCH)
        for nt in range(4):
            ps = c.pP[nt % 2]
            em.wait("tensor", swg, ld)
            em.wait("tensor", s["sc"])
            for fc in range(NCH):
                fn = (lambda e, ps=ps, fc=fc, nt=nt, w=wv: e.matmul(
                    ps[:, :], w[:, fc, :],
                    c.fmv(c.arena2, fc)[:, nt * 512:(nt + 1) * 512],
                    start=(fc == 0), stop=(fc == NCH - 1)))
                if fc < NCH - 1:
                    em.qop("tensor", fn)
                else:
                    em.op("tensor", s["tn"], fn)
            em.wait("scalar", s["tn"])
            em.wait("scalar", s["vc"])
            sg = em.op("scalar", s["sc"],
                       lambda e, ps=ps, nt=nt: e.activation(
                           c.sig[nt % 2], ps[:, :], AF.Sigmoid))
            em.wait("vector", s["sc"], sg)
            em.op("vector", s["vc"],
                  lambda e, gc=gc, nt=nt: e.tensor_tensor(
                      c.arena1[:, gc * T + nt * 512:gc * T + (nt + 1) * 512],
                      c.arena1[:, gc * T + nt * 512:gc * T + (nt + 1) * 512],
                      c.sig[nt % 2], ALU.mult))
        prev_mm[gc % 2] = c.s["tn"].n


def ph_outproj(c):
    import os
    _lvl = int(os.environ.get("P4_LEVEL", "3"))
    em, s = c.em, c.s
    evac_done = None
    _ntc = int(os.environ.get("P4_NTC", str(NCH)))
    for tc in range(_ntc):
        if evac_done is not None:
            em.wait("gpsimd", s["vc"], evac_done)
        if _lvl >= 1:
            ld = em.op("gpsimd", s["xp4"],
                       lambda e, tc=tc: e.dma_start(
                           c.xchunk_p4, c.dram["x"][tc * 128:(tc + 1) * 128, :]), 16)
        if evac_done is not None:
            em.wait("tensor", s["vc"], evac_done)
            em.wait("tensor", s["sc"])
        for ft in range(4):
            for kc in range(NCH):
                fn = (lambda e, tc=tc, ft=ft, kc=kc: e.matmul(
                    c.pBIG[:, ft * 512:(ft + 1) * 512],
                    c.arena1[:, kc * T + tc * 128:kc * T + (tc + 1) * 128],
                    c.arena3[:, kc * T + ft * 512:kc * T + (ft + 1) * 512],
                    start=(kc == 0), stop=(kc == NCH - 1)))
                if ft == 3 and kc == NCH - 1:
                    em.op("tensor", s["tn"], fn)
                else:
                    em.qop("tensor", fn)
        mm = c.s["tn"].n
        em.wait("scalar", s["tn"], mm)
        cp1 = em.op("scalar", s["sc"],
                    lambda e: e.activation(c.sq_p4_bf, c.pBIG[:, :], AF.Copy,
                                           accum_out=c.CO("s1")))
        em.wait("scalar", s["sc"], cp1)
        sq = em.op("scalar", s["sc"],
                   lambda e: e.activation(c.sq_p4_bf, c.pBIG[:, :], AF.Square,
                                          accum_out=c.CO("s2")))
        em.wait("vector", s["sc"], sq)

        def vv(fn):
            return em.opw("vector", s["vc"], fn)

        vv(lambda e: e.tensor_scalar(c.CO("mneg"), c.CO("s1"), -1.0 / D, None, ALU.mult))
        vv(lambda e: e.tensor_tensor(c.CO("mm"), c.CO("mneg"), c.CO("mneg"), ALU.mult))
        vv(lambda e: e.tensor_scalar(c.CO("tv"), c.CO("s2"), 1.0 / D, None, ALU.mult))
        vv(lambda e: e.tensor_tensor(c.CO("tv"), c.CO("tv"), c.CO("mm"), ALU.subtract))
        vv(lambda e, tc=tc: e.scalar_tensor_tensor(
            c.CO("tv"), c.CO("tv"), c.CO("a2sq")[:, tc:tc + 1], c.CO("epsln"),
            ALU.mult, ALU.add))
        rc = vv(lambda e: e.reciprocal(c.CO("rc"), c.CO("tv")))
        em.wait("scalar", s["vc"], rc)
        sq2 = em.op("scalar", s["sc"],
                    lambda e: e.activation(c.CO("acol"), c.CO("rc"), AF.Sqrt))
        em.wait("vector", s["sc"], sq2)
        vv(lambda e, tc=tc: e.tensor_tensor(
            c.CO("c1"), c.CO("a2")[:, tc:tc + 1], c.CO("acol"), ALU.mult))
        vv(lambda e: e.tensor_tensor(c.CO("bcol"), c.CO("mneg"), c.CO("c1"),
                                     ALU.mult))
        if _lvl >= 1:
            em.wait("vector", s["xp4"], ld)
        vv(lambda e: e.tensor_scalar(c.xchunk_p4, c.xchunk_p4, c.CO("bcol"), None,
                                     ALU.add))
        evac_done = vv(lambda e, tc=tc: e.scalar_tensor_tensor(
            c.arena2[:, tc * 2048:(tc + 1) * 2048], c.pBIG[:, :], c.CO("c1"),
            c.xchunk_p4, ALU.mult, ALU.add))


def ph_ffn(c):
    em, s = c.em, c.s
    for kc in range(NCH):
        em.op("gpsimd", s["wf"],
              lambda e, kc=kc: e.dma_start(
                  c.arena3[:, kc * T:(kc + 1) * T],
                  c.dram["wf"][kc * 128:(kc + 1) * 128, :]), 16)
    wf_ld = c.s["wf"].n
    em.op("gpsimd", s["cst"],
          lambda e: e.dma_start(c.wsb, c.dram["ws"][None, :].broadcast_to((128, D))), 16)
    em.op("gpsimd", s["cst"],
          lambda e: e.dma_start(c.biasb[:, :],
                                c.dram["fb"][None, :].broadcast_to((128, D))), 16)

    qxv_q = [q.rearrange("p (cc t) -> p cc t", cc=NCH) for q in c.qxf_q]
    tp_done = [None, None]
    store_done = [None]
    q_mm_done = [None, None]
    pp_evac = [None, None]

    def quant_tc(tc, qq):
        ti = tc % 4
        x2c = c.arena2[:, tc * 2048:(tc + 1) * 2048]
        em.wait("scalar", s["vc"])
        sq = em.op("scalar", s["sc"],
                   lambda e, x2c=x2c: e.activation(c.Xbf[:, 0:2048], x2c, AF.Square,
                                                   accum_out=c.CO("s2")))
        em.opw("vector", s["vc"],
               lambda e, x2c=x2c: e.reduce_sum(c.CO("s1"), x2c, axis=AX.X))
        em.opw("vector", s["vc"],
               lambda e, x2c=x2c: e.tensor_reduce(c.CO("mx"), x2c, AX.X, ALU.max))
        em.opw("vector", s["vc"],
               lambda e, x2c=x2c: e.tensor_reduce(c.CO("mn"), x2c, AX.X, ALU.min))
        em.wait("vector", s["sc"], sq)

        def vv(fn):
            return em.opw("vector", s["vc"], fn)

        vv(lambda e: e.tensor_scalar(c.CO("mneg"), c.CO("s1"), -1.0 / D, None,
                                     ALU.mult))
        vv(lambda e: e.tensor_tensor(c.CO("mm"), c.CO("mneg"), c.CO("mneg"), ALU.mult))
        vv(lambda e: e.tensor_scalar(c.CO("tv"), c.CO("s2"), 1.0 / D, EPS_LN,
                                     ALU.mult, ALU.add))
        vv(lambda e: e.tensor_tensor(c.CO("tv"), c.CO("tv"), c.CO("mm"), ALU.subtract))
        rc = vv(lambda e: e.reciprocal(c.CO("rc"), c.CO("tv")))
        em.wait("scalar", s["vc"], rc)
        sq2 = em.op("scalar", s["sc"],
                    lambda e: e.activation(c.CO("acol"), c.CO("rc"), AF.Sqrt))
        em.wait("vector", s["sc"], sq2)
        vv(lambda e: e.tensor_scalar(c.CO("t1"), c.CO("mx"), c.CO("mneg"), None,
                                     ALU.add))
        vv(lambda e: e.tensor_scalar(c.CO("t2"), c.CO("mn"), c.CO("mneg"), -1.0,
                                     ALU.add, ALU.mult))
        vv(lambda e: e.tensor_tensor(c.CO("t1"), c.CO("t1"), c.CO("t2"), ALU.max))
        vv(lambda e: e.tensor_tensor(c.CO("t1"), c.CO("t1"), c.CO("acol"), ALU.mult))
        vv(lambda e: e.tensor_scalar(c.CO("scol"), c.CO("t1"), EPS_Q, 1.0 / 127.0,
                                     ALU.max, ALU.mult))
        vv(lambda e: e.reciprocal(c.CO("invs"), c.CO("scol")))
        vv(lambda e: e.tensor_tensor(c.CO("c1"), c.CO("acol"), c.CO("invs"), ALU.mult))
        vv(lambda e: e.tensor_tensor(c.CO("c2"), c.CO("mneg"), c.CO("c1"), ALU.mult))
        vv(lambda e, tc=tc: e.tensor_copy(c.CO("insc")[:, tc:tc + 1], c.CO("scol")))
        bi = tc % 2
        vv(lambda e, x2c=x2c: e.tensor_scalar(c.X32, x2c, c.CO("c1"), c.CO("c2"),
                                              ALU.mult, ALU.add))
        vv(lambda e: e.tensor_scalar(c.X32, c.X32, MAGIC, MAGIC, ALU.add,
                                     ALU.subtract))
        stq = s["t0"] if bi == 0 else s["t1"]
        if tp_done[bi] is not None:
            em.wait("vector", stq, tp_done[bi])
        qd = em.opw("vector", s["vc"],
                    lambda e, bi=bi: e.tensor_scalar(c.qf[bi], c.X32, 127.0, -128.0,
                                                     ALU.min, ALU.max))
        em.wait("sync", s["vc"], qd)
        if q_mm_done[qq] is not None:
            em.wait("sync", s["tn"], q_mm_done[qq])
        tp_done[bi] = em.op("sync", stq,
                            lambda e, bi=bi, ti=ti, qq=qq: e.dma_start(
                                qxv_q[qq][:, :, ti * 128:(ti + 1) * 128], c.qf[bi],
                                transpose=True), 16)

    def mm_tc(tc, qq):
        ti = tc % 4
        em.wait("tensor", s["wf"], wf_ld)
        em.wait("tensor", s["t0"])
        em.wait("tensor", s["t1"])
        for ft in range(4):
            pp = ft % 2
            if pp_evac[pp] is not None:
                em.wait("tensor", s["vc"], pp_evac[pp])
            for kc in range(NCH):
                fn = (lambda e, ti=ti, ft=ft, kc=kc, qq=qq, pp=pp: e.matmul(
                    c.pP[pp][:, :],
                    c.qxf_q[qq][:, kc * 512 + ti * 128:kc * 512 + (ti + 1) * 128],
                    c.arena3[:, kc * T + ft * 512:kc * T + (ft + 1) * 512],
                    start=(kc == 0), stop=(kc == NCH - 1)))
                if kc == NCH - 1:
                    em.op("tensor", s["tn"], fn)
                else:
                    em.qop("tensor", fn)
            mmn = c.s["tn"].n
            em.wait("vector", s["tn"], mmn)
            em.wait("vector", s["cst"])
            em.opw("vector", s["vc"],
                   lambda e, ft=ft, tc=tc, pp=pp: e.scalar_tensor_tensor(
                       c.tmp32, c.pP[pp][:, :], c.CO("insc")[:, tc:tc + 1],
                       c.wsb[:, ft * 512:(ft + 1) * 512], ALU.mult, ALU.mult))
            pp_evac[pp] = em.opw("vector", s["vc"],
                                 lambda e, ft=ft: e.tensor_tensor(
                                     c.ffn16[:, ft * 512:(ft + 1) * 512], c.tmp32,
                                     c.biasb[:, ft * 512:(ft + 1) * 512], ALU.add))

    def epi_tc(tc):
        ep = c.s["vc"].n
        em.wait("scalar", s["vc"], ep)
        if store_done[0] is not None:
            em.wait("scalar", s["st"], store_done[0])
            em.wait("vector", s["st"], store_done[0])
        sq = em.op("scalar", s["sc"],
                   lambda e: e.activation(c.outbuf, c.ffn16, AF.Square,
                                          accum_out=c.CO("s2")))
        em.opw("vector", s["vc"],
               lambda e: e.reduce_sum(c.CO("s1"), c.ffn16, axis=AX.X))

        def vv(fn):
            return em.opw("vector", s["vc"], fn)

        vv(lambda e: e.tensor_scalar(c.CO("mneg"), c.CO("s1"), -1.0 / D, None,
                                     ALU.mult))
        vv(lambda e: e.tensor_tensor(c.CO("mm"), c.CO("mneg"), c.CO("mneg"), ALU.mult))
        em.wait("vector", s["sc"], sq)
        vv(lambda e: e.tensor_scalar(c.CO("tv"), c.CO("s2"), 1.0 / D, EPS_LN,
                                     ALU.mult, ALU.add))
        vv(lambda e: e.tensor_tensor(c.CO("tv"), c.CO("tv"), c.CO("mm"), ALU.subtract))
        rc = vv(lambda e: e.reciprocal(c.CO("rc"), c.CO("tv")))
        em.wait("scalar", s["vc"], rc)
        sq2 = em.op("scalar", s["sc"],
                    lambda e: e.activation(c.CO("acol"), c.CO("rc"), AF.Sqrt))
        em.wait("vector", s["sc"], sq2)
        vv(lambda e: e.tensor_tensor(c.CO("bcol"), c.CO("mneg"), c.CO("acol"),
                                     ALU.mult))
        vv(lambda e: e.tensor_scalar(c.outbuf, c.ffn16, c.CO("acol"), c.CO("bcol"),
                                     ALU.mult, ALU.add))
        od = vv(lambda e, tc=tc: e.tensor_tensor(
            c.outbuf, c.outbuf, c.arena2[:, tc * 2048:(tc + 1) * 2048], ALU.add))
        em.wait("sync", s["vc"], od)
        store_done[0] = em.op("sync", s["st"],
                              lambda e, tc=tc: e.dma_start(
                                  c.dram["out"][tc * 128:(tc + 1) * 128, :],
                                  c.outbuf), 16)

    for ti in range(4):
        quant_tc(ti, 0)
    for hf in range(4):
        qq = hf % 2
        for ti in range(4):
            tc = hf * 4 + ti
            mm_tc(tc, qq)
            if ti == 3:
                q_mm_done[qq] = c.s["tn"].n
            if hf < 3:
                quant_tc((hf + 1) * 4 + ti, (hf + 1) % 2)
            epi_tc(tc)


def _host_prep(inputs):
    x = np.asarray(inputs["L0_attn_hidden_states"], dtype=np.float32)
    slopes = np.ascontiguousarray(
        np.asarray(inputs["L0_attn_slope_rate"], dtype=np.float32).reshape(H))
    wqkv = np.asarray(inputs["L0_attn_w_qkv"], dtype=np.float32)
    r = wqkv.reshape(H, 3, HD, D)
    wqT = np.ascontiguousarray(r[:, 0].transpose(2, 0, 1).reshape(D, D))
    wkT = np.ascontiguousarray(r[:, 1].transpose(2, 0, 1).reshape(D, D))
    wvT = np.ascontiguousarray(r[:, 2].transpose(2, 0, 1).reshape(D, D))
    wgT = np.ascontiguousarray(np.asarray(inputs["L0_attn_w_gate"], np.float32).T)
    woT = np.ascontiguousarray(np.asarray(inputs["L0_attn_w_out"], np.float32).T)
    wfT = np.ascontiguousarray(np.asarray(inputs["L0_ffn_weight"]).T)
    normw = np.ascontiguousarray(np.asarray(inputs["L0_attn_norm_w"], np.float32))
    wscale = np.ascontiguousarray(
        np.asarray(inputs["L0_ffn_w_scale"], np.float32).reshape(D))
    fbias = np.ascontiguousarray(np.asarray(inputs["L0_ffn_bias"], np.float32))
    zeros = np.zeros((T, D), np.float32)
    in_maps = []
    for core in range(8):
        b, hf = core // 2, core % 2
        x_own = np.ascontiguousarray(x[b, hf * T:(hf + 1) * T])
        x_pre = np.ascontiguousarray(x[b, 0:T]) if hf == 1 else zeros
        in_maps.append(dict(x_own=x_own, x_pre=x_pre, wqT=wqT, wkT=wkT, wvT=wvT,
                            wgT=wgT, woT=woT, wfT=wfT, slopes=slopes, normw=normw,
                            wscale=wscale, fbias=fbias))
    return x, in_maps


def kernel(**inputs):
    x, in_maps = _host_prep(inputs)
    nc = build_nc()
    res = run_bass_kernel_spmd(nc, in_maps, core_ids=list(range(8)))
    out = np.empty((4, 4096, D), np.float32)
    for core in range(8):
        b, hf = core // 2, core % 2
        out[b, hf * T:(hf + 1) * T, :] = res.results[core]["out"]
    return out


# version 06
# speedup vs baseline: 1.1844x; 1.0903x over previous
import os
import numpy as np
from contextlib import ExitStack

import concourse.bass as bass
import concourse.mybir as mybir
from concourse.bass_utils import run_bass_kernel_spmd

F32 = mybir.dt.float32
BF16 = mybir.dt.bfloat16
F16 = mybir.dt.float16
I8 = mybir.dt.int8
AF = mybir.ActivationFunctionType
ALU = mybir.AluOpType
AX = mybir.AxisListType

T = 2048
D = 2048
H = 16
HD = 128
NCH = 16
NBLK = 8
MAGIC = 12582912.0
EPS_LN = 1e-5
EPS_RMS = 1e-6
EPS_Q = 1e-7


class Sem:
    def __init__(self, ctx, nc, name):
        self.h = ctx.enter_context(nc.semaphore(name))
        self.n = 0

    def bump(self, k):
        self.n += k
        return self.n


class Em:

    ENGS = ("tensor", "vector", "scalar", "gpsimd", "sync")

    def __init__(self):
        self.q = {e: [] for e in self.ENGS}
        self.lastwait = {e: {} for e in self.ENGS}
        self.trace = {e: [] for e in self.ENGS}

    def wait(self, eng, sem, cnt=None):
        cnt = sem.n if cnt is None else cnt
        if cnt is None or cnt <= 0:
            return
        lw = self.lastwait[eng]
        if lw.get(id(sem), -1) >= cnt:
            return
        lw[id(sem)] = cnt
        self.trace[eng].append(("wait", sem.h.name, cnt))
        self.q[eng].append(lambda e, s=sem.h, c=cnt: e.wait_ge(s, c))

    def qop(self, eng, fn):
        self.q[eng].append(fn)

    def op(self, eng, sem, fn, k=1):
        v = sem.bump(k)
        import inspect
        try:
            srcl = inspect.getsource(fn).strip().split("\n")[0][:80]
        except Exception:
            srcl = "?"
        self.trace[eng].append(("op", sem.h.name, v, srcl))
        self.q[eng].append(lambda e, f=fn, s=sem.h, kk=k: f(e).then_inc(s, kk))
        return v

    def opw(self, eng, sem, fn, k=1):
        v = self.op(eng, sem, fn, k)
        self.wait(eng, sem, v)
        return v

    def raw(self, eng, fn):
        self.q[eng].append(fn)

    def barrier(self, sems):
        for eng in self.ENGS:
            for s in sems:
                self.wait(eng, s)


class C:
    pass


def build_nc(taps=(), stages=99, replica_groups=None):
    taps = set(taps)
    nc = bass.Bass("TRN2", target_bir_lowering=False)
    c = C()
    c.nc = nc
    c.taps = taps
    c.replica_groups = replica_groups or [[0, 1], [2, 3], [4, 5], [6, 7]]

    dx = nc.dram_tensor("x_own", [T, D], F32, kind="ExternalInput")
    dmsk = nc.dram_tensor("msk", [2], F32, kind="ExternalInput")
    dcci = nc.dram_tensor("cc_in", [128, 256], F32, kind="Internal")
    dcco = nc.dram_tensor("cc_out", [256, 256], F32, kind="Internal")
    dwq = nc.dram_tensor("wqT", [D, D], F32, kind="ExternalInput")
    dwk = nc.dram_tensor("wkT", [D, D], F32, kind="ExternalInput")
    dwv = nc.dram_tensor("wvT", [D, D], F32, kind="ExternalInput")
    dwg = nc.dram_tensor("wgT", [D, D], F32, kind="ExternalInput")
    dwo = nc.dram_tensor("woT", [D, D], F32, kind="ExternalInput")
    dwf = nc.dram_tensor("wfT", [D, D], I8, kind="ExternalInput")
    dsl = nc.dram_tensor("slopes", [H], F32, kind="ExternalInput")
    dnw = nc.dram_tensor("normw", [D], F32, kind="ExternalInput")
    dws = nc.dram_tensor("wscale", [D], F32, kind="ExternalInput")
    dfb = nc.dram_tensor("fbias", [D], F32, kind="ExternalInput")
    dout = nc.dram_tensor("out", [T, D], F32, kind="ExternalOutput")
    c.dram = dict(x=dx, msk=dmsk, cci=dcci, cco=dcco, wq=dwq, wk=dwk, wv=dwv,
                  wg=dwg, wo=dwo, wf=dwf, sl=dsl, nw=dnw, ws=dws, fb=dfb, out=dout)
    c.tap_dram = {}
    for name, shape, dt in [
        ("tap_xnfpre", [128, NCH * T], BF16), ("tap_xnf", [128, NCH * T], BF16),
        ("tap_kv", [128, H * HD], F32), ("tap_q0", [128, T], BF16),
        ("tap_k0", [128, T], BF16), ("tap_v0", [128, 4096], BF16),
        ("tap_outf", [128, NCH * T], BF16), ("tap_gout", [128, NCH * T], BF16),
        ("tap_x2", [128, NCH * T], BF16), ("tap_qxf", [128, NCH * 1024], BF16),
        ("tap_insc", [128, 16], F32),
        ("tap_cols", [128, 16 * 8], F32),
    ]:
        if name in taps:
            c.tap_dram[name] = nc.dram_tensor(name, shape, dt, kind="ExternalOutput")

    big = ExitStack()
    e = big.enter_context

    em = Em()
    c.em = em
    names = ["wl", "tp", "st", "tn", "vc", "sc", "gp",
             "x0", "x1", "t0", "t1", "wq", "wk", "wv", "wg0", "wg1",
             "wo", "wf", "cst", "xp4", "tp2"]
    c.s = {nm: Sem(big, nc, "s_" + nm) for nm in names}
    c.sems = list(c.s.values())

    arena1 = e(nc.sbuf_tensor("arena1", [128, 32768], BF16))
    arena2 = e(nc.sbuf_tensor("arena2", [128, 32768], BF16))
    arena3 = e(nc.sbuf_tensor("arena3", [128, 32768], BF16))
    c.arena1, c.arena2, c.arena3 = arena1, arena2, arena3

    z = {}
    off = 0

    def zalloc(name, n):
        nonlocal off
        z[name] = (off, off + n)
        off += n

    zalloc("qdec", H * 256)
    zalloc("dt", H * 256)
    zalloc("qbuf", T)
    zalloc("kbuf", T)
    zalloc("ktbuf", T)
    zalloc("wq", T)
    zalloc("wk", T)
    zalloc("qkdt0", 256)
    zalloc("qkdt1", 256)
    zalloc("kv_bf", 128)
    zalloc("kv_bf2", 128)
    zalloc("qkdt0b", 256)
    zalloc("qkdt1b", 256)
    zalloc("kbuf2", T)
    zalloc("shared", 10240)
    assert off <= 32768, off
    c.z = z

    def Z(name):
        lo, hi = z[name]
        return arena3[:, lo:hi]

    c.Z = Z
    sh = z["shared"][0]
    c.xn_tok = [arena3[:, sh:sh + 2048], arena3[:, sh + 2048:sh + 4096]]
    c.xchunk = [arena3[:, sh + 4096:sh + 6144], arena3[:, sh + 6144:sh + 8192]]
    c.sqz = arena3[:, sh + 8192:sh + 10240]
    c.vbuf = arena3[:, sh:sh + 4096]
    c.wv = arena3[:, sh + 4096:sh + 8192]

    kvst = e(nc.sbuf_tensor("kvst", [128, 2048], F32))
    c.kvst = kvst
    cols = e(nc.sbuf_tensor("cols", [128, 224], F32))
    c.cols = cols
    co = {}
    coff = 0

    def calloc(name, n):
        nonlocal coff
        co[name] = (coff, coff + n)
        coff += n

    for nm, n in [("slneg", 16), ("nw", 16), ("bd", 16), ("kdec", 32),
                  ("a2", 16), ("a2sq", 16), ("insc", 16), ("c256", 1),
                  ("s1", 1), ("s2", 2), ("mneg", 1), ("mm", 1), ("tv", 1),
                  ("rc", 1), ("acol", 1), ("bcol", 1), ("mx", 1), ("mn", 1),
                  ("t1", 1), ("t2", 1), ("scol", 1), ("invs", 1), ("c1", 1),
                  ("c2", 1), ("epsln", 1), ("var16", 16),
                  ("s1b", 1), ("s2b", 2), ("mnegb", 1), ("mmb", 1), ("tvb", 1),
                  ("rcb", 1), ("acolb", 1), ("bcolb", 1), ("m0", 1), ("m1", 1)]:
        calloc(nm, n)
    assert coff <= 224

    def CO(name):
        lo, hi = co[name]
        return cols[:, lo:lo + 1] if hi - lo == 2 else cols[:, lo:hi]

    c.CO = CO

    consts = e(nc.sbuf_tensor("consts", [128, 288], BF16))
    c.ones_col = consts[:, 0:1]
    c.onesq = consts[:, 32:160]
    c.ident = consts[:, 160:288]
    sigt = e(nc.sbuf_tensor("sigt", [128, 1024], BF16))
    c.sig = [sigt[:, 0:512], sigt[:, 512:1024]]
    biasb = e(nc.sbuf_tensor("biasb", [128, 2048], BF16))
    c.biasb = biasb
    c.qbuf2 = biasb[:, :]
    c.ktbuf2 = c.sqz
    kb_lo, _ = z["kbuf"]
    a3f = arena3.bitcast(F32)
    c.S0 = a3f[:, kb_lo // 2:kb_lo // 2 + 256]
    c.S1 = a3f[:, kb_lo // 2 + 256:kb_lo // 2 + 512]
    kb2_lo, _ = z["kbuf2"]
    c.Seff = a3f[:, kb2_lo // 2:kb2_lo // 2 + 256]
    c.Sbf = arena3[:, kb2_lo + 1024:kb2_lo + 1024 + 128]

    a1f = arena1.bitcast(F32)
    c.iota_m1 = a1f[:, 0:256]
    c.iota_kd = a1f[:, 256:258]
    c.idxT0 = a1f[:, 512:768]
    c.slpos = a1f[:, 768:784]
    c.dt_scr = a1f[:, 1024:1280]

    kvb = kvst.bitcast(BF16)
    c.wg = [kvb[:, 0:2048], kvb[:, 2048:4096]]
    c.xchunk_p4 = kvb[:, 0:2048]
    c.sq_p4_bf = kvb[:, 2048:4096]
    c.outbuf = kvst[:, 0:2048]

    c.qxf_q = [arena1[:, 0:8192], arena1[:, 8192:16384]]
    c.X32 = a1f[:, 8192:10240]
    c.Xbf = arena1[:, 16384:20480]
    c.qf = [arena1[:, 20480:22528], arena1[:, 22528:24576]]
    c.wsb = a1f[:, 12288:14336]
    c.ffn16 = arena1.bitcast(F16)[:, 28672:30720]
    c.tmp32 = a1f[:, 15360:15872]

    c.pP = [e(nc.psum_tensor("pP0", [128, 512], F32)),
            e(nc.psum_tensor("pP1", [128, 512], F32))]
    c.pBIG = e(nc.psum_tensor("pBIG", [128, 2048], F32))
    c.pS = [c.pBIG[:, 0:512], c.pBIG[:, 512:1024]]
    c.pO = [c.pBIG[:, 1024:1536], c.pBIG[:, 1536:2048]]
    c.pAUX = e(nc.psum_tensor("pAUX", [128, 512], F32))
    c.pAUX2 = e(nc.psum_tensor("pAUX2", [128, 512], F32))
    c.ps_kv = c.pAUX[:, 0:128]
    c.ps_kv2 = c.pAUX2[:, 0:128]
    c.ps_var = c.pAUX2[:, 256:272]

    def fmv(arena, fc):
        return arena[:, fc * T:(fc + 1) * T]

    c.fmv = fmv

    _skip = set(int(x) for x in os.environ.get("SKIP_STAGES", "").split(",") if x)

    def stage(i):
        return stages >= i and i not in _skip

    ph_consts(c)
    em.barrier(c.sems)
    if stage(3):
        ph_ln_transpose(c, dx, arena2, "tap_xnf")
        em.barrier(c.sems)
    if stage(4):
        ph_attn(c)
        em.barrier(c.sems)
        if "tap_outf" in taps:
            tap_store(c, c.tap_dram["tap_outf"], arena1[:, :])
            em.barrier(c.sems)
    if stage(5):
        ph_rms(c)
        em.barrier(c.sems)
    if stage(6):
        ph_gate(c)
        em.barrier(c.sems)
        if "tap_gout" in taps:
            tap_store(c, c.tap_dram["tap_gout"], arena1[:, :])
            em.barrier(c.sems)
    if stage(7):
        ph_outproj(c)
        em.barrier(c.sems)
        if "tap_x2" in taps:
            tap_store(c, c.tap_dram["tap_x2"], arena2[:, :])
            em.barrier(c.sems)
    if stage(8):
        ph_ffn(c)
    em.raw("sync", lambda e2, s=c.s["st"].h, v=c.s["st"].n: e2.wait_ge(s, v))

    with nc.Block() as block:
        @block.tensor
        def _(t):
            for f in em.q["tensor"]:
                f(t)

        @block.vector
        def _(v):
            for f in em.q["vector"]:
                f(v)

        @block.scalar
        def _(s2):
            for f in em.q["scalar"]:
                f(s2)

        @block.gpsimd
        def _(g):
            for f in em.q["gpsimd"]:
                f(g)

        @block.sync
        def _(sy):
            for f in em.q["sync"]:
                f(sy)

    big.close()
    return nc


def tap_store(c, dram, sbuf_ap):
    em, s = c.em, c.s
    n = sbuf_ap.shape[1]
    CH = 4096
    for i in range(0, n, CH):
        w = min(CH, n - i)
        em.op("sync", s["st"],
              lambda e, d=dram[:, i:i + w], src=sbuf_ap[:, i:i + w]: e.dma_start(d, src),
              16)
    em.wait("sync", s["st"])


def ph_consts(c):
    em, s = c.em, c.s
    _lite = int(os.environ.get("CONSTS_LITE", "0"))

    def g(fn):
        return em.op("gpsimd", s["gp"], fn)

    g(lambda e: e.memset(c.ones_col, 1.0))
    g(lambda e: e.memset(c.onesq, 1.0))
    em.wait("gpsimd", s["gp"])
    g(lambda e: e.affine_select(c.ident, c.onesq, [[-1, 128]], ALU.is_equal, 0.0,
                                base=0, channel_multiplier=1))
    g(lambda e: e.iota(c.iota_m1, [[1, 256]], base=1, channel_multiplier=0,
                       allow_small_or_imprecise_dtypes=True))
    g(lambda e: e.iota(c.iota_kd, [[-128, 2]], base=255, channel_multiplier=-1,
                       allow_small_or_imprecise_dtypes=True))
    g(lambda e: e.iota(c.idxT0, [[1, 256]], base=0, channel_multiplier=-1,
                       allow_small_or_imprecise_dtypes=True))
    g(lambda e: e.memset(c.CO("c256"), 256.0))
    g(lambda e: e.memset(c.CO("epsln"), EPS_LN))
    g(lambda e: e.memset(c.kvst[:, :], 0.0))
    g(lambda e: e.memset(c.Z("qkdt1"), 0.0))
    g(lambda e: e.memset(c.Z("qkdt1b"), 0.0))

    if _lite >= 3:
        return
    em.op("sync", s["st"],
          lambda e: e.dma_start(c.slpos, c.dram["sl"][None, :].broadcast_to((128, 16))),
          16)
    em.op("gpsimd", s["cst"],
          lambda e: e.dma_start(c.CO("m0"), c.dram["msk"][None, 0:1].broadcast_to((128, 1))), 16)
    em.op("gpsimd", s["cst"],
          lambda e: e.dma_start(c.CO("m1"), c.dram["msk"][None, 1:2].broadcast_to((128, 1))), 16)
    em.op("sync", s["st"],
          lambda e: e.dma_start(c.CO("nw"),
                                c.dram["nw"].rearrange("(cc p) -> p cc", p=128),
                                allow_slow_non_contiguous=True), 16)
    em.wait("vector", s["st"])
    em.op("vector", s["vc"],
          lambda e: e.tensor_scalar(c.CO("slneg"), c.slpos, -1.0, None, ALU.mult))

    em.wait("scalar", s["vc"])
    em.wait("scalar", s["gp"])
    em.wait("vector", s["gp"])
    if _lite >= 2:
        return
    for h in range(H):
        sl = c.CO("slneg")[:, h:h + 1]
        em.op("scalar", s["sc"],
              lambda e, h=h, sl=sl: e.activation(
                  c.Z("qdec")[:, h * 256:(h + 1) * 256], c.iota_m1, AF.Exp, scale=sl))
        em.op("scalar", s["sc"],
              lambda e, h=h, sl=sl: e.activation(
                  c.CO("kdec")[:, 2 * h:2 * h + 2], c.iota_kd, AF.Exp, scale=sl))
        em.op("scalar", s["sc"],
              lambda e, h=h, sl=sl: e.activation(
                  c.CO("bd")[:, h:h + 1], c.CO("c256"), AF.Exp, scale=sl))
        em.wait("vector", s["sc"])
        em.wait("vector", s["gp"])
        dm = em.op("vector", s["vc"],
                   lambda e, sl=sl: e.tensor_scalar(c.dt_scr, c.idxT0, sl, None, ALU.mult))
        em.wait("gpsimd", s["vc"], dm)
        af = em.op("gpsimd", s["gp"],
                   lambda e: e.affine_select(c.dt_scr, c.dt_scr, [[1, 256]], ALU.is_ge,
                                             -20000.0, base=0, channel_multiplier=-1))
        em.wait("scalar", s["gp"], af)
        em.op("scalar", s["sc"],
              lambda e, h=h: e.activation(
                  c.Z("dt")[:, h * 256:(h + 1) * 256], c.dt_scr, AF.Exp))


def ph_ln_transpose(c, src_dram, dst_arena, tapname):
    em, s = c.em, c.s
    import os
    _mode = os.environ.get("LN_MODE", "")
    _ser = os.environ.get("LN_SERIAL", "0") == "1"
    stats_done = [None, None]
    tp_done = [None, None]
    last_tv = [None, None]
    dst_v = dst_arena[:, :].rearrange("p (cc t) -> p cc t", cc=NCH)
    for tc in range(NCH):
        bi = tc % 2
        buf = c.xchunk[bi]
        if stats_done[bi] is not None:
            em.wait("gpsimd", s["vc"], stats_done[bi][0])
            em.wait("gpsimd", s["sc"], stats_done[bi][1])
        if _mode == "s3" and stats_done[1 - bi] is not None:
            em.wait("gpsimd", s["vc"], stats_done[1 - bi][0])
            em.wait("gpsimd", s["sc"], stats_done[1 - bi][1])
        sx = s["x0"] if bi == 0 else s["x1"]
        ld = em.op("gpsimd", sx,
                   lambda e, b=buf, d=src_dram, i=tc: e.dma_start(
                       b, d[i * 128:(i + 1) * 128, :]), 16)
        sfx = "b" if bi else ""
        S1, S2, MNEG, MM, TV, RC, ACOL, BCOL = (c.CO("s1" + sfx), c.CO("s2" + sfx),
                                                c.CO("mneg" + sfx), c.CO("mm" + sfx),
                                                c.CO("tv" + sfx), c.CO("rc" + sfx),
                                                c.CO("acol" + sfx), c.CO("bcol" + sfx))
        em.wait("scalar", sx, ld)
        if last_tv[bi] is not None:
            em.wait("scalar", s["vc"], last_tv[bi])
        sq = em.op("scalar", s["sc"],
                   lambda e, b=buf, S2=S2: e.activation(c.sqz, b, AF.Square,
                                                        accum_out=S2))
        em.wait("vector", sx, ld)
        em.opw("vector", s["vc"],
               lambda e, b=buf, S1=S1: e.reduce_sum(S1, b, axis=AX.X))
        em.wait("vector", s["sc"], sq)

        def vv(fn):
            return em.opw("vector", s["vc"], fn)

        vv(lambda e, S1=S1, MNEG=MNEG: e.tensor_scalar(MNEG, S1, -1.0 / D, None, ALU.mult))
        vv(lambda e, MNEG=MNEG, MM=MM: e.tensor_tensor(MM, MNEG, MNEG, ALU.mult))
        vv(lambda e, S2=S2, TV=TV: e.tensor_scalar(TV, S2, 1.0 / D, EPS_LN,
                                                   ALU.mult, ALU.add))
        last_tv[bi] = vv(lambda e, TV=TV, MM=MM: e.tensor_tensor(TV, TV, MM,
                                                                 ALU.subtract))
        rc = vv(lambda e, TV=TV, RC=RC: e.reciprocal(RC, TV))
        em.wait("scalar", s["vc"], rc)
        sq2 = em.op("scalar", s["sc"],
                    lambda e, RC=RC, ACOL=ACOL: e.activation(ACOL, RC, AF.Sqrt))
        em.wait("vector", s["sc"], sq2)
        bc = vv(lambda e, MNEG=MNEG, ACOL=ACOL, BCOL=BCOL: e.tensor_tensor(
            BCOL, MNEG, ACOL, ALU.mult))
        if _mode == "s1":
            em.barrier(c.sems)
        xb = c.xn_tok[bi]
        stp = s["t0"] if bi == 0 else s["t1"]
        if tp_done[bi] is not None:
            em.wait("vector", stp, tp_done[bi])
        ap = vv(lambda e, b=buf, o=xb, ACOL=ACOL, BCOL=BCOL: e.tensor_scalar(
            o, b, ACOL, BCOL, ALU.mult, ALU.add))
        stats_done[bi] = (ap, sq)
        if "tap_cols" in c.taps:
            em.wait("sync", s["vc"], ap)
            for ci, col in enumerate([S1, S2, MNEG, TV, RC, ACOL, BCOL]):
                em.op("sync", s["st"],
                      lambda e, tc=tc, ci=ci, col=col: e.dma_start(
                          c.tap_dram["tap_cols"][:, tc * 8 + ci:tc * 8 + ci + 1], col,
                          allow_slow_non_contiguous=True), 16)
        em.wait("sync", s["vc"], ap)
        tp_done[bi] = em.op("sync", stp,
                            lambda e, o=dst_v[:, :, tc * 128:(tc + 1) * 128], i=xb:
                            e.dma_start(o, i, transpose=True), 16)
    if tapname in c.taps and _mode != "notp":
        em.barrier(c.sems)
        tap_store(c, c.tap_dram[tapname], dst_arena[:, :])


def _load_w_slice(c, dram, col0, ncols, dst, sem, war_tn=None):
    em, s = c.em, c.s
    if war_tn is not None:
        em.wait("gpsimd", s["tn"], war_tn)
    src = dram[:, col0:col0 + ncols].rearrange("(cc p) j -> p cc j", p=128)
    dstv = dst.rearrange("p (cc j) -> p cc j", cc=NCH)
    return em.op("gpsimd", sem, lambda e: e.dma_start(dstv, src), 16)


def _vproj(c, xnF_arena, g, st):
    em, s = c.em, c.s
    wv_ld = _load_w_slice(c, c.dram["wv"], g * 256, 256, c.wv, s["wv"], st.get("wv_war"))
    wvv = c.wv.rearrange("p (cc j) -> p cc j", cc=NCH)
    em.wait("scalar", s["tn"])
    for tt in range(NCH):
        ps = c.pP[tt % 2]
        em.wait("tensor", s["wv"], wv_ld)
        pe = st["pP_evac"][tt % 2]
        if pe is not None:
            em.wait("tensor", s["sc"], pe)
        for fc in range(NCH):
            fn = (lambda e, ps=ps, fc=fc, tt=tt: e.matmul(
                ps[:, 0:256], c.fmv(xnF_arena, fc)[:, tt * 128:(tt + 1) * 128],
                wvv[:, fc, :], start=(fc == 0), stop=(fc == NCH - 1)))
            if fc < NCH - 1:
                em.qop("tensor", fn)
            else:
                em.op("tensor", s["tn"], fn)
        em.wait("scalar", s["tn"])
        st["pP_evac"][tt % 2] = em.op("scalar", s["sc"],
                                      lambda e, ps=ps, tt=tt: e.activation(
                                          c.vbuf[:, tt * 256:(tt + 1) * 256],
                                          ps[:, 0:256], AF.Silu))
    st["wv_war"] = c.s["tn"].n


def _fproj(c, xnF_arena, wdram, col0, dstbuf, wbuf, st, wkey, wsem):
    em, s = c.em, c.s
    wld = _load_w_slice(c, wdram, col0, 128, wbuf, wsem, st.get(wkey))
    wv = wbuf.rearrange("p (cc j) -> p cc j", cc=NCH)
    em.wait("scalar", s["tn"])
    em.wait("scalar", s["tp"])
    for nt in range(4):
        ps = c.pP[nt % 2]
        em.wait("tensor", wsem, wld)
        pe = st["pP_evac"][nt % 2]
        if pe is not None:
            em.wait("tensor", s["sc"], pe)
        for fc in range(NCH):
            fn = (lambda e, ps=ps, fc=fc, nt=nt, w=wv: e.matmul(
                ps[:, :], w[:, fc, :],
                c.fmv(xnF_arena, fc)[:, nt * 512:(nt + 1) * 512],
                start=(fc == 0), stop=(fc == NCH - 1)))
            if fc < NCH - 1:
                em.qop("tensor", fn)
            else:
                em.op("tensor", s["tn"], fn)
        em.wait("scalar", s["tn"])
        st["pP_evac"][nt % 2] = em.op("scalar", s["sc"],
                                      lambda e, ps=ps, nt=nt, d=dstbuf: e.activation(
                                          d[:, nt * 512:(nt + 1) * 512], ps[:, :],
                                          AF.Silu))
    st[wkey] = c.s["tn"].n


def _ktranspose(c, h, kb, ktb, tsem):
    em, s = c.em, c.s
    ktv = ktb.rearrange("p (cc t) -> p cc t", cc=NCH)
    em.wait("sync", s["sc"])
    em.wait("sync", s["tn"])
    em.wait("sync", s["vc"])
    tp = em.op("sync", tsem,
               lambda e, ktv=ktv, kb=kb: e.dma_start(ktv, kb, transpose=True), 16)
    em.wait("vector", tsem, tp)
    last = None
    for tile in range(NCH):
        u = tile % 2
        last = em.op("vector", s["vc"],
                     lambda e, t=tile, u=u, h=h, ktb=ktb: e.tensor_scalar(
                         ktb[:, t * 128:(t + 1) * 128],
                         ktb[:, t * 128:(t + 1) * 128],
                         c.CO("kdec")[:, 2 * h + u:2 * h + u + 1], None, ALU.mult))
    return last


def _kv_chain(c, h, hg, ktiles_ready):
    em, s = c.em, c.s
    kvs = c.kvst[:, h * 128:(h + 1) * 128]
    stt_done = None
    em.wait("tensor", s["sc"])
    for j in range(NBLK):
        em.wait("tensor", s["vc"], stt_done if stt_done is not None else ktiles_ready)
        for u in range(2):
            tile = 2 * j + u
            fn = (lambda e, t=tile, u=u, hg=hg: e.matmul(
                c.ps_kv, c.Z("ktbuf")[:, t * 128:(t + 1) * 128],
                c.vbuf[:, t * 256 + hg * 128:t * 256 + (hg + 1) * 128],
                start=(u == 0), stop=(u == 1)))
            if u == 0:
                em.qop("tensor", fn)
            else:
                em.op("tensor", s["tn"], fn)
        em.wait("vector", s["tn"])
        stt_done = em.opw("vector", s["vc"],
                          lambda e, kvs=kvs, h=h: e.scalar_tensor_tensor(
                              kvs, kvs, c.CO("bd")[:, h:h + 1], c.ps_kv,
                              ALU.mult, ALU.add))


def _kv_chain2(c, h, hg, ktiles_ready):
    em, s = c.em, c.s
    kvs = c.kvst[:, h * 128:(h + 1) * 128]
    stt_done = None
    em.wait("tensor", s["sc"])
    for j in range(NBLK):
        em.wait("tensor", s["vc"], stt_done if stt_done is not None else ktiles_ready)
        for u in range(2):
            tile = 2 * j + u
            fn = (lambda e, t=tile, u=u, hg=hg: e.matmul(
                c.ps_kv2, c.ktbuf2[:, t * 128:(t + 1) * 128],
                c.vbuf[:, t * 256 + hg * 128:t * 256 + (hg + 1) * 128],
                start=(u == 0), stop=(u == 1)))
            if u == 0:
                em.qop("tensor", fn)
            else:
                em.op("tensor", s["tn"], fn)
        em.wait("vector", s["tn"])
        stt_done = em.opw("vector", s["vc"],
                          lambda e, kvs=kvs, h=h: e.scalar_tensor_tensor(
                              kvs, kvs, c.CO("bd")[:, h:h + 1], c.ps_kv2,
                              ALU.mult, ALU.add))


def ph_prefix(c):
    em, s = c.em, c.s
    st = {"pP_evac": [None, None]}
    for g in range(H // 2):
        h0, h1 = 2 * g, 2 * g + 1
        _vproj(c, c.arena1, g, st)
        _fproj(c, c.arena1, c.dram["wk"], h0 * 128, c.Z("kbuf"), c.Z("wk"), st,
               "wk_war", s["wk"])
        _fproj(c, c.arena1, c.dram["wk"], h1 * 128, c.Z("kbuf2"), c.Z("wk"), st,
               "wk_war", s["wk"])
        kr0 = _ktranspose(c, h0, c.Z("kbuf"), c.Z("ktbuf"), s["tp"])
        kr1 = _ktranspose(c, h1, c.Z("kbuf2"), c.ktbuf2, s["tp2"])
        _kv_chain(c, h0, 0, kr0)
        _kv_chain2(c, h1, 1, kr1)


def ph_attn(c):
    em, s = c.em, c.s
    st = {"pP_evac": [None, None]}
    for g in range(int(os.environ.get("ATTN_GROUPS", str(H // 2)))):
        h0, h1 = 2 * g, 2 * g + 1
        _vproj(c, c.arena2, g, st)
        _fproj(c, c.arena2, c.dram["wk"], h0 * 128, c.Z("kbuf"), c.Z("wk"), st,
               "wk_war", s["wk"])
        _fproj(c, c.arena2, c.dram["wk"], h1 * 128, c.Z("kbuf2"), c.Z("wk"), st,
               "wk_war", s["wk"])
        kr0 = _ktranspose(c, h0, c.Z("kbuf"), c.Z("ktbuf"), s["tp"])
        kr1 = _ktranspose(c, h1, c.Z("kbuf2"), c.ktbuf2, s["tp2"])
        _fproj(c, c.arena2, c.dram["wq"], h0 * 128, c.Z("qbuf"), c.Z("wq"), st,
               "wq_war", s["wq"])
        _fproj(c, c.arena2, c.dram["wq"], h1 * 128, c.qbuf2, c.Z("wq"), st,
               "wq_war", s["wq"])
        if "tap_q0" in c.taps and g == 0:
            em.barrier(c.sems)
            tap_store(c, c.tap_dram["tap_q0"], c.Z("qbuf"))
            tap_store(c, c.tap_dram["tap_k0"], c.Z("kbuf"))
            tap_store(c, c.tap_dram["tap_v0"], c.vbuf)
        _scan_pair(c, h0, h1, kr0, kr1)
        _exchange_correct(c, g)


def _scan_pair(c, h0, h1, kr0, kr1):
    em, s = c.em, c.s
    hh_state = []
    for hh, h in enumerate((h0, h1)):
        hh_state.append(dict(
            h=h,
            qb=c.Z("qbuf") if hh == 0 else c.qbuf2,
            kb=c.Z("kbuf") if hh == 0 else c.Z("kbuf2"),
            ktb=c.Z("ktbuf") if hh == 0 else c.ktbuf2,
            q0t=c.Z("qkdt0") if hh == 0 else c.Z("qkdt0b"),
            q1t=c.Z("qkdt1") if hh == 0 else c.Z("qkdt1b"),
            kvbf=c.Z("kv_bf") if hh == 0 else c.Z("kv_bf2"),
            kvs=c.kvst[:, h * 128:(h + 1) * 128],
            qdec=c.Z("qdec")[:, h * 256:(h + 1) * 256],
            dt=c.Z("dt")[:, h * 256:(h + 1) * 256],
            pS=c.pS[hh], pO=c.pO[hh],
            pKV=c.pAUX[:, 0:128] if hh == 0 else c.pAUX2[:, 0:128],
            stt=None, mask=None, evac=None, inter_mm=None, intra_mm=None,
            ktr=kr0 if hh == 0 else kr1,
        ))
    em.wait("tensor", s["sc"])
    em.wait("tensor", s["vc"])
    for j in range(NBLK):
        for hh in range(2):
            S = hh_state[hh]
            if S["mask"] is not None:
                em.wait("tensor", s["vc"], S["mask"])
            for u in range(2):
                fn = (lambda e, j=j, u=u, S=S: e.matmul(
                    S["pS"][:, u * 256:(u + 1) * 256],
                    S["kb"][:, j * 256 + u * 128:j * 256 + (u + 1) * 128],
                    S["qb"][:, j * 256:(j + 1) * 256], start=True, stop=True))
                if u == 0:
                    em.qop("tensor", fn)
                else:
                    em.op("tensor", s["tn"], fn)
            S["sc_mm"] = c.s["tn"].n
        for hh in range(2):
            S = hh_state[hh]
            em.wait("vector", s["tn"], S["sc_mm"])
            if S["intra_mm"] is not None:
                em.wait("vector", s["tn"], S["intra_mm"])
            em.op("vector", s["vc"],
                  lambda e, S=S: e.tensor_tensor(S["q0t"], S["pS"][:, 0:256],
                                                 S["dt"], ALU.mult))
            S["mask"] = em.op("vector", s["vc"],
                              lambda e, S=S: e.tensor_tensor(
                                  S["q1t"][:, 128:256], S["pS"][:, 384:512],
                                  S["dt"][:, 0:128], ALU.mult))
            S["qt"] = em.op("vector", s["vc"],
                            lambda e, j=j, S=S: e.tensor_tensor(
                                S["qb"][:, j * 256:(j + 1) * 256],
                                S["qb"][:, j * 256:(j + 1) * 256], S["qdec"],
                                ALU.mult))
            em.wait("scalar", s["vc"], S["stt"] if S["stt"] is not None else S["ktr"])
            if S["inter_mm"] is not None:
                em.wait("scalar", s["tn"], S["inter_mm"])
            S["kc"] = em.op("scalar", s["sc"],
                            lambda e, S=S: e.copy(S["kvbf"], S["kvs"]))
        for hh in range(2):
            S = hh_state[hh]
            em.wait("tensor", s["sc"], S["kc"])
            em.wait("tensor", s["vc"], S["qt"])
            if S["evac"] is not None:
                em.wait("tensor", s["sc"], S["evac"])
            em.qop("tensor",
                   lambda e, j=j, S=S: e.matmul(
                       S["pO"][:, 0:256], S["kvbf"],
                       S["qb"][:, j * 256:(j + 1) * 256], start=True, stop=False))
            S["inter_mm"] = None
            for u in range(2):
                qk = S["q0t"] if u == 0 else S["q1t"]
                fn = (lambda e, j=j, u=u, S=S, qk=qk: e.matmul(
                    S["pO"][:, 0:256],
                    c.vbuf[:, (2 * j + u) * 256 + (S["h"] % 2) * 128:
                           (2 * j + u) * 256 + (S["h"] % 2 + 1) * 128],
                    qk, start=False, stop=(u == 1)))
                if u == 0:
                    em.qop("tensor", fn)
                else:
                    em.op("tensor", s["tn"], fn)
            S["inter_mm"] = S["intra_mm"] = c.s["tn"].n
            em.wait("tensor", s["vc"], S["stt"] if S["stt"] is not None else S["ktr"])
            for u in range(2):
                tile = 2 * j + u
                fn = (lambda e, t=tile, u=u, S=S: e.matmul(
                    S["pKV"], S["ktb"][:, t * 128:(t + 1) * 128],
                    c.vbuf[:, t * 256 + (S["h"] % 2) * 128:
                           t * 256 + (S["h"] % 2 + 1) * 128],
                    start=(u == 0), stop=(u == 1)))
                if u == 0:
                    em.qop("tensor", fn)
                else:
                    em.op("tensor", s["tn"], fn)
            S["kv_mm"] = c.s["tn"].n
        for hh in range(2):
            S = hh_state[hh]
            em.wait("scalar", s["tn"], S["intra_mm"])
            S["evac"] = em.op("scalar", s["sc"],
                              lambda e, j=j, S=S: e.copy(
                                  c.arena1[:, S["h"] * 2048 + j * 256:
                                           S["h"] * 2048 + (j + 1) * 256],
                                  S["pO"][:, 0:256]))
            em.wait("vector", s["tn"], S["kv_mm"])
            S["stt"] = em.opw("vector", s["vc"],
                              lambda e, S=S: e.scalar_tensor_tensor(
                                  S["kvs"], S["kvs"], c.CO("bd")[:, S["h"]:S["h"] + 1],
                                  S["pKV"], ALU.mult, ALU.add))


def _exchange_correct(c, g):
    em, s = c.em, c.s
    h0, h1 = 2 * g, 2 * g + 1
    rg = c.replica_groups
    em.wait("gpsimd", s["vc"])
    em.wait("gpsimd", s["tn"])
    em.op("gpsimd", s["cst"],
          lambda e: e.dma_start(c.dram["cci"][:, :],
                                c.kvst[:, h0 * 128:(h1 + 1) * 128]), 16)
    em.wait("gpsimd", s["cst"])
    cc = em.op("gpsimd", s["gp"],
               lambda e: e.collective_compute(
                   "AllGather", ALU.bypass, rg,
                   [c.dram["cci"][:, :]], [c.dram["cco"][:, :]]))
    em.wait("gpsimd", s["gp"], cc)
    em.op("gpsimd", s["cst"],
          lambda e: e.dma_start(c.S0, c.dram["cco"][0:128, :]), 16)
    em.op("gpsimd", s["cst"],
          lambda e: e.dma_start(c.S1, c.dram["cco"][128:256, :]), 16)
    em.wait("vector", s["cst"])
    em.wait("vector", s["sc"])
    em.opw("vector", s["vc"],
           lambda e: e.tensor_scalar(c.Seff, c.S0, c.CO("m0"), None, ALU.mult))
    sef = em.opw("vector", s["vc"],
                 lambda e: e.scalar_tensor_tensor(c.Seff, c.S1, c.CO("m1"), c.Seff,
                                                  ALU.mult, ALU.add))
    for hg, h in enumerate((h0, h1)):
        qb = c.Z("qbuf") if hg == 0 else c.qbuf2
        pS = c.pS[hg]
        Sh = c.Seff[:, hg * 128:(hg + 1) * 128]
        prev_add = None
        for j in range(NBLK):
            cast = em.opw("vector", s["vc"],
                          lambda e, Sh=Sh: e.tensor_copy(c.Sbf, Sh))
            em.wait("tensor", s["vc"], cast)
            if prev_add is not None:
                em.wait("tensor", s["vc"], prev_add)
            mmn = em.op("tensor", s["tn"],
                        lambda e, j=j, qb=qb, pS=pS: e.matmul(
                            pS[:, 0:256], c.Sbf, qb[:, j * 256:(j + 1) * 256],
                            start=True, stop=True))
            em.wait("vector", s["tn"], mmn)
            prev_add = em.opw("vector", s["vc"],
                              lambda e, j=j, h=h, pS=pS: e.tensor_tensor(
                                  c.arena1[:, h * 2048 + j * 256:h * 2048 + (j + 1) * 256],
                                  c.arena1[:, h * 2048 + j * 256:h * 2048 + (j + 1) * 256],
                                  pS[:, 0:256], ALU.add))
            if j < NBLK - 1:
                em.opw("vector", s["vc"],
                       lambda e, Sh=Sh, h=h: e.tensor_scalar(
                           Sh, Sh, c.CO("bd")[:, h:h + 1], None, ALU.mult))


def ph_rms(c):
    em, s = c.em, c.s
    acc_done = None
    for oc in range(NCH):
        em.wait("vector", s["tn"])
        sq = em.op("vector", s["vc"],
                   lambda e, oc=oc: e.tensor_tensor(
                       c.sqz, c.arena1[:, oc * T:(oc + 1) * T],
                       c.arena1[:, oc * T:(oc + 1) * T], ALU.mult))
        em.wait("tensor", s["vc"], sq)
        if acc_done is not None:
            em.wait("tensor", s["vc"], acc_done)
        for tc in range(NCH):
            fn = (lambda e, tc=tc: e.matmul(
                c.ps_var[:, tc:tc + 1], c.sqz[:, tc * 128:(tc + 1) * 128],
                c.ones_col, start=True, stop=True))
            if tc < NCH - 1:
                em.qop("tensor", fn)
            else:
                em.op("tensor", s["tn"], fn)
        em.wait("vector", s["tn"])
        if oc == 0:
            acc_done = em.opw("vector", s["vc"],
                              lambda e: e.tensor_copy(c.CO("var16"), c.ps_var))
        else:
            acc_done = em.opw("vector", s["vc"],
                              lambda e: e.tensor_tensor(c.CO("var16"), c.CO("var16"),
                                                        c.ps_var, ALU.add))

    def vv(fn):
        return em.opw("vector", s["vc"], fn)

    vv(lambda e: e.tensor_scalar(c.CO("var16"), c.CO("var16"), 1.0 / D, EPS_RMS,
                                 ALU.mult, ALU.add))
    rc = vv(lambda e: e.reciprocal(c.CO("var16"), c.CO("var16")))
    em.wait("scalar", s["vc"], rc)
    sq = em.op("scalar", s["sc"],
               lambda e: e.activation(c.CO("a2"), c.CO("var16"), AF.Sqrt))
    em.wait("vector", s["sc"], sq)
    vv(lambda e: e.tensor_tensor(c.CO("a2sq"), c.CO("a2"), c.CO("a2"), ALU.mult))


def ph_gate(c):
    em, s = c.em, c.s
    for oc in range(NCH):
        em.op("gpsimd", s["wo"],
              lambda e, oc=oc: e.dma_start(
                  c.arena3[:, oc * T:(oc + 1) * T],
                  c.dram["wo"][oc * 128:(oc + 1) * 128, :]), 16)
    em.wait("vector", s["wo"])
    for oc in range(NCH):
        em.op("vector", s["vc"],
              lambda e, oc=oc: e.tensor_scalar(
                  c.arena3[:, oc * T:(oc + 1) * T], c.arena3[:, oc * T:(oc + 1) * T],
                  c.CO("nw")[:, oc:oc + 1], None, ALU.mult))
    prev_mm = [None, None]
    for gc in range(NCH):
        wb = c.wg[gc % 2]
        swg = s["wg0"] if gc % 2 == 0 else s["wg1"]
        if prev_mm[gc % 2] is not None:
            em.wait("gpsimd", s["tn"], prev_mm[gc % 2])
        ld = em.op("gpsimd", swg,
                   lambda e, gc=gc, wb=wb: e.dma_start(
                       wb.rearrange("p (cc j) -> p cc j", cc=NCH),
                       c.dram["wg"][:, gc * 128:(gc + 1) * 128].rearrange(
                           "(cc p) j -> p cc j", p=128)), 16)
        wv = wb.rearrange("p (cc j) -> p cc j", cc=NCH)
        for nt in range(4):
            ps = c.pP[nt % 2]
            em.wait("tensor", swg, ld)
            em.wait("tensor", s["sc"])
            for fc in range(NCH):
                fn = (lambda e, ps=ps, fc=fc, nt=nt, w=wv: e.matmul(
                    ps[:, :], w[:, fc, :],
                    c.fmv(c.arena2, fc)[:, nt * 512:(nt + 1) * 512],
                    start=(fc == 0), stop=(fc == NCH - 1)))
                if fc < NCH - 1:
                    em.qop("tensor", fn)
                else:
                    em.op("tensor", s["tn"], fn)
            em.wait("scalar", s["tn"])
            em.wait("scalar", s["vc"])
            sg = em.op("scalar", s["sc"],
                       lambda e, ps=ps, nt=nt: e.activation(
                           c.sig[nt % 2], ps[:, :], AF.Sigmoid))
            em.wait("vector", s["sc"], sg)
            em.op("vector", s["vc"],
                  lambda e, gc=gc, nt=nt: e.tensor_tensor(
                      c.arena1[:, gc * T + nt * 512:gc * T + (nt + 1) * 512],
                      c.arena1[:, gc * T + nt * 512:gc * T + (nt + 1) * 512],
                      c.sig[nt % 2], ALU.mult))
        prev_mm[gc % 2] = c.s["tn"].n


def ph_outproj(c):
    import os
    _lvl = int(os.environ.get("P4_LEVEL", "3"))
    em, s = c.em, c.s
    evac_done = None
    _ntc = int(os.environ.get("P4_NTC", str(NCH)))
    for tc in range(_ntc):
        if evac_done is not None:
            em.wait("gpsimd", s["vc"], evac_done)
        if _lvl >= 1:
            ld = em.op("gpsimd", s["xp4"],
                       lambda e, tc=tc: e.dma_start(
                           c.xchunk_p4, c.dram["x"][tc * 128:(tc + 1) * 128, :]), 16)
        if evac_done is not None:
            em.wait("tensor", s["vc"], evac_done)
            em.wait("tensor", s["sc"])
        for ft in range(4):
            for kc in range(NCH):
                fn = (lambda e, tc=tc, ft=ft, kc=kc: e.matmul(
                    c.pBIG[:, ft * 512:(ft + 1) * 512],
                    c.arena1[:, kc * T + tc * 128:kc * T + (tc + 1) * 128],
                    c.arena3[:, kc * T + ft * 512:kc * T + (ft + 1) * 512],
                    start=(kc == 0), stop=(kc == NCH - 1)))
                if ft == 3 and kc == NCH - 1:
                    em.op("tensor", s["tn"], fn)
                else:
                    em.qop("tensor", fn)
        mm = c.s["tn"].n
        em.wait("scalar", s["tn"], mm)
        cp1 = em.op("scalar", s["sc"],
                    lambda e: e.activation(c.sq_p4_bf, c.pBIG[:, :], AF.Copy,
                                           accum_out=c.CO("s1")))
        em.wait("scalar", s["sc"], cp1)
        sq = em.op("scalar", s["sc"],
                   lambda e: e.activation(c.sq_p4_bf, c.pBIG[:, :], AF.Square,
                                          accum_out=c.CO("s2")))
        em.wait("vector", s["sc"], sq)

        def vv(fn):
            return em.opw("vector", s["vc"], fn)

        vv(lambda e: e.tensor_scalar(c.CO("mneg"), c.CO("s1"), -1.0 / D, None, ALU.mult))
        vv(lambda e: e.tensor_tensor(c.CO("mm"), c.CO("mneg"), c.CO("mneg"), ALU.mult))
        vv(lambda e: e.tensor_scalar(c.CO("tv"), c.CO("s2"), 1.0 / D, None, ALU.mult))
        vv(lambda e: e.tensor_tensor(c.CO("tv"), c.CO("tv"), c.CO("mm"), ALU.subtract))
        vv(lambda e, tc=tc: e.scalar_tensor_tensor(
            c.CO("tv"), c.CO("tv"), c.CO("a2sq")[:, tc:tc + 1], c.CO("epsln"),
            ALU.mult, ALU.add))
        rc = vv(lambda e: e.reciprocal(c.CO("rc"), c.CO("tv")))
        em.wait("scalar", s["vc"], rc)
        sq2 = em.op("scalar", s["sc"],
                    lambda e: e.activation(c.CO("acol"), c.CO("rc"), AF.Sqrt))
        em.wait("vector", s["sc"], sq2)
        vv(lambda e, tc=tc: e.tensor_tensor(
            c.CO("c1"), c.CO("a2")[:, tc:tc + 1], c.CO("acol"), ALU.mult))
        vv(lambda e: e.tensor_tensor(c.CO("bcol"), c.CO("mneg"), c.CO("c1"),
                                     ALU.mult))
        if _lvl >= 1:
            em.wait("vector", s["xp4"], ld)
        vv(lambda e: e.tensor_scalar(c.xchunk_p4, c.xchunk_p4, c.CO("bcol"), None,
                                     ALU.add))
        evac_done = vv(lambda e, tc=tc: e.scalar_tensor_tensor(
            c.arena2[:, tc * 2048:(tc + 1) * 2048], c.pBIG[:, :], c.CO("c1"),
            c.xchunk_p4, ALU.mult, ALU.add))


def ph_ffn(c):
    em, s = c.em, c.s
    for kc in range(NCH):
        em.op("gpsimd", s["wf"],
              lambda e, kc=kc: e.dma_start(
                  c.arena3[:, kc * T:(kc + 1) * T],
                  c.dram["wf"][kc * 128:(kc + 1) * 128, :]), 16)
    wf_ld = c.s["wf"].n
    em.op("gpsimd", s["cst"],
          lambda e: e.dma_start(c.wsb, c.dram["ws"][None, :].broadcast_to((128, D))), 16)
    em.op("gpsimd", s["cst"],
          lambda e: e.dma_start(c.biasb[:, :],
                                c.dram["fb"][None, :].broadcast_to((128, D))), 16)

    qxv_q = [q.rearrange("p (cc t) -> p cc t", cc=NCH) for q in c.qxf_q]
    tp_done = [None, None]
    store_done = [None]
    q_mm_done = [None, None]
    pp_evac = [None, None]

    def quant_tc(tc, qq):
        ti = tc % 4
        x2c = c.arena2[:, tc * 2048:(tc + 1) * 2048]
        em.wait("scalar", s["vc"])
        sq = em.op("scalar", s["sc"],
                   lambda e, x2c=x2c: e.activation(c.Xbf[:, 0:2048], x2c, AF.Square,
                                                   accum_out=c.CO("s2")))
        em.opw("vector", s["vc"],
               lambda e, x2c=x2c: e.reduce_sum(c.CO("s1"), x2c, axis=AX.X))
        em.opw("vector", s["vc"],
               lambda e, x2c=x2c: e.tensor_reduce(c.CO("mx"), x2c, AX.X, ALU.max))
        em.opw("vector", s["vc"],
               lambda e, x2c=x2c: e.tensor_reduce(c.CO("mn"), x2c, AX.X, ALU.min))
        em.wait("vector", s["sc"], sq)

        def vv(fn):
            return em.opw("vector", s["vc"], fn)

        vv(lambda e: e.tensor_scalar(c.CO("mneg"), c.CO("s1"), -1.0 / D, None,
                                     ALU.mult))
        vv(lambda e: e.tensor_tensor(c.CO("mm"), c.CO("mneg"), c.CO("mneg"), ALU.mult))
        vv(lambda e: e.tensor_scalar(c.CO("tv"), c.CO("s2"), 1.0 / D, EPS_LN,
                                     ALU.mult, ALU.add))
        vv(lambda e: e.tensor_tensor(c.CO("tv"), c.CO("tv"), c.CO("mm"), ALU.subtract))
        rc = vv(lambda e: e.reciprocal(c.CO("rc"), c.CO("tv")))
        em.wait("scalar", s["vc"], rc)
        sq2 = em.op("scalar", s["sc"],
                    lambda e: e.activation(c.CO("acol"), c.CO("rc"), AF.Sqrt))
        em.wait("vector", s["sc"], sq2)
        vv(lambda e: e.tensor_scalar(c.CO("t1"), c.CO("mx"), c.CO("mneg"), None,
                                     ALU.add))
        vv(lambda e: e.tensor_scalar(c.CO("t2"), c.CO("mn"), c.CO("mneg"), -1.0,
                                     ALU.add, ALU.mult))
        vv(lambda e: e.tensor_tensor(c.CO("t1"), c.CO("t1"), c.CO("t2"), ALU.max))
        vv(lambda e: e.tensor_tensor(c.CO("t1"), c.CO("t1"), c.CO("acol"), ALU.mult))
        vv(lambda e: e.tensor_scalar(c.CO("scol"), c.CO("t1"), EPS_Q, 1.0 / 127.0,
                                     ALU.max, ALU.mult))
        vv(lambda e: e.reciprocal(c.CO("invs"), c.CO("scol")))
        vv(lambda e: e.tensor_tensor(c.CO("c1"), c.CO("acol"), c.CO("invs"), ALU.mult))
        vv(lambda e: e.tensor_tensor(c.CO("c2"), c.CO("mneg"), c.CO("c1"), ALU.mult))
        vv(lambda e, tc=tc: e.tensor_copy(c.CO("insc")[:, tc:tc + 1], c.CO("scol")))
        bi = tc % 2
        vv(lambda e, x2c=x2c: e.tensor_scalar(c.X32, x2c, c.CO("c1"), c.CO("c2"),
                                              ALU.mult, ALU.add))
        vv(lambda e: e.tensor_scalar(c.X32, c.X32, MAGIC, MAGIC, ALU.add,
                                     ALU.subtract))
        stq = s["t0"] if bi == 0 else s["t1"]
        if tp_done[bi] is not None:
            em.wait("vector", stq, tp_done[bi])
        qd = em.opw("vector", s["vc"],
                    lambda e, bi=bi: e.tensor_scalar(c.qf[bi], c.X32, 127.0, -128.0,
                                                     ALU.min, ALU.max))
        em.wait("sync", s["vc"], qd)
        if q_mm_done[qq] is not None:
            em.wait("sync", s["tn"], q_mm_done[qq])
        tp_done[bi] = em.op("sync", stq,
                            lambda e, bi=bi, ti=ti, qq=qq: e.dma_start(
                                qxv_q[qq][:, :, ti * 128:(ti + 1) * 128], c.qf[bi],
                                transpose=True), 16)

    def mm_tc(tc, qq):
        ti = tc % 4
        em.wait("tensor", s["wf"], wf_ld)
        em.wait("tensor", s["t0"])
        em.wait("tensor", s["t1"])
        for ft in range(4):
            pp = ft % 2
            if pp_evac[pp] is not None:
                em.wait("tensor", s["vc"], pp_evac[pp])
            for kc in range(NCH):
                fn = (lambda e, ti=ti, ft=ft, kc=kc, qq=qq, pp=pp: e.matmul(
                    c.pP[pp][:, :],
                    c.qxf_q[qq][:, kc * 512 + ti * 128:kc * 512 + (ti + 1) * 128],
                    c.arena3[:, kc * T + ft * 512:kc * T + (ft + 1) * 512],
                    start=(kc == 0), stop=(kc == NCH - 1)))
                if kc == NCH - 1:
                    em.op("tensor", s["tn"], fn)
                else:
                    em.qop("tensor", fn)
            mmn = c.s["tn"].n
            em.wait("vector", s["tn"], mmn)
            em.wait("vector", s["cst"])
            em.opw("vector", s["vc"],
                   lambda e, ft=ft, tc=tc, pp=pp: e.scalar_tensor_tensor(
                       c.tmp32, c.pP[pp][:, :], c.CO("insc")[:, tc:tc + 1],
                       c.wsb[:, ft * 512:(ft + 1) * 512], ALU.mult, ALU.mult))
            pp_evac[pp] = em.opw("vector", s["vc"],
                                 lambda e, ft=ft: e.tensor_tensor(
                                     c.ffn16[:, ft * 512:(ft + 1) * 512], c.tmp32,
                                     c.biasb[:, ft * 512:(ft + 1) * 512], ALU.add))

    def epi_tc(tc):
        ep = c.s["vc"].n
        em.wait("scalar", s["vc"], ep)
        if store_done[0] is not None:
            em.wait("scalar", s["st"], store_done[0])
            em.wait("vector", s["st"], store_done[0])
        sq = em.op("scalar", s["sc"],
                   lambda e: e.activation(c.outbuf, c.ffn16, AF.Square,
                                          accum_out=c.CO("s2")))
        em.opw("vector", s["vc"],
               lambda e: e.reduce_sum(c.CO("s1"), c.ffn16, axis=AX.X))

        def vv(fn):
            return em.opw("vector", s["vc"], fn)

        vv(lambda e: e.tensor_scalar(c.CO("mneg"), c.CO("s1"), -1.0 / D, None,
                                     ALU.mult))
        vv(lambda e: e.tensor_tensor(c.CO("mm"), c.CO("mneg"), c.CO("mneg"), ALU.mult))
        em.wait("vector", s["sc"], sq)
        vv(lambda e: e.tensor_scalar(c.CO("tv"), c.CO("s2"), 1.0 / D, EPS_LN,
                                     ALU.mult, ALU.add))
        vv(lambda e: e.tensor_tensor(c.CO("tv"), c.CO("tv"), c.CO("mm"), ALU.subtract))
        rc = vv(lambda e: e.reciprocal(c.CO("rc"), c.CO("tv")))
        em.wait("scalar", s["vc"], rc)
        sq2 = em.op("scalar", s["sc"],
                    lambda e: e.activation(c.CO("acol"), c.CO("rc"), AF.Sqrt))
        em.wait("vector", s["sc"], sq2)
        vv(lambda e: e.tensor_tensor(c.CO("bcol"), c.CO("mneg"), c.CO("acol"),
                                     ALU.mult))
        vv(lambda e: e.tensor_scalar(c.outbuf, c.ffn16, c.CO("acol"), c.CO("bcol"),
                                     ALU.mult, ALU.add))
        od = vv(lambda e, tc=tc: e.tensor_tensor(
            c.outbuf, c.outbuf, c.arena2[:, tc * 2048:(tc + 1) * 2048], ALU.add))
        em.wait("sync", s["vc"], od)
        store_done[0] = em.op("sync", s["st"],
                              lambda e, tc=tc: e.dma_start(
                                  c.dram["out"][tc * 128:(tc + 1) * 128, :],
                                  c.outbuf), 16)

    for ti in range(4):
        quant_tc(ti, 0)
    for hf in range(4):
        qq = hf % 2
        for ti in range(4):
            tc = hf * 4 + ti
            mm_tc(tc, qq)
            if ti == 3:
                q_mm_done[qq] = c.s["tn"].n
            if hf < 3:
                quant_tc((hf + 1) * 4 + ti, (hf + 1) % 2)
            epi_tc(tc)


def _host_prep(inputs):
    x = np.asarray(inputs["L0_attn_hidden_states"], dtype=np.float32)
    slopes = np.ascontiguousarray(
        np.asarray(inputs["L0_attn_slope_rate"], dtype=np.float32).reshape(H))
    wqkv = np.asarray(inputs["L0_attn_w_qkv"], dtype=np.float32)
    r = wqkv.reshape(H, 3, HD, D)
    wqT = np.ascontiguousarray(r[:, 0].transpose(2, 0, 1).reshape(D, D))
    wkT = np.ascontiguousarray(r[:, 1].transpose(2, 0, 1).reshape(D, D))
    wvT = np.ascontiguousarray(r[:, 2].transpose(2, 0, 1).reshape(D, D))
    wgT = np.ascontiguousarray(np.asarray(inputs["L0_attn_w_gate"], np.float32).T)
    woT = np.ascontiguousarray(np.asarray(inputs["L0_attn_w_out"], np.float32).T)
    wfT = np.ascontiguousarray(np.asarray(inputs["L0_ffn_weight"]).T)
    normw = np.ascontiguousarray(np.asarray(inputs["L0_attn_norm_w"], np.float32))
    wscale = np.ascontiguousarray(
        np.asarray(inputs["L0_ffn_w_scale"], np.float32).reshape(D))
    fbias = np.ascontiguousarray(np.asarray(inputs["L0_ffn_bias"], np.float32))
    in_maps = []
    for core in range(8):
        b, hf = core // 2, core % 2
        x_own = np.ascontiguousarray(x[b, hf * T:(hf + 1) * T])
        msk = np.array([1.0, 0.0], np.float32) if hf == 1 else np.zeros(2, np.float32)
        in_maps.append(dict(x_own=x_own, msk=msk, wqT=wqT, wkT=wkT, wvT=wvT,
                            wgT=wgT, woT=woT, wfT=wfT, slopes=slopes, normw=normw,
                            wscale=wscale, fbias=fbias))
    return x, in_maps


def kernel(**inputs):
    x, in_maps = _host_prep(inputs)
    nc = build_nc()
    res = run_bass_kernel_spmd(nc, in_maps, core_ids=list(range(8)))
    out = np.empty((4, 4096, D), np.float32)
    for core in range(8):
        b, hf = core // 2, core % 2
        out[b, hf * T:(hf + 1) * T, :] = res.results[core]["out"]
    return out


# version 09 (final)
# speedup vs baseline: 1.3866x; 1.1707x over previous
import os
import numpy as np
from contextlib import ExitStack

import concourse.bass as bass
import concourse.mybir as mybir
from concourse.bass_utils import run_bass_kernel_spmd

F32 = mybir.dt.float32
BF16 = mybir.dt.bfloat16
F16 = mybir.dt.float16
I8 = mybir.dt.int8
AF = mybir.ActivationFunctionType
ALU = mybir.AluOpType
AX = mybir.AxisListType

T = 2048
D = 2048
H = 16
HD = 128
NCH = 16
NBLK = 8
MAGIC = 12582912.0
EPS_LN = 1e-5
EPS_RMS = 1e-6
EPS_Q = 1e-7


class Sem:
    def __init__(self, ctx, nc, name):
        self.h = ctx.enter_context(nc.semaphore(name))
        self.n = 0

    def bump(self, k):
        self.n += k
        return self.n


class Em:

    ENGS = ("tensor", "vector", "scalar", "gpsimd", "sync")

    def __init__(self):
        self.q = {e: [] for e in self.ENGS}
        self.lastwait = {e: {} for e in self.ENGS}
        self.trace = {e: [] for e in self.ENGS}

    def wait(self, eng, sem, cnt=None):
        cnt = sem.n if cnt is None else cnt
        if cnt is None or cnt <= 0:
            return
        lw = self.lastwait[eng]
        if lw.get(id(sem), -1) >= cnt:
            return
        lw[id(sem)] = cnt
        self.trace[eng].append(("wait", sem.h.name, cnt))
        self.q[eng].append(lambda e, s=sem.h, c=cnt: e.wait_ge(s, c))

    def qop(self, eng, fn):
        self.q[eng].append(fn)

    def op(self, eng, sem, fn, k=1):
        v = sem.bump(k)
        import inspect
        try:
            srcl = inspect.getsource(fn).strip().split("\n")[0][:80]
        except Exception:
            srcl = "?"
        self.trace[eng].append(("op", sem.h.name, v, srcl))
        self.q[eng].append(lambda e, f=fn, s=sem.h, kk=k: f(e).then_inc(s, kk))
        return v

    def opw(self, eng, sem, fn, k=1):
        v = self.op(eng, sem, fn, k)
        self.wait(eng, sem, v)
        return v

    def raw(self, eng, fn):
        self.q[eng].append(fn)

    def barrier(self, sems):
        for eng in self.ENGS:
            for s in sems:
                self.wait(eng, s)


class C:
    pass


def build_nc(taps=(), stages=99, replica_groups=None):
    taps = set(taps)
    nc = bass.Bass("TRN2", target_bir_lowering=False)
    c = C()
    c.nc = nc
    c.taps = taps
    c.replica_groups = replica_groups or [[0, 1], [2, 3], [4, 5], [6, 7]]

    dx = nc.dram_tensor("x_own", [T, D], F32, kind="ExternalInput")
    dmsk = nc.dram_tensor("msk", [2], F32, kind="ExternalInput")
    dcci = nc.dram_tensor("cc_in", [128, 256], F32, kind="Internal")
    dcco = nc.dram_tensor("cc_out", [256, 256], F32, kind="Internal")
    dwq = nc.dram_tensor("wqT", [D, D], F32, kind="ExternalInput")
    dwk = nc.dram_tensor("wkT", [D, D], F32, kind="ExternalInput")
    dwv = nc.dram_tensor("wvT", [D, D], F32, kind="ExternalInput")
    dwg = nc.dram_tensor("wgT", [D, D], F32, kind="ExternalInput")
    dwo = nc.dram_tensor("woT", [D, D], F32, kind="ExternalInput")
    dwf = nc.dram_tensor("wfT", [D, D], I8, kind="ExternalInput")
    dsl = nc.dram_tensor("slopes", [H], F32, kind="ExternalInput")
    dnw = nc.dram_tensor("normw", [D], F32, kind="ExternalInput")
    dws = nc.dram_tensor("wscale", [D], F32, kind="ExternalInput")
    dfb = nc.dram_tensor("fbias", [D], F32, kind="ExternalInput")
    dout = nc.dram_tensor("out", [T, D], F32, kind="ExternalOutput")
    c.dram = dict(x=dx, msk=dmsk, cci=dcci, cco=dcco, wq=dwq, wk=dwk, wv=dwv,
                  wg=dwg, wo=dwo, wf=dwf, sl=dsl, nw=dnw, ws=dws, fb=dfb, out=dout)
    c.tap_dram = {}
    for name, shape, dt in [
        ("tap_xnfpre", [128, NCH * T], BF16), ("tap_xnf", [128, NCH * T], BF16),
        ("tap_kv", [128, H * HD], F32), ("tap_q0", [128, T], BF16),
        ("tap_k0", [128, T], BF16), ("tap_v0", [128, 4096], BF16),
        ("tap_outf", [128, NCH * T], BF16), ("tap_gout", [128, NCH * T], BF16),
        ("tap_x2", [128, NCH * T], BF16), ("tap_qxf", [128, NCH * 1024], BF16),
        ("tap_insc", [128, 16], F32),
        ("tap_cols", [128, 16 * 8], F32),
    ]:
        if name in taps:
            c.tap_dram[name] = nc.dram_tensor(name, shape, dt, kind="ExternalOutput")

    big = ExitStack()
    e = big.enter_context

    em = Em()
    c.em = em
    names = ["wl", "tp", "st", "tn", "vc", "sc", "gp",
             "x0", "x1", "t0", "t1", "wq", "wk", "wv", "wg0", "wg1",
             "wo", "wf", "cst", "xp4", "tp2"]
    c.s = {nm: Sem(big, nc, "s_" + nm) for nm in names}
    c.sems = list(c.s.values())

    arena1 = e(nc.sbuf_tensor("arena1", [128, 32768], BF16))
    arena2 = e(nc.sbuf_tensor("arena2", [128, 32768], BF16))
    arena3 = e(nc.sbuf_tensor("arena3", [128, 32768], BF16))
    c.arena1, c.arena2, c.arena3 = arena1, arena2, arena3

    z = {}
    off = 0

    def zalloc(name, n):
        nonlocal off
        z[name] = (off, off + n)
        off += n

    zalloc("qdec", H * 256)
    zalloc("dt", H * 256)
    zalloc("qbuf", T)
    zalloc("kbuf", T)
    zalloc("ktbuf", T)
    zalloc("wq", T)
    zalloc("wk", T)
    zalloc("qkdt0", 256)
    zalloc("qkdt1", 256)
    zalloc("kv_bf", 128)
    zalloc("kv_bf2", 128)
    zalloc("qkdt0b", 256)
    zalloc("qkdt1b", 256)
    zalloc("kbuf2", T)
    zalloc("shared", 10240)
    assert off <= 32768, off
    c.z = z

    def Z(name):
        lo, hi = z[name]
        return arena3[:, lo:hi]

    c.Z = Z
    sh = z["shared"][0]
    c.xn_tok = [arena3[:, sh:sh + 2048], arena3[:, sh + 2048:sh + 4096]]
    c.xchunk = [arena3[:, sh + 4096:sh + 6144], arena3[:, sh + 6144:sh + 8192]]
    c.sqz = arena3[:, sh + 8192:sh + 10240]
    c.vbuf = arena3[:, sh:sh + 4096]
    c.wv = arena3[:, sh + 4096:sh + 8192]

    kvst = e(nc.sbuf_tensor("kvst", [128, 2048], F32))
    c.kvst = kvst
    cols = e(nc.sbuf_tensor("cols", [128, 224], F32))
    c.cols = cols
    co = {}
    coff = 0

    def calloc(name, n):
        nonlocal coff
        co[name] = (coff, coff + n)
        coff += n

    for nm, n in [("slneg", 16), ("nw", 16), ("bd", 16), ("kdec", 32),
                  ("a2", 16), ("a2sq", 16), ("insc", 16), ("c256", 1),
                  ("s1", 1), ("s2", 2), ("mneg", 1), ("mm", 1), ("tv", 1),
                  ("rc", 1), ("acol", 1), ("bcol", 1), ("mx", 1), ("mn", 1),
                  ("t1", 1), ("t2", 1), ("scol", 1), ("invs", 1), ("c1", 1),
                  ("c2", 1), ("epsln", 1), ("var16", 16),
                  ("s1b", 1), ("s2b", 2), ("mnegb", 1), ("mmb", 1), ("tvb", 1),
                  ("rcb", 1), ("acolb", 1), ("bcolb", 1), ("m0", 1), ("m1", 1)]:
        calloc(nm, n)
    assert coff <= 224

    def CO(name):
        lo, hi = co[name]
        return cols[:, lo:lo + 1] if hi - lo == 2 else cols[:, lo:hi]

    c.CO = CO

    consts = e(nc.sbuf_tensor("consts", [128, 288], BF16))
    c.ones_col = consts[:, 0:1]
    c.onesq = consts[:, 32:160]
    c.ident = consts[:, 160:288]
    sigt = e(nc.sbuf_tensor("sigt", [128, 1024], BF16))
    c.sig = [sigt[:, 0:512], sigt[:, 512:1024]]
    biasb = e(nc.sbuf_tensor("biasb", [128, 2048], BF16))
    c.biasb = biasb
    c.qbuf2 = biasb[:, :]
    c.ktbuf2 = c.sqz
    kb_lo, _ = z["kbuf"]
    a3f = arena3.bitcast(F32)
    c.S0 = a3f[:, kb_lo // 2:kb_lo // 2 + 256]
    c.S1 = a3f[:, kb_lo // 2 + 256:kb_lo // 2 + 512]
    kb2_lo, _ = z["kbuf2"]
    c.Seff = a3f[:, kb2_lo // 2:kb2_lo // 2 + 256]
    c.sbf_tiles0 = sigt[:, 0:1024]
    c.sbf_tiles1 = arena3[:, kb2_lo + 512:kb2_lo + 1536]

    a1f = arena1.bitcast(F32)
    c.iota_m1 = a1f[:, 0:256]
    c.iota_kd = a1f[:, 256:258]
    c.idxT0 = a1f[:, 512:768]
    c.slpos = a1f[:, 768:784]
    c.dt_scr = a1f[:, 1024:1280]

    kvb = kvst.bitcast(BF16)
    c.wg = [kvb[:, 0:2048], kvb[:, 2048:4096]]
    c.xchunk_p4 = kvb[:, 0:2048]
    c.sq_p4_bf = kvb[:, 2048:4096]
    c.outbuf = kvst[:, 0:2048]

    c.qxf_q = [arena1[:, 0:8192], arena1[:, 8192:16384]]
    c.X32 = a1f[:, 8192:10240]
    c.Xbf = arena1[:, 16384:20480]
    c.qf = [arena1[:, 20480:22528], arena1[:, 22528:24576]]
    c.wsb = a1f[:, 12288:14336]
    c.ffn16 = arena1.bitcast(F16)[:, 28672:30720]
    c.tmp32 = a1f[:, 15360:15872]

    c.pP = [e(nc.psum_tensor("pP0", [128, 512], F32)),
            e(nc.psum_tensor("pP1", [128, 512], F32))]
    c.pBIG = e(nc.psum_tensor("pBIG", [128, 2048], F32))
    c.pS = [c.pBIG[:, 0:512], c.pBIG[:, 512:1024]]
    c.pO = [c.pBIG[:, 1024:1536], c.pBIG[:, 1536:2048]]
    c.pAUX = e(nc.psum_tensor("pAUX", [128, 512], F32))
    c.pAUX2 = e(nc.psum_tensor("pAUX2", [128, 512], F32))
    c.ps_kv = c.pAUX[:, 0:128]
    c.ps_kv2 = c.pAUX2[:, 0:128]
    c.ps_var = c.pAUX2[:, 256:272]

    def fmv(arena, fc):
        return arena[:, fc * T:(fc + 1) * T]

    c.fmv = fmv

    _skip = set(int(x) for x in os.environ.get("SKIP_STAGES", "").split(",") if x)

    def stage(i):
        return stages >= i and i not in _skip

    ph_consts(c)
    em.barrier(c.sems)
    if stage(3):
        ph_ln_transpose(c, dx, arena2, "tap_xnf")
        em.barrier(c.sems)
    if stage(4):
        ph_attn(c)
        em.barrier(c.sems)
        if "tap_outf" in taps:
            tap_store(c, c.tap_dram["tap_outf"], arena1[:, :])
            em.barrier(c.sems)
    if stage(5):
        ph_rms(c)
        em.barrier(c.sems)
    if stage(6):
        ph_gate(c)
        em.barrier(c.sems)
        if "tap_gout" in taps:
            tap_store(c, c.tap_dram["tap_gout"], arena1[:, :])
            em.barrier(c.sems)
    if stage(7):
        ph_outproj(c)
        em.barrier(c.sems)
        if "tap_x2" in taps:
            tap_store(c, c.tap_dram["tap_x2"], arena2[:, :])
            em.barrier(c.sems)
    if stage(8):
        ph_ffn(c)
    em.raw("sync", lambda e2, s=c.s["st"].h, v=c.s["st"].n: e2.wait_ge(s, v))

    with nc.Block() as block:
        @block.tensor
        def _(t):
            for f in em.q["tensor"]:
                f(t)

        @block.vector
        def _(v):
            for f in em.q["vector"]:
                f(v)

        @block.scalar
        def _(s2):
            for f in em.q["scalar"]:
                f(s2)

        @block.gpsimd
        def _(g):
            for f in em.q["gpsimd"]:
                f(g)

        @block.sync
        def _(sy):
            for f in em.q["sync"]:
                f(sy)

    big.close()
    return nc


def tap_store(c, dram, sbuf_ap):
    em, s = c.em, c.s
    n = sbuf_ap.shape[1]
    CH = 4096
    for i in range(0, n, CH):
        w = min(CH, n - i)
        em.op("sync", s["st"],
              lambda e, d=dram[:, i:i + w], src=sbuf_ap[:, i:i + w]: e.dma_start(d, src),
              16)
    em.wait("sync", s["st"])


def ph_consts(c):
    em, s = c.em, c.s
    _lite = int(os.environ.get("CONSTS_LITE", "0"))

    def g(fn):
        return em.op("gpsimd", s["gp"], fn)

    g(lambda e: e.memset(c.ones_col, 1.0))
    g(lambda e: e.memset(c.onesq, 1.0))
    em.wait("gpsimd", s["gp"])
    g(lambda e: e.affine_select(c.ident, c.onesq, [[-1, 128]], ALU.is_equal, 0.0,
                                base=0, channel_multiplier=1))
    g(lambda e: e.iota(c.iota_m1, [[1, 256]], base=1, channel_multiplier=0,
                       allow_small_or_imprecise_dtypes=True))
    g(lambda e: e.iota(c.iota_kd, [[-128, 2]], base=255, channel_multiplier=-1,
                       allow_small_or_imprecise_dtypes=True))
    g(lambda e: e.iota(c.idxT0, [[1, 256]], base=0, channel_multiplier=-1,
                       allow_small_or_imprecise_dtypes=True))
    g(lambda e: e.memset(c.CO("c256"), 256.0))
    g(lambda e: e.memset(c.CO("epsln"), EPS_LN))
    g(lambda e: e.memset(c.kvst[:, :], 0.0))
    g(lambda e: e.memset(c.Z("qkdt1"), 0.0))
    g(lambda e: e.memset(c.Z("qkdt1b"), 0.0))

    if _lite >= 3:
        return
    em.op("sync", s["st"],
          lambda e: e.dma_start(c.slpos, c.dram["sl"][None, :].broadcast_to((128, 16))),
          16)
    em.op("gpsimd", s["cst"],
          lambda e: e.dma_start(c.CO("m0"), c.dram["msk"][None, 0:1].broadcast_to((128, 1))), 16)
    em.op("gpsimd", s["cst"],
          lambda e: e.dma_start(c.CO("m1"), c.dram["msk"][None, 1:2].broadcast_to((128, 1))), 16)
    em.op("sync", s["st"],
          lambda e: e.dma_start(c.CO("nw"),
                                c.dram["nw"].rearrange("(cc p) -> p cc", p=128),
                                allow_slow_non_contiguous=True), 16)
    em.wait("vector", s["st"])
    em.op("vector", s["vc"],
          lambda e: e.tensor_scalar(c.CO("slneg"), c.slpos, -1.0, None, ALU.mult))

    em.wait("scalar", s["vc"])
    em.wait("scalar", s["gp"])
    em.wait("vector", s["gp"])
    if _lite >= 2:
        return
    for h in range(H):
        sl = c.CO("slneg")[:, h:h + 1]
        em.op("scalar", s["sc"],
              lambda e, h=h, sl=sl: e.activation(
                  c.Z("qdec")[:, h * 256:(h + 1) * 256], c.iota_m1, AF.Exp, scale=sl))
        em.op("scalar", s["sc"],
              lambda e, h=h, sl=sl: e.activation(
                  c.CO("kdec")[:, 2 * h:2 * h + 2], c.iota_kd, AF.Exp, scale=sl))
        em.op("scalar", s["sc"],
              lambda e, h=h, sl=sl: e.activation(
                  c.CO("bd")[:, h:h + 1], c.CO("c256"), AF.Exp, scale=sl))
        em.wait("vector", s["sc"])
        em.wait("vector", s["gp"])
        dm = em.op("vector", s["vc"],
                   lambda e, sl=sl: e.tensor_scalar(c.dt_scr, c.idxT0, sl, None, ALU.mult))
        em.wait("gpsimd", s["vc"], dm)
        af = em.op("gpsimd", s["gp"],
                   lambda e: e.affine_select(c.dt_scr, c.dt_scr, [[1, 256]], ALU.is_ge,
                                             -20000.0, base=0, channel_multiplier=-1))
        em.wait("scalar", s["gp"], af)
        em.op("scalar", s["sc"],
              lambda e, h=h: e.activation(
                  c.Z("dt")[:, h * 256:(h + 1) * 256], c.dt_scr, AF.Exp))


def ph_ln_transpose(c, src_dram, dst_arena, tapname):
    em, s = c.em, c.s
    import os
    _mode = os.environ.get("LN_MODE", "")
    _ser = os.environ.get("LN_SERIAL", "0") == "1"
    stats_done = [None, None]
    tp_done = [None, None]
    last_tv = [None, None]
    dst_v = dst_arena[:, :].rearrange("p (cc t) -> p cc t", cc=NCH)
    for tc in range(NCH):
        bi = tc % 2
        buf = c.xchunk[bi]
        if stats_done[bi] is not None:
            em.wait("gpsimd", s["vc"], stats_done[bi][0])
            em.wait("gpsimd", s["sc"], stats_done[bi][1])
        if _mode == "s3" and stats_done[1 - bi] is not None:
            em.wait("gpsimd", s["vc"], stats_done[1 - bi][0])
            em.wait("gpsimd", s["sc"], stats_done[1 - bi][1])
        sx = s["x0"] if bi == 0 else s["x1"]
        ld = em.op("gpsimd", sx,
                   lambda e, b=buf, d=src_dram, i=tc: e.dma_start(
                       b, d[i * 128:(i + 1) * 128, :]), 16)
        sfx = "b" if bi else ""
        S1, S2, MNEG, MM, TV, RC, ACOL, BCOL = (c.CO("s1" + sfx), c.CO("s2" + sfx),
                                                c.CO("mneg" + sfx), c.CO("mm" + sfx),
                                                c.CO("tv" + sfx), c.CO("rc" + sfx),
                                                c.CO("acol" + sfx), c.CO("bcol" + sfx))
        em.wait("scalar", sx, ld)
        if last_tv[bi] is not None:
            em.wait("scalar", s["vc"], last_tv[bi])
        sq = em.op("scalar", s["sc"],
                   lambda e, b=buf, S2=S2: e.activation(c.sqz, b, AF.Square,
                                                        accum_out=S2))
        em.wait("vector", sx, ld)
        em.opw("vector", s["vc"],
               lambda e, b=buf, S1=S1: e.reduce_sum(S1, b, axis=AX.X))
        em.wait("vector", s["sc"], sq)

        def vv(fn):
            return em.opw("vector", s["vc"], fn)

        vv(lambda e, S1=S1, MNEG=MNEG: e.tensor_scalar(MNEG, S1, -1.0 / D, None, ALU.mult))
        vv(lambda e, MNEG=MNEG, MM=MM: e.tensor_tensor(MM, MNEG, MNEG, ALU.mult))
        vv(lambda e, S2=S2, TV=TV: e.tensor_scalar(TV, S2, 1.0 / D, EPS_LN,
                                                   ALU.mult, ALU.add))
        last_tv[bi] = vv(lambda e, TV=TV, MM=MM: e.tensor_tensor(TV, TV, MM,
                                                                 ALU.subtract))
        rc = vv(lambda e, TV=TV, RC=RC: e.reciprocal(RC, TV))
        em.wait("scalar", s["vc"], rc)
        sq2 = em.op("scalar", s["sc"],
                    lambda e, RC=RC, ACOL=ACOL: e.activation(ACOL, RC, AF.Sqrt))
        em.wait("vector", s["sc"], sq2)
        bc = vv(lambda e, MNEG=MNEG, ACOL=ACOL, BCOL=BCOL: e.tensor_tensor(
            BCOL, MNEG, ACOL, ALU.mult))
        if _mode == "s1":
            em.barrier(c.sems)
        xb = c.xn_tok[bi]
        stp = s["t0"] if bi == 0 else s["t1"]
        if tp_done[bi] is not None:
            em.wait("vector", stp, tp_done[bi])
        ap = vv(lambda e, b=buf, o=xb, ACOL=ACOL, BCOL=BCOL: e.tensor_scalar(
            o, b, ACOL, BCOL, ALU.mult, ALU.add))
        stats_done[bi] = (ap, sq)
        if "tap_cols" in c.taps:
            em.wait("sync", s["vc"], ap)
            for ci, col in enumerate([S1, S2, MNEG, TV, RC, ACOL, BCOL]):
                em.op("sync", s["st"],
                      lambda e, tc=tc, ci=ci, col=col: e.dma_start(
                          c.tap_dram["tap_cols"][:, tc * 8 + ci:tc * 8 + ci + 1], col,
                          allow_slow_non_contiguous=True), 16)
        em.wait("sync", s["vc"], ap)
        tp_done[bi] = em.op("sync", stp,
                            lambda e, o=dst_v[:, :, tc * 128:(tc + 1) * 128], i=xb:
                            e.dma_start(o, i, transpose=True), 16)
    if tapname in c.taps and _mode != "notp":
        em.barrier(c.sems)
        tap_store(c, c.tap_dram[tapname], dst_arena[:, :])


def _load_w_slice(c, dram, col0, ncols, dst, sem, war_tn=None):
    em, s = c.em, c.s
    if war_tn is not None:
        em.wait("gpsimd", s["tn"], war_tn)
    src = dram[:, col0:col0 + ncols].rearrange("(cc p) j -> p cc j", p=128)
    dstv = dst.rearrange("p (cc j) -> p cc j", cc=NCH)
    return em.op("gpsimd", sem, lambda e: e.dma_start(dstv, src), 16)


def _vproj(c, xnF_arena, g, st, preld=None):
    em, s = c.em, c.s
    wv_ld = preld if preld is not None else _load_w_slice(
        c, c.dram["wv"], g * 256, 256, c.wv, s["wv"], st.get("wv_war"))
    wvv = c.wv.rearrange("p (cc j) -> p cc j", cc=NCH)
    em.wait("scalar", s["tn"])
    for tt in range(NCH):
        ps = c.pP[tt % 2]
        em.wait("tensor", s["wv"], wv_ld)
        pe = st["pP_evac"][tt % 2]
        if pe is not None:
            em.wait("tensor", s["sc"], pe)
        for fc in range(NCH):
            fn = (lambda e, ps=ps, fc=fc, tt=tt: e.matmul(
                ps[:, 0:256], c.fmv(xnF_arena, fc)[:, tt * 128:(tt + 1) * 128],
                wvv[:, fc, :], start=(fc == 0), stop=(fc == NCH - 1)))
            if fc < NCH - 1:
                em.qop("tensor", fn)
            else:
                em.op("tensor", s["tn"], fn)
        em.wait("scalar", s["tn"])
        st["pP_evac"][tt % 2] = em.op("scalar", s["sc"],
                                      lambda e, ps=ps, tt=tt: e.activation(
                                          c.vbuf[:, tt * 256:(tt + 1) * 256],
                                          ps[:, 0:256], AF.Silu))
    st["wv_war"] = c.s["tn"].n


def _fproj(c, xnF_arena, wdram, col0, dstbuf, wbuf, st, wkey, wsem, preld=None):
    em, s = c.em, c.s
    wld = preld if preld is not None else _load_w_slice(
        c, wdram, col0, 128, wbuf, wsem, st.get(wkey))
    wv = wbuf.rearrange("p (cc j) -> p cc j", cc=NCH)
    em.wait("scalar", s["tn"])
    em.wait("scalar", s["tp"])
    for nt in range(4):
        ps = c.pP[nt % 2]
        em.wait("tensor", wsem, wld)
        pe = st["pP_evac"][nt % 2]
        if pe is not None:
            em.wait("tensor", s["sc"], pe)
        for fc in range(NCH):
            fn = (lambda e, ps=ps, fc=fc, nt=nt, w=wv: e.matmul(
                ps[:, :], w[:, fc, :],
                c.fmv(xnF_arena, fc)[:, nt * 512:(nt + 1) * 512],
                start=(fc == 0), stop=(fc == NCH - 1)))
            if fc < NCH - 1:
                em.qop("tensor", fn)
            else:
                em.op("tensor", s["tn"], fn)
        em.wait("scalar", s["tn"])
        st["pP_evac"][nt % 2] = em.op("scalar", s["sc"],
                                      lambda e, ps=ps, nt=nt, d=dstbuf: e.activation(
                                          d[:, nt * 512:(nt + 1) * 512], ps[:, :],
                                          AF.Silu))
    st[wkey] = c.s["tn"].n


def _ktranspose(c, h, kb, ktb, tsem):
    em, s = c.em, c.s
    ktv = ktb.rearrange("p (cc t) -> p cc t", cc=NCH)
    em.wait("sync", s["sc"])
    em.wait("sync", s["tn"])
    em.wait("sync", s["vc"])
    tp = em.op("sync", tsem,
               lambda e, ktv=ktv, kb=kb: e.dma_start(ktv, kb, transpose=True), 16)
    em.wait("vector", tsem, tp)
    last = None
    for tile in range(NCH):
        u = tile % 2
        last = em.op("vector", s["vc"],
                     lambda e, t=tile, u=u, h=h, ktb=ktb: e.tensor_scalar(
                         ktb[:, t * 128:(t + 1) * 128],
                         ktb[:, t * 128:(t + 1) * 128],
                         c.CO("kdec")[:, 2 * h + u:2 * h + u + 1], None, ALU.mult))
    return last


def _kv_chain(c, h, hg, ktiles_ready):
    em, s = c.em, c.s
    kvs = c.kvst[:, h * 128:(h + 1) * 128]
    stt_done = None
    em.wait("tensor", s["sc"])
    for j in range(NBLK):
        em.wait("tensor", s["vc"], stt_done if stt_done is not None else ktiles_ready)
        for u in range(2):
            tile = 2 * j + u
            fn = (lambda e, t=tile, u=u, hg=hg: e.matmul(
                c.ps_kv, c.Z("ktbuf")[:, t * 128:(t + 1) * 128],
                c.vbuf[:, t * 256 + hg * 128:t * 256 + (hg + 1) * 128],
                start=(u == 0), stop=(u == 1)))
            if u == 0:
                em.qop("tensor", fn)
            else:
                em.op("tensor", s["tn"], fn)
        em.wait("vector", s["tn"])
        stt_done = em.opw("vector", s["vc"],
                          lambda e, kvs=kvs, h=h: e.scalar_tensor_tensor(
                              kvs, kvs, c.CO("bd")[:, h:h + 1], c.ps_kv,
                              ALU.mult, ALU.add))


def _kv_chain2(c, h, hg, ktiles_ready):
    em, s = c.em, c.s
    kvs = c.kvst[:, h * 128:(h + 1) * 128]
    stt_done = None
    em.wait("tensor", s["sc"])
    for j in range(NBLK):
        em.wait("tensor", s["vc"], stt_done if stt_done is not None else ktiles_ready)
        for u in range(2):
            tile = 2 * j + u
            fn = (lambda e, t=tile, u=u, hg=hg: e.matmul(
                c.ps_kv2, c.ktbuf2[:, t * 128:(t + 1) * 128],
                c.vbuf[:, t * 256 + hg * 128:t * 256 + (hg + 1) * 128],
                start=(u == 0), stop=(u == 1)))
            if u == 0:
                em.qop("tensor", fn)
            else:
                em.op("tensor", s["tn"], fn)
        em.wait("vector", s["tn"])
        stt_done = em.opw("vector", s["vc"],
                          lambda e, kvs=kvs, h=h: e.scalar_tensor_tensor(
                              kvs, kvs, c.CO("bd")[:, h:h + 1], c.ps_kv2,
                              ALU.mult, ALU.add))


def ph_prefix(c):
    em, s = c.em, c.s
    st = {"pP_evac": [None, None]}
    for g in range(H // 2):
        h0, h1 = 2 * g, 2 * g + 1
        _vproj(c, c.arena1, g, st)
        _fproj(c, c.arena1, c.dram["wk"], h0 * 128, c.Z("kbuf"), c.Z("wk"), st,
               "wk_war", s["wk"])
        _fproj(c, c.arena1, c.dram["wk"], h1 * 128, c.Z("kbuf2"), c.Z("wk"), st,
               "wk_war", s["wk"])
        kr0 = _ktranspose(c, h0, c.Z("kbuf"), c.Z("ktbuf"), s["tp"])
        kr1 = _ktranspose(c, h1, c.Z("kbuf2"), c.ktbuf2, s["tp2"])
        _kv_chain(c, h0, 0, kr0)
        _kv_chain2(c, h1, 1, kr1)


def ph_attn(c):
    em, s = c.em, c.s
    st = {"pP_evac": [None, None]}
    ng = int(os.environ.get("ATTN_GROUPS", str(H // 2)))
    ld_wv = _load_w_slice(c, c.dram["wv"], 0, 256, c.wv, s["wv"], None)
    ld_wk = _load_w_slice(c, c.dram["wk"], 0, 128, c.Z("wk"), s["wk"], None)
    ld_wq = _load_w_slice(c, c.dram["wq"], 0, 128, c.Z("wq"), s["wq"], None)
    for g in range(ng):
        h0, h1 = 2 * g, 2 * g + 1
        _vproj(c, c.arena2, g, st, preld=ld_wv)
        if g > 0:
            _exchange_apply(c, g - 1)
            em.wait("scalar", s["vc"])
        _fproj(c, c.arena2, c.dram["wk"], h0 * 128, c.Z("kbuf"), c.Z("wk"), st,
               "wk_war", s["wk"], preld=ld_wk)
        _fproj(c, c.arena2, c.dram["wk"], h1 * 128, c.Z("kbuf2"), c.Z("wk"), st,
               "wk_war", s["wk"])
        kr0 = _ktranspose(c, h0, c.Z("kbuf"), c.Z("ktbuf"), s["tp"])
        kr1 = _ktranspose(c, h1, c.Z("kbuf2"), c.ktbuf2, s["tp2"])
        _fproj(c, c.arena2, c.dram["wq"], h0 * 128, c.Z("qbuf"), c.Z("wq"), st,
               "wq_war", s["wq"], preld=ld_wq)
        _fproj(c, c.arena2, c.dram["wq"], h1 * 128, c.qbuf2, c.Z("wq"), st,
               "wq_war", s["wq"])
        if "tap_q0" in c.taps and g == 0:
            em.barrier(c.sems)
            tap_store(c, c.tap_dram["tap_q0"], c.Z("qbuf"))
            tap_store(c, c.tap_dram["tap_k0"], c.Z("kbuf"))
            tap_store(c, c.tap_dram["tap_v0"], c.vbuf)
        _scan_pair(c, h0, h1, kr0, kr1)
        if g + 1 < ng:
            ld_wv = _load_w_slice(c, c.dram["wv"], (g + 1) * 256, 256, c.wv,
                                  s["wv"], st.get("wv_war"))
            ld_wk = _load_w_slice(c, c.dram["wk"], (h1 + 1) * 128, 128, c.Z("wk"),
                                  s["wk"], st.get("wk_war"))
            ld_wq = _load_w_slice(c, c.dram["wq"], (h1 + 1) * 128, 128, c.Z("wq"),
                                  s["wq"], st.get("wq_war"))
        _exchange_start(c, g)
        if g == 3:
            c.preld_wg0 = em.op("gpsimd", s["wg0"],
                                lambda e: e.dma_start(
                                    c.wg[0].rearrange("p (cc j) -> p cc j", cc=NCH),
                                    c.dram["wg"][:, 0:128].rearrange(
                                        "(cc p) j -> p cc j", p=128)), 16)
    _exchange_apply(c, ng - 1)


def _scan_pair(c, h0, h1, kr0, kr1):
    em, s = c.em, c.s
    em.wait("tensor", s["vc"])
    hh_state = []
    for hh, h in enumerate((h0, h1)):
        hh_state.append(dict(
            h=h,
            qb=c.Z("qbuf") if hh == 0 else c.qbuf2,
            kb=c.Z("kbuf") if hh == 0 else c.Z("kbuf2"),
            ktb=c.Z("ktbuf") if hh == 0 else c.ktbuf2,
            q0t=c.Z("qkdt0") if hh == 0 else c.Z("qkdt0b"),
            q1t=c.Z("qkdt1") if hh == 0 else c.Z("qkdt1b"),
            kvbf=c.Z("kv_bf") if hh == 0 else c.Z("kv_bf2"),
            kvs=c.kvst[:, h * 128:(h + 1) * 128],
            qdec=c.Z("qdec")[:, h * 256:(h + 1) * 256],
            dt=c.Z("dt")[:, h * 256:(h + 1) * 256],
            pS=c.pS[hh], pO=c.pO[hh],
            pKV=c.pAUX[:, 0:128] if hh == 0 else c.pAUX2[:, 0:128],
            stt=None, mask=None, evac=None, inter_mm=None, intra_mm=None,
            ktr=kr0 if hh == 0 else kr1,
        ))
    em.wait("tensor", s["sc"])
    em.wait("tensor", s["vc"])
    for j in range(NBLK):
        for hh in range(2):
            S = hh_state[hh]
            if S["mask"] is not None:
                em.wait("tensor", s["vc"], S["mask"])
            for u in range(2):
                fn = (lambda e, j=j, u=u, S=S: e.matmul(
                    S["pS"][:, u * 256:(u + 1) * 256],
                    S["kb"][:, j * 256 + u * 128:j * 256 + (u + 1) * 128],
                    S["qb"][:, j * 256:(j + 1) * 256], start=True, stop=True))
                if u == 0:
                    em.qop("tensor", fn)
                else:
                    em.op("tensor", s["tn"], fn)
            S["sc_mm"] = c.s["tn"].n
        for hh in range(2):
            S = hh_state[hh]
            em.wait("vector", s["tn"], S["sc_mm"])
            if S["intra_mm"] is not None:
                em.wait("vector", s["tn"], S["intra_mm"])
            em.op("vector", s["vc"],
                  lambda e, S=S: e.tensor_tensor(S["q0t"], S["pS"][:, 0:256],
                                                 S["dt"], ALU.mult))
            S["mask"] = em.op("vector", s["vc"],
                              lambda e, S=S: e.tensor_tensor(
                                  S["q1t"][:, 128:256], S["pS"][:, 384:512],
                                  S["dt"][:, 0:128], ALU.mult))
            S["qt"] = em.op("vector", s["vc"],
                            lambda e, j=j, S=S: e.tensor_tensor(
                                S["qb"][:, j * 256:(j + 1) * 256],
                                S["qb"][:, j * 256:(j + 1) * 256], S["qdec"],
                                ALU.mult))
            em.wait("scalar", s["vc"], S["stt"] if S["stt"] is not None else S["ktr"])
            if S["inter_mm"] is not None:
                em.wait("scalar", s["tn"], S["inter_mm"])
            S["kc"] = em.op("scalar", s["sc"],
                            lambda e, S=S: e.copy(S["kvbf"], S["kvs"]))
        for hh in range(2):
            S = hh_state[hh]
            em.wait("tensor", s["sc"], S["kc"])
            em.wait("tensor", s["vc"], S["qt"])
            if S["evac"] is not None:
                em.wait("tensor", s["sc"], S["evac"])
            em.qop("tensor",
                   lambda e, j=j, S=S: e.matmul(
                       S["pO"][:, 0:256], S["kvbf"],
                       S["qb"][:, j * 256:(j + 1) * 256], start=True, stop=False))
            S["inter_mm"] = None
            for u in range(2):
                qk = S["q0t"] if u == 0 else S["q1t"]
                fn = (lambda e, j=j, u=u, S=S, qk=qk: e.matmul(
                    S["pO"][:, 0:256],
                    c.vbuf[:, (2 * j + u) * 256 + (S["h"] % 2) * 128:
                           (2 * j + u) * 256 + (S["h"] % 2 + 1) * 128],
                    qk, start=False, stop=(u == 1)))
                if u == 0:
                    em.qop("tensor", fn)
                else:
                    em.op("tensor", s["tn"], fn)
            S["inter_mm"] = S["intra_mm"] = c.s["tn"].n
            em.wait("tensor", s["vc"], S["stt"] if S["stt"] is not None else S["ktr"])
            for u in range(2):
                tile = 2 * j + u
                fn = (lambda e, t=tile, u=u, S=S: e.matmul(
                    S["pKV"], S["ktb"][:, t * 128:(t + 1) * 128],
                    c.vbuf[:, t * 256 + (S["h"] % 2) * 128:
                           t * 256 + (S["h"] % 2 + 1) * 128],
                    start=(u == 0), stop=(u == 1)))
                if u == 0:
                    em.qop("tensor", fn)
                else:
                    em.op("tensor", s["tn"], fn)
            S["kv_mm"] = c.s["tn"].n
        for hh in range(2):
            S = hh_state[hh]
            em.wait("scalar", s["tn"], S["intra_mm"])
            S["evac"] = em.op("scalar", s["sc"],
                              lambda e, j=j, S=S: e.copy(
                                  c.arena1[:, S["h"] * 2048 + j * 256:
                                           S["h"] * 2048 + (j + 1) * 256],
                                  S["pO"][:, 0:256]))
            em.wait("vector", s["tn"], S["kv_mm"])
            S["stt"] = em.opw("vector", s["vc"],
                              lambda e, S=S: e.scalar_tensor_tensor(
                                  S["kvs"], S["kvs"], c.CO("bd")[:, S["h"]:S["h"] + 1],
                                  S["pKV"], ALU.mult, ALU.add))


def _exchange_start(c, g):
    em, s = c.em, c.s
    h0, h1 = 2 * g, 2 * g + 1
    rg = c.replica_groups
    em.wait("gpsimd", s["vc"])
    em.wait("gpsimd", s["tn"])
    em.op("gpsimd", s["cst"],
          lambda e: e.dma_start(c.dram["cci"][:, :],
                                c.kvst[:, h0 * 128:(h1 + 1) * 128]), 16)
    em.wait("gpsimd", s["cst"])
    cc = em.op("gpsimd", s["gp"],
               lambda e: e.collective_compute(
                   "AllGather", ALU.bypass, rg,
                   [c.dram["cci"][:, :]], [c.dram["cco"][:, :]]))
    em.wait("gpsimd", s["gp"], cc)
    em.op("gpsimd", s["cst"],
          lambda e: e.dma_start(c.S0, c.dram["cco"][0:128, :]), 16)
    em.op("gpsimd", s["cst"],
          lambda e: e.dma_start(c.S1, c.dram["cco"][128:256, :]), 16)


def _exchange_apply(c, g):
    em, s = c.em, c.s
    h0, h1 = 2 * g, 2 * g + 1
    em.wait("vector", s["cst"])
    em.wait("vector", s["sc"])
    em.opw("vector", s["vc"],
           lambda e: e.tensor_scalar(c.Seff, c.S0, c.CO("m0"), None, ALU.mult))
    em.opw("vector", s["vc"],
           lambda e: e.scalar_tensor_tensor(c.Seff, c.S1, c.CO("m1"), c.Seff,
                                            ALU.mult, ALU.add))
    tiles = (c.sbf_tiles0, c.sbf_tiles1)
    for hg, h in enumerate((h0, h1)):
        Sh = c.Seff[:, hg * 128:(hg + 1) * 128]
        tb = tiles[hg]
        for j in range(NBLK):
            em.opw("vector", s["vc"],
                   lambda e, j=j, Sh=Sh, tb=tb: e.tensor_copy(
                       tb[:, j * 128:(j + 1) * 128], Sh))
            if j < NBLK - 1:
                em.opw("vector", s["vc"],
                       lambda e, Sh=Sh, h=h: e.tensor_scalar(
                           Sh, Sh, c.CO("bd")[:, h:h + 1], None, ALU.mult))
    em.wait("tensor", s["vc"])
    last_add = [None, None]
    for hg, h in enumerate((h0, h1)):
        qb = c.Z("qbuf") if hg == 0 else c.qbuf2
        tb = tiles[hg]
        for j in range(NBLK):
            b = (hg * NBLK + j) % 2
            pS = c.pS[b]
            if last_add[b] is not None:
                em.wait("tensor", s["vc"], last_add[b])
            mmn = em.op("tensor", s["tn"],
                        lambda e, j=j, qb=qb, pS=pS, tb=tb: e.matmul(
                            pS[:, 0:256], tb[:, j * 128:(j + 1) * 128],
                            qb[:, j * 256:(j + 1) * 256],
                            start=True, stop=True))
            em.wait("vector", s["tn"], mmn)
            last_add[b] = em.opw(
                "vector", s["vc"],
                lambda e, j=j, h=h, pS=pS: e.tensor_tensor(
                    c.arena1[:, h * 2048 + j * 256:h * 2048 + (j + 1) * 256],
                    c.arena1[:, h * 2048 + j * 256:h * 2048 + (j + 1) * 256],
                    pS[:, 0:256], ALU.add))


def ph_rms(c):
    em, s = c.em, c.s
    acc_done = None
    for oc in range(NCH):
        em.wait("vector", s["tn"])
        sq = em.op("vector", s["vc"],
                   lambda e, oc=oc: e.tensor_tensor(
                       c.sqz, c.arena1[:, oc * T:(oc + 1) * T],
                       c.arena1[:, oc * T:(oc + 1) * T], ALU.mult))
        em.wait("tensor", s["vc"], sq)
        if acc_done is not None:
            em.wait("tensor", s["vc"], acc_done)
        for tc in range(NCH):
            fn = (lambda e, tc=tc: e.matmul(
                c.ps_var[:, tc:tc + 1], c.sqz[:, tc * 128:(tc + 1) * 128],
                c.ones_col, start=True, stop=True))
            if tc < NCH - 1:
                em.qop("tensor", fn)
            else:
                em.op("tensor", s["tn"], fn)
        em.wait("vector", s["tn"])
        if oc == 0:
            acc_done = em.opw("vector", s["vc"],
                              lambda e: e.tensor_copy(c.CO("var16"), c.ps_var))
        else:
            acc_done = em.opw("vector", s["vc"],
                              lambda e: e.tensor_tensor(c.CO("var16"), c.CO("var16"),
                                                        c.ps_var, ALU.add))

    def vv(fn):
        return em.opw("vector", s["vc"], fn)

    vv(lambda e: e.tensor_scalar(c.CO("var16"), c.CO("var16"), 1.0 / D, EPS_RMS,
                                 ALU.mult, ALU.add))
    rc = vv(lambda e: e.reciprocal(c.CO("var16"), c.CO("var16")))
    em.wait("scalar", s["vc"], rc)
    sq = em.op("scalar", s["sc"],
               lambda e: e.activation(c.CO("a2"), c.CO("var16"), AF.Sqrt))
    em.wait("vector", s["sc"], sq)
    vv(lambda e: e.tensor_tensor(c.CO("a2sq"), c.CO("a2"), c.CO("a2"), ALU.mult))


def ph_gate(c):
    em, s = c.em, c.s
    em.wait("gpsimd", s["vc"])
    em.wait("gpsimd", s["tn"])
    for oc in range(NCH):
        em.op("gpsimd", s["wo"],
              lambda e, oc=oc: e.dma_start(
                  c.arena3[:, oc * T:(oc + 1) * T],
                  c.dram["wo"][oc * 128:(oc + 1) * 128, :]), 16)
    em.wait("vector", s["wo"])
    for oc in range(NCH):
        em.op("vector", s["vc"],
              lambda e, oc=oc: e.tensor_scalar(
                  c.arena3[:, oc * T:(oc + 1) * T], c.arena3[:, oc * T:(oc + 1) * T],
                  c.CO("nw")[:, oc:oc + 1], None, ALU.mult))
    prev_mm = [None, None]
    for gc in range(NCH):
        wb = c.wg[gc % 2]
        swg = s["wg0"] if gc % 2 == 0 else s["wg1"]
        if gc == 0 and getattr(c, "preld_wg0", None) is not None:
            ld = c.preld_wg0
        else:
            if prev_mm[gc % 2] is not None:
                em.wait("gpsimd", s["tn"], prev_mm[gc % 2])
            ld = em.op("gpsimd", swg,
                       lambda e, gc=gc, wb=wb: e.dma_start(
                           wb.rearrange("p (cc j) -> p cc j", cc=NCH),
                           c.dram["wg"][:, gc * 128:(gc + 1) * 128].rearrange(
                               "(cc p) j -> p cc j", p=128)), 16)
        wv = wb.rearrange("p (cc j) -> p cc j", cc=NCH)
        for nt in range(4):
            ps = c.pP[nt % 2]
            em.wait("tensor", swg, ld)
            em.wait("tensor", s["sc"])
            for fc in range(NCH):
                fn = (lambda e, ps=ps, fc=fc, nt=nt, w=wv: e.matmul(
                    ps[:, :], w[:, fc, :],
                    c.fmv(c.arena2, fc)[:, nt * 512:(nt + 1) * 512],
                    start=(fc == 0), stop=(fc == NCH - 1)))
                if fc < NCH - 1:
                    em.qop("tensor", fn)
                else:
                    em.op("tensor", s["tn"], fn)
            em.wait("scalar", s["tn"])
            em.wait("scalar", s["vc"])
            sg = em.op("scalar", s["sc"],
                       lambda e, ps=ps, nt=nt: e.activation(
                           c.sig[nt % 2], ps[:, :], AF.Sigmoid))
            em.wait("vector", s["sc"], sg)
            em.op("vector", s["vc"],
                  lambda e, gc=gc, nt=nt: e.tensor_tensor(
                      c.arena1[:, gc * T + nt * 512:gc * T + (nt + 1) * 512],
                      c.arena1[:, gc * T + nt * 512:gc * T + (nt + 1) * 512],
                      c.sig[nt % 2], ALU.mult))
        prev_mm[gc % 2] = c.s["tn"].n


def ph_outproj(c):
    import os
    _lvl = int(os.environ.get("P4_LEVEL", "3"))
    em, s = c.em, c.s
    evac_done = None
    prev_sq = None
    _ntc = int(os.environ.get("P4_NTC", str(NCH)))
    for tc in range(_ntc):
        if evac_done is not None:
            em.wait("gpsimd", s["vc"], evac_done)
        if _lvl >= 1:
            ld = em.op("gpsimd", s["xp4"],
                       lambda e, tc=tc: e.dma_start(
                           c.xchunk_p4, c.dram["x"][tc * 128:(tc + 1) * 128, :]), 16)
        if prev_sq is not None:
            em.wait("tensor", s["sc"], prev_sq)
        for ft in range(4):
            for kc in range(NCH):
                fn = (lambda e, tc=tc, ft=ft, kc=kc: e.matmul(
                    c.pBIG[:, ft * 512:(ft + 1) * 512],
                    c.arena1[:, kc * T + tc * 128:kc * T + (tc + 1) * 128],
                    c.arena3[:, kc * T + ft * 512:kc * T + (ft + 1) * 512],
                    start=(kc == 0), stop=(kc == NCH - 1)))
                if ft == 3 and kc == NCH - 1:
                    em.op("tensor", s["tn"], fn)
                else:
                    em.qop("tensor", fn)
        mm = c.s["tn"].n
        em.wait("scalar", s["tn"], mm)
        cp1 = em.op("scalar", s["sc"],
                    lambda e, tc=tc: e.activation(
                        c.arena2[:, tc * 2048:(tc + 1) * 2048], c.pBIG[:, :],
                        AF.Copy, accum_out=c.CO("s1")))
        sq = em.op("scalar", s["sc"],
                   lambda e: e.activation(c.sq_p4_bf, c.pBIG[:, :], AF.Square,
                                          accum_out=c.CO("s2")))
        prev_sq = sq
        em.wait("vector", s["sc"], sq)

        def vv(fn):
            return em.opw("vector", s["vc"], fn)

        vv(lambda e: e.tensor_scalar(c.CO("mneg"), c.CO("s1"), -1.0 / D, None, ALU.mult))
        vv(lambda e: e.tensor_tensor(c.CO("mm"), c.CO("mneg"), c.CO("mneg"), ALU.mult))
        vv(lambda e: e.tensor_scalar(c.CO("tv"), c.CO("s2"), 1.0 / D, None, ALU.mult))
        vv(lambda e: e.tensor_tensor(c.CO("tv"), c.CO("tv"), c.CO("mm"), ALU.subtract))
        vv(lambda e, tc=tc: e.scalar_tensor_tensor(
            c.CO("tv"), c.CO("tv"), c.CO("a2sq")[:, tc:tc + 1], c.CO("epsln"),
            ALU.mult, ALU.add))
        rc = vv(lambda e: e.reciprocal(c.CO("rc"), c.CO("tv")))
        em.wait("scalar", s["vc"], rc)
        sq2 = em.op("scalar", s["sc"],
                    lambda e: e.activation(c.CO("acol"), c.CO("rc"), AF.Sqrt))
        em.wait("vector", s["sc"], sq2)
        vv(lambda e, tc=tc: e.tensor_tensor(
            c.CO("c1"), c.CO("a2")[:, tc:tc + 1], c.CO("acol"), ALU.mult))
        vv(lambda e: e.tensor_tensor(c.CO("bcol"), c.CO("mneg"), c.CO("c1"),
                                     ALU.mult))
        if _lvl >= 1:
            em.wait("vector", s["xp4"], ld)
        vv(lambda e: e.tensor_scalar(c.xchunk_p4, c.xchunk_p4, c.CO("bcol"), None,
                                     ALU.add))
        evac_done = vv(lambda e, tc=tc: e.scalar_tensor_tensor(
            c.arena2[:, tc * 2048:(tc + 1) * 2048],
            c.arena2[:, tc * 2048:(tc + 1) * 2048], c.CO("c1"),
            c.xchunk_p4, ALU.mult, ALU.add))


def ph_ffn(c):
    em, s = c.em, c.s
    for kc in range(NCH):
        em.op("gpsimd", s["wf"],
              lambda e, kc=kc: e.dma_start(
                  c.arena3[:, kc * T:(kc + 1) * T],
                  c.dram["wf"][kc * 128:(kc + 1) * 128, :]), 16)
    wf_ld = c.s["wf"].n
    em.op("gpsimd", s["cst"],
          lambda e: e.dma_start(c.wsb, c.dram["ws"][None, :].broadcast_to((128, D))), 16)
    em.op("gpsimd", s["cst"],
          lambda e: e.dma_start(c.biasb[:, :],
                                c.dram["fb"][None, :].broadcast_to((128, D))), 16)

    qxv_q = [q.rearrange("p (cc t) -> p cc t", cc=NCH) for q in c.qxf_q]
    tp_done = [None, None]
    store_done = [None]
    q_mm_done = [None, None]
    pp_evac = [None, None]

    def quant_tc(tc, qq):
        ti = tc % 4
        x2c = c.arena2[:, tc * 2048:(tc + 1) * 2048]
        em.wait("scalar", s["vc"])
        sq = em.op("scalar", s["sc"],
                   lambda e, x2c=x2c: e.activation(c.Xbf[:, 0:2048], x2c, AF.Square,
                                                   accum_out=c.CO("s2")))
        em.opw("vector", s["vc"],
               lambda e, x2c=x2c: e.reduce_sum(c.CO("s1"), x2c, axis=AX.X))
        em.opw("vector", s["vc"],
               lambda e, x2c=x2c: e.tensor_reduce(c.CO("mx"), x2c, AX.X, ALU.max))
        em.opw("vector", s["vc"],
               lambda e, x2c=x2c: e.tensor_reduce(c.CO("mn"), x2c, AX.X, ALU.min))
        em.wait("vector", s["sc"], sq)

        def vv(fn):
            return em.opw("vector", s["vc"], fn)

        vv(lambda e: e.tensor_scalar(c.CO("mneg"), c.CO("s1"), -1.0 / D, None,
                                     ALU.mult))
        vv(lambda e: e.tensor_tensor(c.CO("mm"), c.CO("mneg"), c.CO("mneg"), ALU.mult))
        vv(lambda e: e.tensor_scalar(c.CO("tv"), c.CO("s2"), 1.0 / D, EPS_LN,
                                     ALU.mult, ALU.add))
        vv(lambda e: e.tensor_tensor(c.CO("tv"), c.CO("tv"), c.CO("mm"), ALU.subtract))
        rc = vv(lambda e: e.reciprocal(c.CO("rc"), c.CO("tv")))
        em.wait("scalar", s["vc"], rc)
        sq2 = em.op("scalar", s["sc"],
                    lambda e: e.activation(c.CO("acol"), c.CO("rc"), AF.Sqrt))
        em.wait("vector", s["sc"], sq2)
        vv(lambda e: e.tensor_scalar(c.CO("t1"), c.CO("mx"), c.CO("mneg"), None,
                                     ALU.add))
        vv(lambda e: e.tensor_scalar(c.CO("t2"), c.CO("mn"), c.CO("mneg"), -1.0,
                                     ALU.add, ALU.mult))
        vv(lambda e: e.tensor_tensor(c.CO("t1"), c.CO("t1"), c.CO("t2"), ALU.max))
        vv(lambda e: e.tensor_tensor(c.CO("t1"), c.CO("t1"), c.CO("acol"), ALU.mult))
        vv(lambda e: e.tensor_scalar(c.CO("scol"), c.CO("t1"), EPS_Q, 1.0 / 127.0,
                                     ALU.max, ALU.mult))
        vv(lambda e: e.reciprocal(c.CO("invs"), c.CO("scol")))
        vv(lambda e: e.tensor_tensor(c.CO("c1"), c.CO("acol"), c.CO("invs"), ALU.mult))
        vv(lambda e: e.tensor_tensor(c.CO("c2"), c.CO("mneg"), c.CO("c1"), ALU.mult))
        vv(lambda e, tc=tc: e.tensor_copy(c.CO("insc")[:, tc:tc + 1], c.CO("scol")))
        bi = tc % 2
        vv(lambda e, x2c=x2c: e.tensor_scalar(c.X32, x2c, c.CO("c1"), c.CO("c2"),
                                              ALU.mult, ALU.add))
        vv(lambda e: e.tensor_scalar(c.X32, c.X32, MAGIC, MAGIC, ALU.add,
                                     ALU.subtract))
        stq = s["t0"] if bi == 0 else s["t1"]
        if tp_done[bi] is not None:
            em.wait("vector", stq, tp_done[bi])
        qd = em.opw("vector", s["vc"],
                    lambda e, bi=bi: e.tensor_scalar(c.qf[bi], c.X32, 127.0, -128.0,
                                                     ALU.min, ALU.max))
        em.wait("sync", s["vc"], qd)
        if q_mm_done[qq] is not None:
            em.wait("sync", s["tn"], q_mm_done[qq])
        tp_done[bi] = em.op("sync", stq,
                            lambda e, bi=bi, ti=ti, qq=qq: e.dma_start(
                                qxv_q[qq][:, :, ti * 128:(ti + 1) * 128], c.qf[bi],
                                transpose=True), 16)

    def mm_tc(tc, qq):
        ti = tc % 4
        em.wait("tensor", s["wf"], wf_ld)
        em.wait("tensor", s["t0"])
        em.wait("tensor", s["t1"])
        for ft in range(4):
            pp = ft % 2
            if pp_evac[pp] is not None:
                em.wait("tensor", s["vc"], pp_evac[pp])
            for kc in range(NCH):
                fn = (lambda e, ti=ti, ft=ft, kc=kc, qq=qq, pp=pp: e.matmul(
                    c.pP[pp][:, :],
                    c.qxf_q[qq][:, kc * 512 + ti * 128:kc * 512 + (ti + 1) * 128],
                    c.arena3[:, kc * T + ft * 512:kc * T + (ft + 1) * 512],
                    start=(kc == 0), stop=(kc == NCH - 1)))
                if kc == NCH - 1:
                    em.op("tensor", s["tn"], fn)
                else:
                    em.qop("tensor", fn)
            mmn = c.s["tn"].n
            em.wait("vector", s["tn"], mmn)
            em.wait("vector", s["cst"])
            em.opw("vector", s["vc"],
                   lambda e, ft=ft, tc=tc, pp=pp: e.scalar_tensor_tensor(
                       c.tmp32, c.pP[pp][:, :], c.CO("insc")[:, tc:tc + 1],
                       c.wsb[:, ft * 512:(ft + 1) * 512], ALU.mult, ALU.mult))
            pp_evac[pp] = em.opw("vector", s["vc"],
                                 lambda e, ft=ft: e.tensor_tensor(
                                     c.ffn16[:, ft * 512:(ft + 1) * 512], c.tmp32,
                                     c.biasb[:, ft * 512:(ft + 1) * 512], ALU.add))

    def epi_tc(tc):
        ep = c.s["vc"].n
        em.wait("scalar", s["vc"], ep)
        if store_done[0] is not None:
            em.wait("scalar", s["st"], store_done[0])
            em.wait("vector", s["st"], store_done[0])
        sq = em.op("scalar", s["sc"],
                   lambda e: e.activation(c.outbuf, c.ffn16, AF.Square,
                                          accum_out=c.CO("s2")))
        em.opw("vector", s["vc"],
               lambda e: e.reduce_sum(c.CO("s1"), c.ffn16, axis=AX.X))

        def vv(fn):
            return em.opw("vector", s["vc"], fn)

        vv(lambda e: e.tensor_scalar(c.CO("mneg"), c.CO("s1"), -1.0 / D, None,
                                     ALU.mult))
        vv(lambda e: e.tensor_tensor(c.CO("mm"), c.CO("mneg"), c.CO("mneg"), ALU.mult))
        em.wait("vector", s["sc"], sq)
        vv(lambda e: e.tensor_scalar(c.CO("tv"), c.CO("s2"), 1.0 / D, EPS_LN,
                                     ALU.mult, ALU.add))
        vv(lambda e: e.tensor_tensor(c.CO("tv"), c.CO("tv"), c.CO("mm"), ALU.subtract))
        rc = vv(lambda e: e.reciprocal(c.CO("rc"), c.CO("tv")))
        em.wait("scalar", s["vc"], rc)
        sq2 = em.op("scalar", s["sc"],
                    lambda e: e.activation(c.CO("acol"), c.CO("rc"), AF.Sqrt))
        em.wait("vector", s["sc"], sq2)
        vv(lambda e: e.tensor_tensor(c.CO("bcol"), c.CO("mneg"), c.CO("acol"),
                                     ALU.mult))
        vv(lambda e: e.tensor_scalar(c.outbuf, c.ffn16, c.CO("acol"), c.CO("bcol"),
                                     ALU.mult, ALU.add))
        od = vv(lambda e, tc=tc: e.tensor_tensor(
            c.outbuf, c.outbuf, c.arena2[:, tc * 2048:(tc + 1) * 2048], ALU.add))
        em.wait("sync", s["vc"], od)
        store_done[0] = em.op("sync", s["st"],
                              lambda e, tc=tc: e.dma_start(
                                  c.dram["out"][tc * 128:(tc + 1) * 128, :],
                                  c.outbuf), 16)

    for ti in range(4):
        quant_tc(ti, 0)
    for hf in range(4):
        qq = hf % 2
        for ti in range(4):
            tc = hf * 4 + ti
            mm_tc(tc, qq)
            if ti == 3:
                q_mm_done[qq] = c.s["tn"].n
            if hf < 3:
                quant_tc((hf + 1) * 4 + ti, (hf + 1) % 2)
            epi_tc(tc)


def _host_prep(inputs):
    x = np.asarray(inputs["L0_attn_hidden_states"], dtype=np.float32)
    slopes = np.ascontiguousarray(
        np.asarray(inputs["L0_attn_slope_rate"], dtype=np.float32).reshape(H))
    wqkv = np.asarray(inputs["L0_attn_w_qkv"], dtype=np.float32)
    r = wqkv.reshape(H, 3, HD, D)
    wqT = np.ascontiguousarray(r[:, 0].transpose(2, 0, 1).reshape(D, D))
    wkT = np.ascontiguousarray(r[:, 1].transpose(2, 0, 1).reshape(D, D))
    wvT = np.ascontiguousarray(r[:, 2].transpose(2, 0, 1).reshape(D, D))
    wgT = np.ascontiguousarray(np.asarray(inputs["L0_attn_w_gate"], np.float32).T)
    woT = np.ascontiguousarray(np.asarray(inputs["L0_attn_w_out"], np.float32).T)
    wfT = np.ascontiguousarray(np.asarray(inputs["L0_ffn_weight"]).T)
    normw = np.ascontiguousarray(np.asarray(inputs["L0_attn_norm_w"], np.float32))
    wscale = np.ascontiguousarray(
        np.asarray(inputs["L0_ffn_w_scale"], np.float32).reshape(D))
    fbias = np.ascontiguousarray(np.asarray(inputs["L0_ffn_bias"], np.float32))
    in_maps = []
    for core in range(8):
        b, hf = core // 2, core % 2
        x_own = np.ascontiguousarray(x[b, hf * T:(hf + 1) * T])
        msk = np.array([1.0, 0.0], np.float32) if hf == 1 else np.zeros(2, np.float32)
        in_maps.append(dict(x_own=x_own, msk=msk, wqT=wqT, wkT=wkT, wvT=wvT,
                            wgT=wgT, woT=woT, wfT=wfT, slopes=slopes, normw=normw,
                            wscale=wscale, fbias=fbias))
    return x, in_maps


def kernel(**inputs):
    x, in_maps = _host_prep(inputs)
    nc = build_nc()
    res = run_bass_kernel_spmd(nc, in_maps, core_ids=list(range(8)))
    out = np.empty((4, 4096, D), np.float32)
    for core in range(8):
        b, hf = core // 2, core % 2
        out[b, hf * T:(hf + 1) * T, :] = res.results[core]["out"]
    return out
